# version 04
# baseline (speedup 1.0000x reference)
import numpy as np
import ml_dtypes
from contextlib import ExitStack

import concourse.bass as bass
import concourse.mybir as mybir
from concourse.bass_utils import run_bass_kernel_spmd

F32 = mybir.dt.float32
BF16 = mybir.dt.bfloat16
AF = mybir.ActivationFunctionType
ALU = mybir.AluOpType
AX = mybir.AxisListType

S = 4096
D = 1024
NH = 16
HD = 64
TT = S // 128

LN_EPS = 1e-5
RMS_EPS = 1.1920929e-07
EPS_LIN = 1e-6
SHIFT = 20.0

ENGINES = ("sync", "gpsimd", "tensor", "vector", "scalar")
DEBUG_SERIAL = False



def _bc(v, rows=128):
    v = np.ascontiguousarray(np.asarray(v, dtype=np.float32).reshape(-1))
    return np.ascontiguousarray(np.broadcast_to(v[None, :], (rows, v.size)))


def prep_host_inputs(inp):
    g = {}
    g["ident_bf16"] = np.eye(128, dtype=ml_dtypes.bfloat16)
    g["ident_f32"] = np.eye(128, dtype=np.float32)
    cc = {"ln_eps": LN_EPS, "rms_eps": RMS_EPS, "eps_lin": EPS_LIN,
          "neg_shift": -SHIFT, "one": 1.0, "zero": 0.0}
    g["_ccols_order"] = sorted(cc)
    g["ccols"] = np.ascontiguousarray(
        np.stack([np.full(128, cc[k], dtype=np.float32) for k in g["_ccols_order"]], axis=1))
    g["ones_col_bf16"] = np.ones((128, 1), dtype=ml_dtypes.bfloat16)
    g["ones_row_f32"] = np.ones((1, 128), dtype=np.float32)

    i = np.arange(0, HD, 2, dtype=np.float64)
    inv_freq = 1.0 / (10000.0 ** (i / HD))
    ang = np.outer(np.arange(S, dtype=np.float64), inv_freq)
    cos64 = np.repeat(np.cos(ang), 2, axis=1).astype(np.float32)
    sin = np.sin(ang).astype(np.float32)
    sgnsin64 = np.empty((S, HD), dtype=np.float32)
    sgnsin64[:, 0::2] = -sin
    sgnsin64[:, 1::2] = sin
    g["rope_cos"] = np.ascontiguousarray(np.tile(cos64, (1, 2 * NH)))
    g["rope_sgnsin"] = np.ascontiguousarray(np.tile(sgnsin64, (1, 2 * NH)))

    Wqkv = np.asarray(inp["L1_attn_Wqkv"])
    bqkv = np.asarray(inp["L1_attn_bqkv"])
    WqkvT = Wqkv.T
    cols = []
    for h in range(NH):
        cols.extend(range(h * HD, h * HD + HD))
        cols.extend(range(D + h * HD, D + h * HD + HD))
    for vp in range(NH // 2):
        cols.extend(range(2 * D + vp * 128, 2 * D + vp * 128 + 128))
    cols = np.array(cols)
    g["l1a_wqkvT"] = np.ascontiguousarray(WqkvT[:, cols])
    g["l1a_bqkv_bc"] = _bc(bqkv[cols])
    gq = np.asarray(inp["L1_attn_gq"]); gk = np.asarray(inp["L1_attn_gk"])
    gqk = np.empty(2 * D, dtype=np.float32)
    for h in range(NH):
        gqk[h * 128:h * 128 + HD] = gq[h * HD:(h + 1) * HD]
        gqk[h * 128 + HD:h * 128 + 128] = gk[h * HD:(h + 1) * HD]
    g["l1a_gqk_bc"] = _bc(gqk)
    g["l1a_woutT"] = np.ascontiguousarray(np.asarray(inp["L1_attn_Wout"]).T)
    g["l1a_bout_bc"] = _bc(inp["L1_attn_bout"])
    g["l1a_gout_bc"] = _bc(inp["L1_attn_gout"])

    g["l0f_gатe"] = None; del g["l0f_gатe"]
    g["l0f_wg"] = np.ascontiguousarray(np.concatenate(
        [np.asarray(inp["L0_ffn_gate_proj_w"]),
         np.asarray(inp["L0_ffn_shared_gate_w"])], axis=0))
    g["l0f_wu"] = np.ascontiguousarray(np.concatenate(
        [np.asarray(inp["L0_ffn_up_proj_w"]),
         np.asarray(inp["L0_ffn_shared_up_w"])], axis=0))
    g["l0f_wd"] = np.ascontiguousarray(np.concatenate(
        [np.asarray(inp["L0_ffn_down_proj_w"]),
         np.asarray(inp["L0_ffn_shared_down_w"])], axis=0))
    g["l0f_gatewT"] = np.ascontiguousarray(np.asarray(inp["L0_ffn_gate_w"]).T)
    g["l0f_beta_bc"] = _bc(inp["L0_ffn_beta"])

    w_in = np.asarray(inp["L0_attn_w_in"])
    cols = []
    for h in range(NH):
        cols.extend(range(2 * D + h * HD, 2 * D + h * HD + HD))
        cols.extend(range(h * HD, h * HD + HD))
    for vp in range(NH // 2):
        cols.extend(range(D + vp * 128, D + vp * 128 + 128))
    cols = np.array(cols)
    g["l0a_win"] = np.ascontiguousarray(w_in[:, cols])
    g["l0a_wgate"] = np.ascontiguousarray(np.asarray(inp["L0_attn_w_gate"]))
    bgate = np.asarray(inp["L0_attn_b_gate"])
    g["l0a_bga_bc"] = _bc(bgate[0:D])
    g["l0a_bgg_bc"] = _bc(bgate[D:2 * D])
    rep = lambda a: np.ascontiguousarray(np.repeat(np.asarray(a, np.float32), 8, axis=0))
    g["l0a_amp"] = rep(inp["L0_attn_amplitudes"])
    g["l0a_sharp"] = rep(inp["L0_attn_sharpness"])
    g["l0a_off"] = rep(inp["L0_attn_offsets"])
    pgrid = np.arange(128) % 8
    g["l0a_relgrid"] = np.ascontiguousarray(
        (4095 - 1024 * pgrid[:, None] - np.arange(1024)[None, :]).astype(np.float32))

    g["l1f_ln1g_bc"] = _bc(inp["L1_ffn_ln1_g"])
    g["l1f_ln1b_bc"] = _bc(inp["L1_ffn_ln1_b"])
    g["l1f_ln2g_bc"] = _bc(inp["L1_ffn_ln2_g"])
    g["l1f_ln2b_bc"] = _bc(inp["L1_ffn_ln2_b"])
    wqkv = np.asarray(inp["L1_ffn_w_qkv"])
    bqkv2 = np.asarray(inp["L1_ffn_b_qkv"])
    cols = []
    for h in range(NH):
        cols.extend(range(h * HD, h * HD + HD))
        cols.extend(range(D + h * HD, D + h * HD + HD))
    for vp in range(NH // 2):
        cols.extend(range(2 * D + vp * 128, 2 * D + vp * 128 + 128))
    cols = np.array(cols)
    g["l1f_wqkv"] = np.ascontiguousarray(wqkv[:, cols])
    g["l1f_bqkv"] = np.ascontiguousarray(bqkv2[cols].reshape(24, 128).T)
    g["l1f_wo"] = np.ascontiguousarray(np.asarray(inp["L1_ffn_w_o"]))
    g["l1f_bo_bc"] = _bc(inp["L1_ffn_b_o"])
    g["l1f_gatew"] = np.ascontiguousarray(np.asarray(inp["L1_ffn_gate_w"]))
    g["l1f_gateb_bc"] = _bc(inp["L1_ffn_gate_b"])
    g["l1f_W1"] = np.ascontiguousarray(np.asarray(inp["L1_ffn_W1"]))
    g["l1f_W2"] = np.ascontiguousarray(np.asarray(inp["L1_ffn_W2"]))
    b1 = np.asarray(inp["L1_ffn_b1"])
    g["l1f_b1T"] = np.ascontiguousarray(b1.reshape(8, 32, 128).transpose(0, 2, 1))
    g["l1f_b2"] = np.ascontiguousarray(np.asarray(inp["L1_ffn_b2"]))
    return g


def prep_core_inputs(inp, core):
    x = np.asarray(inp["L0_attn_x"])
    return {"x": np.ascontiguousarray(x[core])}



class Sem:
    __slots__ = ("h", "n")
    def __init__(self, handle):
        self.h = handle
        self.n = 0


class Prog:

    def __init__(self, nc):
        self.nc = nc
        self.es = ExitStack()
        self._scopes = [self.es]
        self.q = {e: [] for e in ENGINES}
        self._sems = {}
        self._waited = {e: {} for e in ENGINES}
        self._hzt = self.es.enter_context(nc.sbuf_tensor("hzt", [128, 1], F32))

    def scope(self):
        import contextlib
        prog = self

        @contextlib.contextmanager
        def _cm():
            st = ExitStack()
            prog._scopes.append(st)
            try:
                yield st
            finally:
                prog._scopes.pop()
                st.close()
        return _cm()

    def barrier(self):
        for e in ENGINES:
            for sem in list(self._sems.values()):
                if sem.n > 0:
                    self.wait(e, sem, sem.n)

    def sem(self, name):
        if name not in self._sems:
            self._sems[name] = Sem(self.es.enter_context(self.nc.semaphore(name)))
        return self._sems[name]

    def sbuf(self, name, shape, dtype):
        return self._scopes[-1].enter_context(self.nc.sbuf_tensor(name, shape, dtype))

    def psum(self, name, shape, dtype=F32):
        return self.es.enter_context(self.nc.psum_tensor(name, shape, dtype))

    def wait(self, eng, sem, val=None):
        if val is None:
            val = sem.n
        if val <= self._waited[eng].get(sem, -1):
            return
        self._waited[eng][sem] = val
        self.q[eng].append(lambda e, s=sem.h, v=val: e.wait_ge(s, v))

    def hz(self, eng):
        sem = self.sem("hz_" + eng)
        eng_dummy = {"vector": lambda e: e.memset(self._hzt[:, :], 0.0),
                     "scalar": lambda e: e.memzero(self._hzt[:, :]),
                     "gpsimd": lambda e: e.memset(self._hzt[:, :], 0.0)}[eng]
        self.run(eng, eng_dummy, inc=sem)
        self.wait(eng, sem, sem.n)

    def run(self, eng, fn, inc=None, inc_by=1):
        if inc is not None:
            inc.n += inc_by
            self.q[eng].append((fn, inc, inc_by))
        else:
            self.q[eng].append(fn)

    def dma(self, eng, out, in_, sem, **kw):
        def fn(e, out=out, in_=in_, kw=kw):
            return e.dma_start(out=out, in_=in_, **kw)
        self.run(eng, fn, inc=sem, inc_by=16)

    def emit_into(self, block):
        for eng in ENGINES:
            items = self.q[eng]
            if not items:
                continue
            def body(e, items=items):
                for it in items:
                    if isinstance(it, tuple):
                        fn, sem, k = it
                        fn(e).then_inc(sem.h, k)
                    else:
                        it(e)
            getattr(block, eng)(body)
            self.q[eng] = []


class Builder:

    def __init__(self, nc, g):
        self.nc = nc
        self.p = Prog(nc)
        self.g = g
        self.hbm = {}
        for name, arr in g.items():
            if name.startswith("_"):
                continue
            dt = BF16 if arr.dtype == ml_dtypes.bfloat16 else F32
            self.hbm[name] = nc.dram_tensor("g_" + name, list(arr.shape), dt,
                                            kind="ExternalInput").ap()
        self.ccols_order = g["_ccols_order"]
        p = self.p
        self.identb = p.sbuf("c_identb", [128, 128], BF16)
        self.identf = p.sbuf("c_identf", [128, 128], F32)
        self.ones_b = p.sbuf("c_ones_b", [128, 1], BF16)
        self.ones_row = p.sbuf("c_ones_row", [1, 128], F32)
        self.ccols = p.sbuf("c_ccols", [128, len(self.ccols_order)], F32)
        ld = p.sem("c_ld")
        p.dma("sync", self.identb[:, :], self.hbm["ident_bf16"][:, :], ld)
        p.dma("sync", self.identf[:, :], self.hbm["ident_f32"][:, :], ld)
        p.dma("sync", self.ones_b[:, :], self.hbm["ones_col_bf16"][:, :], ld)
        p.dma("sync", self.ones_row[:, :], self.hbm["ones_row_f32"][:, :], ld)
        p.dma("sync", self.ccols[:, :], self.hbm["ccols"][:, :], ld)
        for e in ("tensor", "vector", "scalar", "gpsimd"):
            p.wait(e, ld)
        self.fmA = p.sbuf("ar_fmA", [128, 8, S], BF16)
        self.fmB = p.sbuf("ar_fmB", [128, 8, S], BF16)
        self.c_tm = p.sbuf("ar_ctm", [128, TT, 9], F32)
        self.pb = [p.psum(f"pb{i}", [128, 512], F32) for i in range(8)]
        self.dump_tags = set()
        self.dumps = {}

    def maybe_dump(self, tag, ap, shape, dtype):
        if tag not in self.dump_tags:
            return
        out = self.nc.dram_tensor("dbg_" + tag, list(shape), dtype,
                                  kind="ExternalOutput").ap()
        self.p.barrier()
        sdbg = self.p.sem("dbg")
        self.p.dma("gpsimd", out, ap, sdbg)
        self.p.wait("gpsimd", sdbg)
        self.dumps[tag] = out

    def pbank(self, i):
        return self.pb[i][:, :]

    def pbank_bf(self, i):
        return self.pb[i][:, :].bitcast(BF16)

    def cc(self, name):
        i = self.ccols_order.index(name)
        return self.ccols[:, i:i + 1]


    def ln_to_fm(self, x_in, xn_fm, uid, xn_tm_hbm=None, g_bc=None, b_bc=None,
                 banks=(0, 1)):
        p = self.p
        with p.scope():
            sld = p.sem(uid + "_ld")
            sve = p.sem(uid + "_ve")
            sse = p.sem(uid + "_se")
            spe = p.sem(uid + "_pe")
            sst = p.sem(uid + "_st")

            xt = [p.sbuf(uid + f"_xt{i}", [128, D], F32) for i in range(2)]
            xc = [p.sbuf(uid + f"_xc{i}", [128, D], F32) for i in range(2)]
            sq = p.sbuf(uid + "_sq", [128, D], F32)
            st_ = p.sbuf(uid + "_stats", [128, 4], F32)
            xnb = [p.sbuf(uid + f"_xnb{i}", [128, D], BF16) for i in range(2)]
            xn32 = p.sbuf(uid + "_xn32", [128, D], F32) if xn_tm_hbm is not None else None
            ps = [self.pbank_bf(banks[0]), self.pbank_bf(banks[1])]
            if g_bc is not None:
                gb = p.sbuf(uid + "_gb", [128, D], F32)
                bb = p.sbuf(uid + "_bb", [128, D], F32)
                p.dma("sync", gb[:, :], g_bc, sld)
                p.dma("sync", bb[:, :], b_bc, sld)

            done = {0: 0, 1: 0}
            for tt in range(TT):
                b = tt % 2
                if tt >= 2:
                    p.wait("sync", sve, done[b])
                p.dma("sync", xt[b][:, :], x_in[tt * 128:(tt + 1) * 128, :], sld)
                p.wait("vector", sld, sld.n)
                p.run("vector", lambda v, b=b: v.reduce_sum(st_[:, 1:2], xt[b][:, :], axis=AX.X))
                p.hz("vector")
                p.run("vector", lambda v: v.tensor_scalar(st_[:, 2:3], st_[:, 1:2], 1.0 / D, None, ALU.mult))
                p.hz("vector")
                p.run("vector", lambda v, b=b: v.tensor_scalar(xc[b][:, :], xt[b][:, :], st_[:, 2:3], None, ALU.subtract), inc=sve)
                p.wait("scalar", sve, sve.n)
                p.run("scalar", lambda s, b=b: s.activation(sq[:, :], xc[b][:, :], AF.Square, accum_out=st_[:, 0:1]))
                p.hz("scalar")
                p.run("scalar", lambda s: s.activation(st_[:, 3:4], st_[:, 0:1], AF.Sqrt,
                                                       bias=self.cc("ln_eps"), scale=1.0 / D), inc=sse)
                p.wait("vector", sse, sse.n)
                p.run("vector", lambda v: v.reciprocal(st_[:, 3:4], st_[:, 3:4]))
                p.hz("vector")
                if g_bc is None:
                    p.run("vector", lambda v, b=b: v.tensor_scalar(
                        xnb[b][:, :], xc[b][:, :], st_[:, 3:4], None, ALU.mult), inc=sve)
                    if xn32 is not None:
                        p.wait("vector", sst, sst.n)
                        p.run("vector", lambda v, b=b: v.tensor_scalar(
                            xn32[:, :], xc[b][:, :], st_[:, 3:4], None, ALU.mult), inc=sve)
                else:
                    p.run("vector", lambda v, b=b: v.scalar_tensor_tensor(
                        sq[:, :], xc[b][:, :], st_[:, 3:4], gb[:, :], ALU.mult, ALU.mult))
                    p.run("vector", lambda v, b=b: v.tensor_add(xnb[b][:, :], sq[:, :], bb[:, :]), inc=sve)
                    if xn32 is not None:
                        p.wait("vector", sst, sst.n)
                        p.run("vector", lambda v: v.tensor_add(xn32[:, :], sq[:, :], bb[:, :]), inc=sve)
                if xn32 is not None:
                    p.wait("sync", sve, sve.n)
                    p.dma("sync", xn_tm_hbm[tt * 128:(tt + 1) * 128, :], xn32[:, :], sst)
                p.wait("tensor", sve, sve.n)
                if tt >= 2:
                    p.wait("tensor", sve, done[b])
                for c in range(8):
                    p.run("tensor", lambda t, b=b, c=c: t.transpose(
                        ps[b][:, c * 128:(c + 1) * 128], xnb[b][:, c * 128:(c + 1) * 128],
                        self.identb[:, :]), inc=(spe if c == 7 else None))
                p.wait("vector", spe, spe.n)
                p.run("vector", lambda v, b=b, tt=tt: v.tensor_copy(
                    xn_fm[:, :, tt * 128:(tt + 1) * 128],
                    ps[b][:, :].rearrange("p (c t) -> p c t", c=8)), inc=sve)
                done[b] = sve.n
                if DEBUG_SERIAL:
                    p.barrier()
        p.barrier()


    def l1attn(self, x_in, x_out, uid):
        nc, p = self.nc, self.p
        qf_s = nc.dram_tensor(uid + "_qf", [NH, S, HD], BF16, kind="Internal").ap()
        kf_s = nc.dram_tensor(uid + "_kf", [NH, S, HD], BF16, kind="Internal").ap()
        v_s = nc.dram_tensor(uid + "_v", [NH, S, HD], BF16, kind="Internal").ap()

        xn_fm = self.fmA
        self.ln_to_fm(x_in, xn_fm, uid + "ln", banks=(0, 1))
        self.maybe_dump("xnfm", xn_fm[:, :, :], [128, 8, S], BF16)

        swl = p.sem(uid + "_wl")
        sld = p.sem(uid + "_ld")
        spe = p.sem(uid + "_pe")
        sve = p.sem(uid + "_ve")
        sse = p.sem(uid + "_se")
        sst = p.sem(uid + "_st")

        flat = self.fmB[:, :, :].rearrange("p c t -> p (c t)")

        def carve32(lo_k, n_elems_f32):
            return flat[:, lo_k:lo_k + 2 * n_elems_f32].bitcast(F32)

        qkv = carve32(0, 3072)
        ta = carve32(6144, 2048)
        tb = carve32(10240, 2048)
        tc = carve32(14336, 2048)
        ct = carve32(18432, 2048)
        stb = carve32(22528, 2048)
        outb = flat[:, 26624:29696]

        with p.scope():
            wq3 = p.sbuf(uid + "_wq", [128, 8, 3072], BF16)
            wq = wq3[:, :, :]
            bq_bc = p.sbuf(uid + "_bqbc", [128, 3072], F32)
            gqk = p.sbuf(uid + "_gqk", [128, 2 * D], F32)
            p.dma("gpsimd", wq,
                  self.hbm["l1a_wqkvT"].rearrange("(kc p) n -> p kc n", p=128), swl)
            p.dma("gpsimd", bq_bc[:, :], self.hbm["l1a_bqkv_bc"][:, :], swl)
            p.dma("gpsimd", gqk[:, :], self.hbm["l1a_gqk_bc"][:, :], swl)
            rst = p.sbuf(uid + "_rst", [128, 4], F32)
            sqscr = p.sbuf(uid + "_sqscr", [128, 2 * D], F32)
            pq = [self.pbank(2), self.pbank(3)]

            def slabs(apx, lo, hi):
                return apx[:, 0:2048].rearrange("p (h td) -> p h td", h=NH)[:, :, lo:hi]

            evac_done = {0: 0, 1: 0}
            st_done = 0
            p.wait("tensor", swl, swl.n)
            for tt in range(TT):
                if tt >= 1:
                    p.wait("sync", sve, ve_done)
                p.dma("sync", ct, self.hbm["rope_cos"][tt * 128:(tt + 1) * 128, :], sld)
                p.dma("sync", stb, self.hbm["rope_sgnsin"][tt * 128:(tt + 1) * 128, :], sld)
                for nb in range(6):
                    pbix = nb % 2
                    if evac_done[pbix]:
                        p.wait("tensor", sve, evac_done[pbix])
                    for kc in range(8):
                        p.run("tensor", lambda t, tt=tt, nb=nb, kc=kc, pbix=pbix: t.matmul(
                            pq[pbix][:, :], xn_fm[:, kc, tt * 128:(tt + 1) * 128],
                            wq[:, kc, nb * 512:(nb + 1) * 512],
                            start=(kc == 0), stop=(kc == 7)),
                            inc=(spe if kc == 7 else None))
                    p.wait("vector", spe, spe.n)
                    if nb == 0:
                        p.wait("vector", sst, st_done)
                    p.run("vector", lambda v, nb=nb, pbix=pbix: v.tensor_add(
                        qkv[:, nb * 512:(nb + 1) * 512], pq[pbix][:, :],
                        bq_bc[:, nb * 512:(nb + 1) * 512]), inc=sve)
                    evac_done[pbix] = sve.n
                p.wait("scalar", sve, sve.n)
                p.run("scalar", lambda s: s.activation(
                    slabs(sqscr, 0, HD), slabs(qkv, 0, HD), AF.Square, accum_out=rst[:, 0:1]))
                p.run("scalar", lambda s: s.activation(
                    slabs(sqscr, HD, 128), slabs(qkv, HD, 128), AF.Square, accum_out=rst[:, 2:3]))
                p.hz("scalar")
                p.run("scalar", lambda s: s.activation(rst[:, 1:2], rst[:, 0:1], AF.Sqrt,
                                                       bias=self.cc("rms_eps"), scale=1.0 / D))
                p.run("scalar", lambda s: s.activation(rst[:, 3:4], rst[:, 2:3], AF.Sqrt,
                                                       bias=self.cc("rms_eps"), scale=1.0 / D), inc=sse)
                p.wait("vector", sse, sse.n)
                p.wait("vector", sld, sld.n)
                p.run("vector", lambda v: v.reciprocal(rst[:, 1:2], rst[:, 1:2]))
                p.run("vector", lambda v: v.reciprocal(rst[:, 3:4], rst[:, 3:4]))
                p.hz("vector")
                p.run("vector", lambda v: v.scalar_tensor_tensor(
                    slabs(ta, 0, HD), slabs(qkv, 0, HD), rst[:, 1:2],
                    slabs(gqk, 0, HD), ALU.mult, ALU.mult))
                p.run("vector", lambda v: v.scalar_tensor_tensor(
                    slabs(ta, HD, 128), slabs(qkv, HD, 128), rst[:, 3:4],
                    slabs(gqk, HD, 128), ALU.mult, ALU.mult))
                p.hz("vector")
                p.run("vector", lambda v: v.tensor_mul(tc[:, :], ta[:, :], ct[:, :]))
                p.run("vector", lambda v: v.tensor_copy(tb[:, 0:2048:2], ta[:, 1:2048:2]))
                p.run("vector", lambda v: v.tensor_copy(tb[:, 1:2048:2], ta[:, 0:2048:2]))
                p.run("vector", lambda v: v.tensor_mul(ta[:, :], tb[:, :], stb[:, :]))
                p.run("vector", lambda v: v.tensor_add(tb[:, :], tc[:, :], ta[:, :]))
                p.wait("vector", sst, st_done)
                p.run("vector", lambda v: v.tensor_scalar(
                    outb[:, 0:2048], tb[:, :], 0.0, None, ALU.max))
                p.run("vector", lambda v: v.tensor_copy(
                    outb[:, 2048:3072], qkv[:, 2048:3072]), inc=sve)
                ve_done = sve.n
                p.wait("scalar", sve, sve.n)
                for half in range(2):
                    r0 = half * 64
                    src_qk = outb[r0:r0 + 64, 0:2048].rearrange("p (h td) -> p h td", h=NH)
                    src_v = outb[r0:r0 + 64, 2048:3072].rearrange("p (vp two d) -> p vp two d", vp=8, two=2)
                    t0 = tt * 128 + r0
                    p.dma("scalar", qf_s[:, t0:t0 + 64, :].rearrange("h p d -> p h d"),
                          src_qk[:, :, 0:HD], sst)
                    p.dma("scalar", kf_s[:, t0:t0 + 64, :].rearrange("h p d -> p h d"),
                          src_qk[:, :, HD:128], sst)
                    p.dma("scalar", v_s[:, t0:t0 + 64, :].rearrange(
                        "(vp two) p d -> p vp two d", two=2), src_v, sst)
                st_done = sst.n
        p.barrier()
        self.maybe_dump("qfs", qf_s[:, :, :], [NH, S, HD], BF16)
        self.maybe_dump("kfs", kf_s[:, :, :], [NH, S, HD], BF16)
        self.maybe_dump("vs", v_s[:, :, :], [NH, S, HD], BF16)

        attn_fm = self.fmB
        with p.scope():
            kh = [p.sbuf(uid + f"_kh{i}", [128, TT, HD], BF16) for i in range(2)]
            vh = [p.sbuf(uid + f"_vh{i}", [128, TT, HD], BF16) for i in range(2)]
            qh = [p.sbuf(uid + f"_qh{i}", [128, TT, HD], BF16) for i in range(2)]
            vkt = p.sbuf(uid + "_vkt", [64, 65], BF16)
            qfm = p.sbuf(uid + "_qfm", [64, TT, 128], BF16)
            atm = p.sbuf(uid + "_atm", [128, TT, HD], BF16)
            rcp = p.sbuf(uid + "_rcp", [128, 1], F32)
            pvk = self.pbank(4)
            pqt = self.pbank_bf(5)
            pres = [self.pbank(6), self.pbank(7)]
            pat = self.pbank_bf(1)

            head_ve = {0: 0, 1: 0}
            head_pe = {0: 0, 1: 0}
            for h in range(NH):
                hb = h % 2
                if h >= 2:
                    p.wait("gpsimd", sve, head_ve[hb])
                    p.wait("gpsimd", spe, head_pe[hb])
                p.dma("gpsimd", kh[hb][:, :, :], kf_s[h].rearrange("(t p) d -> p t d", p=128), swl)
                p.dma("gpsimd", vh[hb][:, :, :], v_s[h].rearrange("(t p) d -> p t d", p=128), swl)
                p.dma("gpsimd", qh[hb][:, :, :], qf_s[h].rearrange("(t p) d -> p t d", p=128), swl)
                p.wait("tensor", swl, swl.n)
                for lt in range(TT):
                    p.run("tensor", lambda t, hb=hb, lt=lt: t.matmul(
                        pvk[0:64, 0:HD], kh[hb][:, lt, :], vh[hb][:, lt, :],
                        start=(lt == 0), stop=(lt == TT - 1)))
                    p.run("tensor", lambda t, hb=hb, lt=lt: t.matmul(
                        pvk[0:64, HD:65], kh[hb][:, lt, :], self.ones_b[:, :],
                        start=False, stop=(lt == TT - 1)),
                        inc=(spe if lt == TT - 1 else None))
                p.wait("scalar", spe, spe.n)
                p.run("scalar", lambda s: s.activation(vkt[:, :], pvk[0:64, 0:65], AF.Copy), inc=sse)
                for r in range(4):
                    for j in range(8):
                        lt = r * 8 + j
                        p.run("tensor", lambda t, hb=hb, lt=lt, j=j: t.transpose(
                            pqt[0:64, j * 128:(j + 1) * 128], qh[hb][:, lt, :],
                            self.identb[:, :]), inc=(spe if j == 7 else None))
                    p.wait("vector", spe, spe.n)
                    p.run("vector", lambda v, r=r: v.tensor_copy(
                        qfm[:, r * 8:(r + 1) * 8, :].rearrange("p t n -> p (t n)"),
                        pqt[0:64, :]), inc=sve)
                    p.wait("tensor", sve, sve.n)
                p.wait("tensor", sse, sse.n)
                res_done = {0: 0, 1: 0}
                for lt in range(TT):
                    pbix = lt % 2
                    if lt >= 2:
                        p.wait("tensor", sve, res_done[pbix])
                    p.run("tensor", lambda t, lt=lt, pbix=pbix: t.matmul(
                        pres[pbix][:, 0:65], qfm[:, lt, :], vkt[:, :],
                        start=True, stop=True), inc=spe)
                    p.wait("vector", spe, spe.n)
                    p.run("vector", lambda v, pbix=pbix: v.tensor_scalar(
                        rcp[:, :], pres[pbix][:, HD:65], self.cc("eps_lin"), None, ALU.add))
                    p.hz("vector")
                    p.run("vector", lambda v: v.reciprocal(rcp[:, :], rcp[:, :]))
                    p.hz("vector")
                    p.run("vector", lambda v, lt=lt, pbix=pbix: v.tensor_scalar(
                        atm[:, lt, :], pres[pbix][:, 0:HD], rcp[:, :], None, ALU.mult), inc=sve)
                    res_done[pbix] = sve.n
                p.wait("tensor", sve, sve.n)
                for r in range(4):
                    for j in range(8):
                        lt = r * 8 + j
                        p.run("tensor", lambda t, lt=lt, j=j: t.transpose(
                            pat[0:64, j * 128:(j + 1) * 128], atm[:, lt, :],
                            self.identb[:, :]), inc=(spe if j == 7 else None))
                    p.wait("vector", spe, spe.n)
                    p.run("vector", lambda v, h=h, r=r: v.tensor_copy(
                        attn_fm[(h % 2) * 64:(h % 2) * 64 + 64, h // 2,
                                r * 1024:(r + 1) * 1024], pat[0:64, :]), inc=sve)
                    p.wait("tensor", sve, sve.n)
                head_ve[hb] = sve.n
                head_pe[hb] = spe.n
        p.barrier()
        self.maybe_dump("attnfm", attn_fm[:, :, :], [128, 8, S], BF16)

        with p.scope():
            wo3 = p.sbuf(uid + "_wo", [128, 8, 1024], BF16)
            wo = wo3[:, :, :]
            p.dma("gpsimd", wo, self.hbm["l1a_woutT"].rearrange("(c p) n -> p c n", p=128), swl)
            bout = p.sbuf(uid + "_bout", [128, D], F32)
            gout = p.sbuf(uid + "_gout", [128, D], F32)
            p.dma("gpsimd", bout[:, :], self.hbm["l1a_bout_bc"][:, :], swl)
            p.dma("gpsimd", gout[:, :], self.hbm["l1a_gout_bc"][:, :], swl)

            xr = [p.sbuf(uid + f"_xr{i}", [128, D], F32) for i in range(2)]
            o32 = [p.sbuf(uid + f"_o32{i}", [128, D], F32) for i in range(2)]
            sq2 = p.sbuf(uid + "_sq2", [128, D], F32)
            st2 = p.sbuf(uid + "_st2", [128, 2], F32)
            po = [self.pbank(2), self.pbank(3)]
            xo = [p.sbuf(uid + f"_xo{i}", [128, D], F32) for i in range(2)]

            p.wait("tensor", swl, swl.n)
            e_done = {0: 0, 1: 0}
            po_done = {0: 0, 1: 0}
            for tt in range(TT):
                b = tt % 2
                if tt >= 2:
                    p.wait("sync", sst, e_done[b])
                p.dma("sync", xr[b][:, :], x_in[tt * 128:(tt + 1) * 128, :], sld)
                for nb in range(2):
                    if po_done[nb]:
                        p.wait("tensor", sve, po_done[nb])
                    for c in range(8):
                        p.run("tensor", lambda t, tt=tt, nb=nb, c=c: t.matmul(
                            po[nb][:, :], attn_fm[:, c, tt * 128:(tt + 1) * 128],
                            wo[:, c, nb * 512:(nb + 1) * 512],
                            start=(c == 0), stop=(c == 7)), inc=(spe if c == 7 else None))
                    p.wait("vector", spe, spe.n)
                    p.run("vector", lambda v, b=b, nb=nb: v.tensor_add(
                        o32[b][:, nb * 512:(nb + 1) * 512], po[nb][:, :],
                        bout[:, nb * 512:(nb + 1) * 512]), inc=sve)
                    po_done[nb] = sve.n
                p.wait("scalar", sve, sve.n)
                p.run("scalar", lambda s, b=b: s.activation(sq2[:, :], o32[b][:, :], AF.Square,
                                                            accum_out=st2[:, 0:1]))
                p.hz("scalar")
                p.run("scalar", lambda s: s.activation(st2[:, 1:2], st2[:, 0:1], AF.Sqrt,
                                                       bias=self.cc("rms_eps"), scale=1.0 / D), inc=sse)
                p.wait("vector", sse, sse.n)
                p.wait("vector", sld, sld.n)
                p.run("vector", lambda v: v.reciprocal(st2[:, 1:2], st2[:, 1:2]))
                p.hz("vector")
                p.run("vector", lambda v, b=b: v.scalar_tensor_tensor(
                    sq2[:, :], o32[b][:, :], st2[:, 1:2], gout[:, :], ALU.mult, ALU.mult))
                p.run("vector", lambda v, b=b: v.tensor_add(xo[b][:, :], xr[b][:, :], sq2[:, :]), inc=sve)
                p.wait("sync", sve, sve.n)
                p.dma("sync", x_out[tt * 128:(tt + 1) * 128, :], xo[b][:, :], sst)
                e_done[b] = sst.n
        p.barrier()
        return sst


    def l0ffn(self, x_in, x_out, uid):
        nc, p = self.nc, self.p
        NE = 9
        DFF = 2048
        xn_fm = self.fmA
        self.ln_to_fm(x_in, xn_fm, uid + "ln", banks=(0, 1))

        swl = p.sem(uid + "_wl")
        sld = p.sem(uid + "_ld")
        spe = p.sem(uid + "_pe")
        sve = p.sem(uid + "_ve")
        sse = p.sem(uid + "_se")
        sst = p.sem(uid + "_st")

        acc = self.fmB[:, :, :].rearrange("p c t -> p (c t)").bitcast(F32)

        with p.scope():
            gw = p.sbuf(uid + "_gw", [128, 8, 8], BF16)
            beta = p.sbuf(uid + "_beta", [128, 8], F32)
            z = p.sbuf(uid + "_z", [128, 8], F32)
            cmp_ = p.sbuf(uid + "_cmp", [128, 8], F32)
            rank = p.sbuf(uid + "_rank", [128, 8], F32)
            sig = p.sbuf(uid + "_sig", [128, 8], F32)
            pl = self.pbank(6)
            p.dma("gpsimd", gw[:, :, :],
                  self.hbm["l0f_gatewT"].rearrange("(kc p) e -> p kc e", p=128), swl)
            p.dma("gpsimd", beta[:, :], self.hbm["l0f_beta_bc"][:, :], swl)
            p.run("vector", lambda v: v.memset(self.c_tm[:, :, 8:9], 1.0), inc=sve)
            p.wait("tensor", swl, swl.n)
            prev_se = 0
            for tc in range(TT):
                if prev_se:
                    p.wait("tensor", sse, prev_se)
                for kc in range(8):
                    p.run("tensor", lambda t, tc=tc, kc=kc: t.matmul(
                        pl[:, 0:8], xn_fm[:, kc, tc * 128:(tc + 1) * 128],
                        gw[:, kc, :], start=(kc == 0), stop=(kc == 7)),
                        inc=(spe if kc == 7 else None))
                p.wait("vector", spe, spe.n)
                p.run("vector", lambda v: v.tensor_add(z[:, :], pl[:, 0:8], beta[:, :]), inc=sve)
                p.hz("vector")
                for j in range(8):
                    p.run("vector", lambda v, j=j: v.tensor_scalar(
                        cmp_[:, :], z[:, :], z[:, j:j + 1], None, ALU.is_lt))
                    if j == 0:
                        p.hz("vector")
                        p.run("vector", lambda v: v.tensor_copy(rank[:, :], cmp_[:, :]))
                    else:
                        p.hz("vector")
                        p.run("vector", lambda v: v.tensor_add(rank[:, :], rank[:, :], cmp_[:, :]))
                    p.hz("vector")
                p.run("vector", lambda v: v.tensor_scalar(
                    cmp_[:, :], rank[:, :], 2.0, None, ALU.is_lt), inc=sve)
                p.wait("scalar", spe, spe.n)
                p.wait("scalar", sve, sve.n)
                p.run("scalar", lambda s: s.activation(sig[:, :], pl[:, 0:8], AF.Sigmoid), inc=sse)
                prev_se = sse.n
                p.wait("vector", sse, sse.n)
                p.run("vector", lambda v, tc=tc: v.tensor_mul(
                    self.c_tm[:, tc, 0:8], sig[:, :], cmp_[:, :]), inc=sve)
        p.barrier()

        TSN = 2
        NQ = 4
        NTT = 4
        with p.scope():
            wg = [p.sbuf(uid + f"_wg{i}", [128, 8, 512], BF16) for i in range(2)]
            wu = [p.sbuf(uid + f"_wu{i}", [128, 8, 512], BF16) for i in range(2)]
            wd = [p.sbuf(uid + f"_wd{i}", [128, 4, 1024], BF16) for i in range(2)]
            hbf = p.sbuf(uid + "_hbf", [128, 4, 512], BF16)
            sg = p.sbuf(uid + "_sg", [128, 512], F32)
            xr = [p.sbuf(uid + f"_xr{i}", [128, D], F32) for i in range(2)]
            xo = [p.sbuf(uid + f"_xo{i}", [128, D], F32) for i in range(2)]

            wdone = {0: 0, 1: 0}
            st_done = {0: 0, 1: 0}
            out_done = {0: 0, 1: 0}
            gu_done = {}
            for ts in range(TSN):
                p.wait("vector", sst, sst.n)
                p.run("vector", lambda v: v.memset(acc[:, :], 0.0), inc=sve)
                p.wait("tensor", sve, sve.n)
                for e in range(NE):
                    for q in range(NQ):
                        it = (ts * NE + e) * NQ + q
                        wb = it % 2
                        if it >= 2:
                            p.wait("gpsimd", spe, wdone[wb])
                        p.dma("gpsimd", wg[wb][:, :, :],
                              self.hbm["l0f_wg"][e][:, q * 512:(q + 1) * 512]
                              .rearrange("(kc p) n -> p kc n", p=128), swl)
                        p.dma("gpsimd", wu[wb][:, :, :],
                              self.hbm["l0f_wu"][e][:, q * 512:(q + 1) * 512]
                              .rearrange("(kc p) n -> p kc n", p=128), swl)
                        p.dma("gpsimd", wd[wb][:, :, :],
                              self.hbm["l0f_wd"][e][q * 512:(q + 1) * 512, :]
                              .rearrange("(dc p) n -> p dc n", p=128), swl)
                        p.wait("tensor", swl, swl.n)
                        for tt in range(NTT):
                            t0 = ts * 2048 + tt * 512
                            for dc in range(4):
                                bA = self.pbank((dc % 2) * 2)
                                bB = self.pbank((dc % 2) * 2 + 1)
                                key = dc % 2
                                if gu_done.get(key):
                                    p.wait("tensor", sve, gu_done[key][0])
                                    p.wait("tensor", sse, gu_done[key][1])
                                for kc in range(8):
                                    p.run("tensor", lambda t, wb=wb, dc=dc, kc=kc, t0=t0, bA=bA: t.matmul(
                                        bA[:, :], wg[wb][:, kc, dc * 128:(dc + 1) * 128],
                                        xn_fm[:, kc, t0:t0 + 512],
                                        start=(kc == 0), stop=(kc == 7)),
                                        inc=(spe if kc == 7 else None))
                                for kc in range(8):
                                    p.run("tensor", lambda t, wb=wb, dc=dc, kc=kc, t0=t0, bB=bB: t.matmul(
                                        bB[:, :], wu[wb][:, kc, dc * 128:(dc + 1) * 128],
                                        xn_fm[:, kc, t0:t0 + 512],
                                        start=(kc == 0), stop=(kc == 7)),
                                        inc=(spe if kc == 7 else None))
                                p.wait("scalar", spe, spe.n - 1)
                                p.wait("scalar", sve, sve.n)
                                p.run("scalar", lambda s, bA=bA: s.activation(
                                    sg[:, :], bA[:, :], AF.Silu), inc=sse)
                                p.wait("vector", sse, sse.n)
                                p.wait("vector", spe, spe.n)
                                p.run("vector", lambda v, dc=dc, bB=bB: v.tensor_mul(
                                    hbf[:, dc, :], sg[:, :], bB[:, :]), inc=sve)
                                gu_done[dc % 2] = (sve.n, sse.n)
                            p.wait("tensor", sve, sve.n)
                            for tck in range(4):
                                ob = 4 + (tck % 2) * 2
                                pO = [self.pbank(ob), self.pbank(ob + 1)]
                                if out_done[tck % 2]:
                                    p.wait("tensor", sve, out_done[tck % 2])
                                for dc in range(4):
                                    for nb in range(2):
                                        p.run("tensor", lambda t, wb=wb, dc=dc, nb=nb, tck=tck, pO=pO: t.matmul(
                                            pO[nb][:, :],
                                            hbf[:, dc, tck * 128:(tck + 1) * 128],
                                            wd[wb][:, dc, nb * 512:(nb + 1) * 512],
                                            start=(dc == 0), stop=(dc == 3)),
                                            inc=(spe if (dc == 3 and nb == 1) else None))
                                p.wait("vector", spe, spe.n)
                                gtc = (t0 // 128) + tck
                                for nb in range(2):
                                    p.run("vector", lambda v, e=e, nb=nb, gtc=gtc, pO=pO: v.scalar_tensor_tensor(
                                        acc[:, (gtc % 16) * 1024 + nb * 512:(gtc % 16) * 1024 + (nb + 1) * 512],
                                        pO[nb][:, :], self.c_tm[:, gtc, e:e + 1],
                                        acc[:, (gtc % 16) * 1024 + nb * 512:(gtc % 16) * 1024 + (nb + 1) * 512],
                                        ALU.mult, ALU.add), inc=sve)
                                out_done[tck % 2] = sve.n
                        wdone[wb] = spe.n
                for tc in range(16):
                    b = tc % 2
                    gtc = ts * 16 + tc
                    if tc >= 2:
                        p.wait("sync", sst, st_done[b])
                    p.dma("sync", xr[b][:, :], x_in[gtc * 128:(gtc + 1) * 128, :], sld)
                    p.wait("vector", sld, sld.n)
                    p.wait("vector", sst, st_done[b])
                    p.run("vector", lambda v, b=b, tc=tc: v.scalar_tensor_tensor(
                        xo[b][:, :], acc[:, tc * 1024:(tc + 1) * 1024], 0.5,
                        xr[b][:, :], ALU.mult, ALU.add), inc=sve)
                    p.wait("sync", sve, sve.n)
                    p.dma("sync", x_out[gtc * 128:(gtc + 1) * 128, :], xo[b][:, :], sst)
                    st_done[b] = sst.n
        p.barrier()
        return sst

    def l1ffn(self, x_in, x_out, uid):
        nc, p = self.nc, self.p
        xin_s = nc.dram_tensor(uid + "_xin", [S, D], F32, kind="Internal").ap()
        xmid_s = nc.dram_tensor(uid + "_xmid", [S, D], F32, kind="Internal").ap()

        swl = p.sem(uid + "_wl")
        sld = p.sem(uid + "_ld")
        spe = p.sem(uid + "_pe")
        sve = p.sem(uid + "_ve")
        sse = p.sem(uid + "_se")
        sst = p.sem(uid + "_st")

        self.ln_to_fm(x_in, self.fmA, uid + "lnA", xn_tm_hbm=xin_s)
        self.ln_to_fm(xin_s, self.fmA, uid + "lnB",
                      g_bc=self.hbm["l1f_ln1g_bc"][:, :], b_bc=self.hbm["l1f_ln1b_bc"][:, :])
        h_fm = self.fmA
        attn_fm = self.fmB
        self.maybe_dump("hfm", self.fmA[:, :, :], [128, 8, S], BF16)

        with p.scope():
            wslab = [p.sbuf(uid + f"_ws{i}", [128, 8, 128], BF16) for i in range(2)]
            bq = p.sbuf(uid + "_bq", [128, 24], F32)
            qt_ = [p.sbuf(uid + f"_qt{i}", [64, 4096], BF16) for i in range(2)]
            kt_ = [p.sbuf(uid + f"_kt{i}", [64, 4096], BF16) for i in range(2)]
            vh_ = [p.sbuf(uid + f"_vh{i}", [64, 4096], BF16) for i in range(2)]
            vt = [p.sbuf(uid + f"_vt{i}", [128, TT, 65], BF16) for i in range(2)]
            esc = [p.sbuf(uid + f"_esc{i}", [128, 512], BF16) for i in range(2)]
            rcq = p.sbuf(uid + "_rcq", [1, 512], F32)
            rb = p.sbuf(uid + "_rb", [64, 512], F32)
            p.dma("gpsimd", bq[:, :], self.hbm["l1f_bqkv"][:, :], swl)
            pslab = [self.pbank(0), self.pbank(1)]
            psc = [self.pbank(2), self.pbank(3)]
            pat = self.pbank(4)
            pbc = self.pbank(5)
            ptr = self.pbank_bf(6)
            wq_h = self.hbm["l1f_wqkv"]

            def emit_slab(slab_idx, wb, dst_top, dst_bot, evac_done_prev):
                p.wait("gpsimd", spe, evac_done_prev)
                p.dma("gpsimd", wslab[wb][:, :, :],
                      wq_h[:, slab_idx * 128:(slab_idx + 1) * 128]
                      .rearrange("(kc p) m -> p kc m", p=128), swl)
                p.wait("tensor", swl, swl.n)
                for r in range(8):
                    pb = pslab[r % 2]
                    if r >= 2:
                        p.wait("tensor", sse, sse.n - 2)
                    for kc in range(8):
                        p.run("tensor", lambda t, wb=wb, r=r, kc=kc, pb=pb: t.matmul(
                            pb[:, :], wslab[wb][:, kc, :],
                            h_fm[:, kc, r * 512:(r + 1) * 512],
                            start=(kc == 0), stop=(kc == 7)),
                            inc=(spe if kc == 7 else None))
                    p.wait("scalar", spe, spe.n)
                    p.run("scalar", lambda s, r=r, pb=pb, dst_top=dst_top, slab_idx=slab_idx:
                          s.activation(dst_top[:, r * 512:(r + 1) * 512], pb[0:64, :],
                                       AF.Identity, bias=bq[0:64, slab_idx:slab_idx + 1]), inc=sse)
                    p.run("scalar", lambda s, r=r, pb=pb, dst_bot=dst_bot, slab_idx=slab_idx:
                          s.activation(dst_bot[:, r * 512:(r + 1) * 512], pb[64:128, :],
                                       AF.Identity, bias=bq[64:128, slab_idx:slab_idx + 1]), inc=sse)
                return spe.n

            qk_used = {0: 0, 1: 0}
            slab_mm_done = 0
            vt_used = 0
            for h in range(NH):
                hb = h % 2
                p.wait("scalar", spe, qk_used[hb])
                slab_mm_done = emit_slab(h, hb, qt_[hb], kt_[hb], slab_mm_done)
                if h % 2 == 0:
                    p.wait("scalar", spe, vt_used)
                    slab_mm_done = emit_slab(16 + h // 2, 1 - hb, vh_[0], vh_[1], slab_mm_done)
                    p.wait("tensor", sse, sse.n)
                    for half in range(2):
                        for r4 in range(4):
                            if not (half == 0 and r4 == 0):
                                p.wait("tensor", sve, sve.n)
                            for j in range(8):
                                ltk = r4 * 8 + j
                                p.run("tensor", lambda t, half=half, ltk=ltk, j=j: t.transpose(
                                    ptr[:, j * 64:(j + 1) * 64],
                                    vh_[half][:, ltk * 128:(ltk + 1) * 128],
                                    self.identb[0:64, 0:64]), inc=(spe if j == 7 else None))
                            p.wait("vector", spe, spe.n)
                            p.run("vector", lambda v, half=half, r4=r4: v.tensor_copy(
                                vt[half][:, r4 * 8:(r4 + 1) * 8, 0:64],
                                ptr[:, 0:512].rearrange("p (t d) -> p t d", t=8)), inc=sve)
                    p.run("vector", lambda v: v.memset(vt[0][:, :, 64:65], 1.0))
                    p.run("vector", lambda v: v.memset(vt[1][:, :, 64:65], 1.0), inc=sve)
                    p.wait("tensor", sve, sve.n)
                p.wait("tensor", sse, sse.n)
                exp_done = {0: 0, 1: 0}
                pv_done = {0: 0, 1: 0}
                for qt in range(8):
                    for kc in range(TT):
                        eb = kc % 2
                        pb = psc[eb]
                        if exp_done[eb]:
                            p.wait("tensor", sse, exp_done[eb])
                        p.run("tensor", lambda t, hb=hb, kc=kc, qt=qt, pb=pb: t.matmul(
                            pb[:, :], kt_[hb][:, kc * 128:(kc + 1) * 128],
                            qt_[hb][:, qt * 512:(qt + 1) * 512],
                            start=True, stop=True), inc=spe)
                        qk_pe = spe.n
                        p.wait("scalar", spe, qk_pe)
                        if pv_done[eb]:
                            p.wait("scalar", spe, pv_done[eb])
                        p.run("scalar", lambda s, eb=eb, pb=pb: s.activation(
                            esc[eb][:, :], pb[:, :], AF.Exp,
                            bias=self.cc("neg_shift"), scale=0.125), inc=sse)
                        exp_done[eb] = sse.n
                        p.wait("tensor", sse, exp_done[eb])
                        p.run("tensor", lambda t, hb=hb, kc=kc, eb=eb: t.matmul(
                            pat[0:65, :], vt[hb][:, kc, :], esc[eb][:, :],
                            start=(kc == 0), stop=(kc == TT - 1)), inc=spe)
                        pv_done[eb] = spe.n
                    p.wait("vector", spe, spe.n)
                    p.run("vector", lambda v: v.tensor_copy(rcq[0:1, :], pat[64:65, :]))
                    p.hz("vector")
                    p.run("vector", lambda v: v.reciprocal(rcq[0:1, :], rcq[0:1, :]), inc=sve)
                    p.wait("tensor", sve, sve.n)
                    p.run("tensor", lambda t: t.matmul(
                        pbc[0:64, :], self.ones_row[0:1, 0:64], rcq[0:1, :],
                        start=True, stop=True), inc=spe)
                    p.wait("scalar", spe, spe.n)
                    p.run("scalar", lambda s: s.activation(rb[:, :], pbc[0:64, :], AF.Copy), inc=sse)
                    p.wait("vector", sse, sse.n)
                    p.run("vector", lambda v, h=h, qt=qt: v.tensor_mul(
                        attn_fm[(h % 2) * 64:(h % 2) * 64 + 64, h // 2, qt * 512:(qt + 1) * 512],
                        pat[0:64, :], rb[:, :]), inc=sve)
                    p.wait("tensor", sve, sve.n)
                qk_used[hb] = spe.n
                if h % 2 == 1:
                    vt_used = spe.n
        p.barrier()
        self.maybe_dump("attnfm", attn_fm[:, :, :], [128, 8, S], BF16)

        with p.scope():
            wo = p.sbuf(uid + "_wo", [128, 8, 1024], BF16)
            bo = p.sbuf(uid + "_bo", [128, D], F32)
            g2 = p.sbuf(uid + "_g2", [128, D], F32)
            b2c = p.sbuf(uid + "_b2c", [128, D], F32)
            gwl = p.sbuf(uid + "_gwl", [128, 8, 8], BF16)
            gbl = p.sbuf(uid + "_gbl", [128, 8], F32)
            xi = [p.sbuf(uid + f"_xi{i}", [128, D], F32) for i in range(2)]
            xm = [p.sbuf(uid + f"_xm{i}", [128, D], F32) for i in range(2)]
            xc2 = p.sbuf(uid + "_xc2", [128, D], F32)
            sqd = p.sbuf(uid + "_sqd", [128, D], F32)
            h2t = p.sbuf(uid + "_h2t", [128, D], BF16)
            st_ = p.sbuf(uid + "_st2", [128, 4], F32)
            zz = p.sbuf(uid + "_zz", [128, 8], F32)
            cmp_ = p.sbuf(uid + "_cmp", [128, 8], F32)
            rank = p.sbuf(uid + "_rank", [128, 8], F32)
            texp = p.sbuf(uid + "_texp", [128, 8], F32)
            tm_ = p.sbuf(uid + "_tm", [128, 8], F32)
            p.dma("gpsimd", wo[:, :, :], self.hbm["l1f_wo"].rearrange("(c p) n -> p c n", p=128), swl)
            p.dma("gpsimd", bo[:, :], self.hbm["l1f_bo_bc"][:, :], swl)
            p.dma("gpsimd", g2[:, :], self.hbm["l1f_ln2g_bc"][:, :], swl)
            p.dma("gpsimd", b2c[:, :], self.hbm["l1f_ln2b_bc"][:, :], swl)
            p.dma("gpsimd", gwl[:, :, :], self.hbm["l1f_gatew"].rearrange("(kc p) e -> p kc e", p=128), swl)
            p.dma("gpsimd", gbl[:, :], self.hbm["l1f_gateb_bc"][:, :], swl)
            po = [self.pbank(0), self.pbank(1)]
            pt2 = self.pbank_bf(2)
            plg = self.pbank(3)
            p.wait("tensor", swl, swl.n)
            st_done = {0: 0, 1: 0}
            po_done = {0: 0, 1: 0}
            lg_done = 0
            for tc in range(TT):
                b = tc % 2
                if tc >= 2:
                    p.wait("sync", sst, st_done[b])
                p.dma("sync", xi[b][:, :], xin_s[tc * 128:(tc + 1) * 128, :], sld)
                for nb in range(2):
                    if po_done[nb]:
                        p.wait("tensor", sve, po_done[nb])
                    for c in range(8):
                        p.run("tensor", lambda t, tc=tc, nb=nb, c=c: t.matmul(
                            po[nb][:, :], attn_fm[:, c, tc * 128:(tc + 1) * 128],
                            wo[:, c, nb * 512:(nb + 1) * 512],
                            start=(c == 0), stop=(c == 7)), inc=(spe if c == 7 else None))
                    p.wait("vector", spe, spe.n)
                    p.run("vector", lambda v, nb=nb: v.tensor_add(
                        sqd[:, nb * 512:(nb + 1) * 512], po[nb][:, :],
                        bo[:, nb * 512:(nb + 1) * 512]), inc=sve)
                    po_done[nb] = sve.n
                p.wait("vector", sld, sld.n)
                p.run("vector", lambda v, b=b: v.tensor_add(xm[b][:, :], sqd[:, :], xi[b][:, :]), inc=sve)
                p.wait("sync", sve, sve.n)
                p.dma("sync", xmid_s[tc * 128:(tc + 1) * 128, :], xm[b][:, :], sst)
                st_done[b] = sst.n
                p.run("vector", lambda v, b=b: v.reduce_sum(st_[:, 1:2], xm[b][:, :], axis=AX.X))
                p.hz("vector")
                p.run("vector", lambda v: v.tensor_scalar(st_[:, 2:3], st_[:, 1:2], 1.0 / D, None, ALU.mult))
                p.hz("vector")
                p.run("vector", lambda v, b=b: v.tensor_scalar(xc2[:, :], xm[b][:, :], st_[:, 2:3], None, ALU.subtract), inc=sve)
                p.wait("scalar", sve, sve.n)
                p.run("scalar", lambda s: s.activation(sqd[:, :], xc2[:, :], AF.Square, accum_out=st_[:, 0:1]))
                p.hz("scalar")
                p.run("scalar", lambda s: s.activation(st_[:, 3:4], st_[:, 0:1], AF.Sqrt,
                                                       bias=self.cc("ln_eps"), scale=1.0 / D), inc=sse)
                p.wait("vector", sse, sse.n)
                p.run("vector", lambda v: v.reciprocal(st_[:, 3:4], st_[:, 3:4]))
                p.hz("vector")
                p.run("vector", lambda v: v.scalar_tensor_tensor(
                    sqd[:, :], xc2[:, :], st_[:, 3:4], g2[:, :], ALU.mult, ALU.mult))
                p.run("vector", lambda v: v.tensor_add(h2t[:, :], sqd[:, :], b2c[:, :]), inc=sve)
                p.wait("tensor", sve, sve.n)
                for c in range(8):
                    p.run("tensor", lambda t, c=c: t.transpose(
                        pt2[:, c * 128:(c + 1) * 128], h2t[:, c * 128:(c + 1) * 128],
                        self.identb[:, :]), inc=(spe if c == 7 else None))
                p.wait("vector", spe, spe.n)
                p.run("vector", lambda v, tc=tc: v.tensor_copy(
                    self.fmA[:, :, tc * 128:(tc + 1) * 128],
                    pt2[:, :].rearrange("p (c t) -> p c t", c=8)), inc=sve)
                if lg_done:
                    p.wait("tensor", sse, lg_done)
                p.wait("tensor", sve, sve.n)
                for kc in range(8):
                    p.run("tensor", lambda t, tc=tc, kc=kc: t.matmul(
                        plg[:, 0:8], self.fmA[:, kc, tc * 128:(tc + 1) * 128], gwl[:, kc, :],
                        start=(kc == 0), stop=(kc == 7)), inc=(spe if kc == 7 else None))
                p.wait("vector", spe, spe.n)
                p.run("vector", lambda v: v.tensor_add(zz[:, :], plg[:, 0:8], gbl[:, :]))
                p.hz("vector")
                p.run("vector", lambda v: v.reduce_max(st_[:, 0:1], zz[:, :], axis=AX.X, negate=True), inc=sve)
                p.hz("vector")
                for j in range(8):
                    p.run("vector", lambda v, j=j: v.tensor_scalar(
                        cmp_[:, :], zz[:, :], zz[:, j:j + 1], None, ALU.is_lt))
                    p.hz("vector")
                    if j == 0:
                        p.run("vector", lambda v: v.tensor_copy(rank[:, :], cmp_[:, :]))
                    else:
                        p.run("vector", lambda v: v.tensor_add(rank[:, :], rank[:, :], cmp_[:, :]))
                    p.hz("vector")
                p.run("vector", lambda v: v.tensor_scalar(
                    rank[:, :], rank[:, :], 2.0, None, ALU.is_lt), inc=sve)
                p.wait("scalar", sve, sve.n)
                p.run("scalar", lambda s: s.activation(texp[:, :], zz[:, :], AF.Exp,
                                                       bias=st_[:, 0:1]), inc=sse)
                p.wait("vector", sse, sse.n)
                p.run("vector", lambda v: v.tensor_mul(tm_[:, :], texp[:, :], rank[:, :]))
                p.hz("vector")
                p.run("vector", lambda v: v.reduce_sum(st_[:, 1:2], tm_[:, :], axis=AX.X))
                p.hz("vector")
                p.run("vector", lambda v: v.reciprocal(st_[:, 1:2], st_[:, 1:2]))
                p.hz("vector")
                p.run("vector", lambda v, tc=tc: v.tensor_scalar(
                    self.c_tm[:, tc, 0:8], tm_[:, :], st_[:, 1:2], None, ALU.mult), inc=sve)
                lg_done = sve.n
        p.barrier()
        self.maybe_dump("xmid", xmid_s, [S, D], F32)
        self.maybe_dump("h2fm", self.fmA[:, :, :], [128, 8, S], BF16)
        self.maybe_dump("ctm", self.c_tm[:, :, :], [128, TT, 9], F32)

        ewt = p.sbuf(uid + "_ewt", [8, TT, 128], BF16)
        with p.scope():
            pew = self.pbank(5)
            for tc in range(TT):
                if tc:
                    p.wait("tensor", sse, sse.n)
                p.run("tensor", lambda t, tc=tc: t.transpose(
                    pew[0:8, 0:128], self.c_tm[:, tc, 0:8], self.identf[:, :]), inc=spe)
                p.wait("scalar", spe, spe.n)
                p.run("scalar", lambda s, tc=tc: s.activation(
                    ewt[:, tc, :], pew[0:8, 0:128], AF.Copy), inc=sse)
        p.barrier()

        NE2 = 8
        NQ = 4
        acc = self.fmB[:, :, :].rearrange("p c t -> p (c t)").bitcast(F32)
        with p.scope():
            w1q = [p.sbuf(uid + f"_w1q{i}", [128, 8, 1024], BF16) for i in range(2)]
            w2q_s = p.sbuf(uid + "_w2q", [128, 8, 1024], BF16)
            w2q = [w2q_s, w2q_s]
            b1s = p.sbuf(uid + "_b1s", [128, 32], F32)
            b2s = p.sbuf(uid + "_b2s", [8, 1024], BF16)
            hid = p.sbuf(uid + "_hid", [128, 8, 512], BF16)
            _w1f0 = w1q[0][:, :, :].rearrange("p a b -> p (a b)")
            _w1f1 = w1q[1][:, :, :].rearrange("p a b -> p (a b)")
            xr = [_w1f0[:, 0:2048].bitcast(F32)] * 2
            xml = [_w1f0[:, 2048:4096].bitcast(F32)] * 2
            t1 = _w1f1[:, 0:2048].bitcast(F32)
            xo = [_w1f1[:, 2048:4096].bitcast(F32)] * 2
            p.dma("gpsimd", b2s[:, :], self.hbm["l1f_b2"][:, :], swl)
            wdone = {0: 0, 1: 0}
            w2_done = 0
            b1_done = 0
            relu_done = {0: 0, 1: 0, 2: 0, 3: 0}
            moe_out_done = {0: 0, 1: 0}
            st_done = {0: 0, 1: 0}
            for ts in range(2):
                p.wait("vector", sst, sst.n)
                p.run("vector", lambda v: v.memset(acc[:, :], 0.0), inc=sve)
                p.wait("gpsimd", sst, sst.n)
                p.wait("gpsimd", sve, sve.n)
                for e in range(NE2):
                    p.wait("gpsimd", sse, b1_done)
                    p.dma("gpsimd", b1s[:, :], self.hbm["l1f_b1T"][e], swl)
                    for qr in range(NQ):
                        it = (ts * NE2 + e) * NQ + qr
                        wb = it % 2
                        if it >= 2:
                            p.wait("gpsimd", spe, wdone[wb])
                        p.dma("gpsimd", w1q[wb][:, :, :],
                              self.hbm["l1f_W1"][e][:, qr * 1024:(qr + 1) * 1024]
                              .rearrange("(kc p) n -> p kc n", p=128), swl)
                        if it >= 1:
                            p.wait("gpsimd", spe, w2_done)
                        p.dma("gpsimd", w2q_s[:, :, :],
                              self.hbm["l1f_W2"][e][qr * 1024:(qr + 1) * 1024, :]
                              .rearrange("(dc p) n -> p dc n", p=128), swl)
                        p.wait("tensor", swl, swl.n)
                        for tt in range(4):
                            t0 = tt * 512
                            for dc in range(8):
                                hb_ = self.pbank(dc % 4)
                                if relu_done[dc % 4]:
                                    p.wait("tensor", sse, relu_done[dc % 4])
                                for kc in range(8):
                                    p.run("tensor", lambda t, wb=wb, dc=dc, kc=kc, ts=ts, t0=t0, hb_=hb_: t.matmul(
                                        hb_[:, :], w1q[wb][:, kc, dc * 128:(dc + 1) * 128],
                                        self.fmA[:, kc, ts * 2048 + t0:ts * 2048 + t0 + 512],
                                        start=(kc == 0), stop=(kc == 7)),
                                        inc=(spe if kc == 7 else None))
                                p.wait("scalar", spe, spe.n)
                                p.run("scalar", lambda s, dc=dc, qr=qr, hb_=hb_: s.activation(
                                    hid[:, dc, :], hb_[:, :], AF.Relu,
                                    bias=b1s[:, qr * 8 + dc:qr * 8 + dc + 1]), inc=sse)
                                relu_done[dc % 4] = sse.n
                            b1_done = sse.n
                            p.wait("tensor", sse, sse.n)
                            for tck in range(4):
                                ob = 4 + (tck % 2) * 2
                                pO = [self.pbank(ob), self.pbank(ob + 1)]
                                if moe_out_done[tck % 2]:
                                    p.wait("tensor", sve, moe_out_done[tck % 2])
                                for dc in range(8):
                                    for nb in range(2):
                                        p.run("tensor", lambda t, wb=wb, dc=dc, nb=nb, tck=tck, t0=t0, pO=pO: t.matmul(
                                            pO[nb][:, :],
                                            hid[:, dc, tck * 128:(tck + 1) * 128],
                                            w2q[wb][:, dc, nb * 512:(nb + 1) * 512],
                                            start=(dc == 0), stop=(dc == 7)),
                                            inc=(spe if (dc == 7 and nb == 1) else None))
                                p.wait("vector", spe, spe.n)
                                gtc = ts * 16 + (t0 // 128) + tck
                                lc = (t0 // 128) + tck
                                for nb in range(2):
                                    p.run("vector", lambda v, e=e, nb=nb, gtc=gtc, lc=lc, pO=pO: v.scalar_tensor_tensor(
                                        acc[:, lc * 1024 + nb * 512:lc * 1024 + (nb + 1) * 512],
                                        pO[nb][:, :], self.c_tm[:, gtc, e:e + 1],
                                        acc[:, lc * 1024 + nb * 512:lc * 1024 + (nb + 1) * 512],
                                        ALU.mult, ALU.add), inc=sve)
                                moe_out_done[tck % 2] = sve.n
                        wdone[wb] = spe.n
                        w2_done = spe.n
                p.wait("tensor", swl, swl.n)
                p.wait("sync", spe, spe.n)
                p.wait("vector", spe, spe.n)
                if ts == 0:
                    self.maybe_dump("acc0", acc[:, :], [128, 16384], F32)
                for lc in range(16):
                    b = lc % 2
                    gtc = ts * 16 + lc
                    pO = [self.pbank(4 + b * 2), self.pbank(5 + b * 2)]
                    if lc >= 1:
                        p.wait("sync", sst, sst.n)
                    p.dma("sync", xr[b][:, :], x_in[gtc * 128:(gtc + 1) * 128, :], sld)
                    p.dma("sync", xml[b][:, :], xmid_s[gtc * 128:(gtc + 1) * 128, :], sld)
                    p.wait("tensor", sve, sve.n)
                    for nb in range(2):
                        p.run("tensor", lambda t, gtc=gtc, nb=nb, pO=pO: t.matmul(
                            pO[nb][:, :], ewt[:, gtc, :], b2s[:, nb * 512:(nb + 1) * 512],
                            start=True, stop=True), inc=(spe if nb == 1 else None))
                    p.wait("vector", spe, spe.n)
                    p.wait("vector", sld, sld.n)
                    p.wait("vector", sst, st_done[b])
                    for nb in range(2):
                        p.run("vector", lambda v, lc=lc, nb=nb, pO=pO: v.tensor_add(
                            acc[:, lc * 1024 + nb * 512:lc * 1024 + (nb + 1) * 512],
                            acc[:, lc * 1024 + nb * 512:lc * 1024 + (nb + 1) * 512],
                            pO[nb][:, :]))
                    p.run("vector", lambda v, b=b, lc=lc: v.scalar_tensor_tensor(
                        t1[:, :], acc[:, lc * 1024:(lc + 1) * 1024], 0.5,
                        xr[b][:, :], ALU.mult, ALU.add))
                    p.run("vector", lambda v, b=b: v.scalar_tensor_tensor(
                        xo[b][:, :], xml[b][:, :], 0.5, t1[:, :], ALU.mult, ALU.add), inc=sve)
                    p.wait("sync", sve, sve.n)
                    p.dma("sync", x_out[gtc * 128:(gtc + 1) * 128, :], xo[b][:, :], sst)
                    st_done[b] = sst.n
        p.barrier()
        return sst

    def l0attn(self, x_in, x_out, uid):
        nc, p = self.nc, self.p
        srel_s = nc.dram_tensor(uid + "_srel", [NH, 9344], BF16, kind="Internal").ap()
        mat3 = nc.dram_tensor(uid + "_mat3", [128, NH, 8, 8192], BF16, kind="Internal").ap()

        swl = p.sem(uid + "_wl")
        sld = p.sem(uid + "_ld")
        spe = p.sem(uid + "_pe")
        sve = p.sem(uid + "_ve")
        sse = p.sem(uid + "_se")
        sst = p.sem(uid + "_st")
        smt = p.sem(uid + "_mt")

        with p.scope():
            relg = p.sbuf(uid + "_relg", [128, 1024], F32)
            amp = p.sbuf(uid + "_amp", [128, 21], F32)
            shp = p.sbuf(uid + "_shp", [128, 21], F32)
            off = p.sbuf(uid + "_off", [128, 21], F32)
            nsh = p.sbuf(uid + "_nsh", [128, 21], F32)
            dd = p.sbuf(uid + "_dd", [128, 1024], F32)
            d2 = p.sbuf(uid + "_d2", [128, 1024], F32)
            gk = p.sbuf(uid + "_gk", [128, 1024], F32)
            accs = p.sbuf(uid + "_accs", [128, 1024], F32)
            p.dma("sync", relg[:, :], self.hbm["l0a_relgrid"][:, :], sld)
            p.dma("sync", amp[:, :], self.hbm["l0a_amp"][:, :], sld)
            p.dma("sync", shp[:, :], self.hbm["l0a_sharp"][:, :], sld)
            p.dma("sync", off[:, :], self.hbm["l0a_off"][:, :], sld)
            p.wait("scalar", sld, sld.n)
            p.run("scalar", lambda s: s.activation(nsh[:, :], shp[:, :], AF.Abs), inc=sse)
            p.wait("vector", sse, sse.n)
            p.run("vector", lambda v: v.tensor_scalar(nsh[:, :], nsh[:, :], -1.0, None, ALU.mult))
            p.run("vector", lambda v: v.memset(accs[:, :], 0.0), inc=sve)
            p.wait("vector", sld, sld.n)
            for k in range(21):
                p.run("vector", lambda v, k=k: v.tensor_scalar(
                    dd[:, :], relg[:, :], off[:, k:k + 1], None, ALU.subtract), inc=sve)
                p.wait("scalar", sve, sve.n)
                p.run("scalar", lambda s: s.activation(d2[:, :], dd[:, :], AF.Square))
                p.run("scalar", lambda s, k=k: s.activation(gk[:, :], d2[:, :], AF.Exp,
                                                            scale=nsh[:, k:k + 1]), inc=sse)
                p.wait("vector", sse, sse.n)
                p.run("vector", lambda v, k=k: v.scalar_tensor_tensor(
                    accs[:, :], gk[:, :], amp[:, k:k + 1], accs[:, :],
                    ALU.mult, ALU.add), inc=sve)
            p.run("vector", lambda v: v.tensor_scalar(accs[:, :], accs[:, :], 8.0, None, ALU.mult), inc=sve)
            p.wait("gpsimd", sve, sve.n)
            p.dma("gpsimd", bass.AP(srel_s.tensor, 1024, [[9344, NH], [1024, 8], [1, 1024]]),
                  accs[:, :], sst)
            p.wait("sync", sst, sst.n)
            p.wait("scalar", sst, sst.n)
            for pp in range(128):
                for ci in range(8):
                    o0 = 1023 - pp - ci * 128
                    p.dma("sync" if (pp % 2 == 0) else "scalar", mat3[pp, :, ci, :],
                          bass.AP(srel_s.tensor, o0, [[9344, NH], [1, 8192]]), smt)

        xn_fm = self.fmA
        self.ln_to_fm(x_in, xn_fm, uid + "ln", banks=(0, 1))
        h_fm = xn_fm
        attn_fm = self.fmB

        with p.scope():
            wslab = [p.sbuf(uid + f"_ws{i}", [128, 8, 128], BF16) for i in range(2)]
            _qk1 = p.sbuf(uid + "_qk1", [128, 4096], BF16)
            _qk2 = p.sbuf(uid + "_qk2", [128, 4096], BF16)
            _vh2 = p.sbuf(uid + "_vh2", [128, 4096], BF16)
            qt_ = [_qk1[0:64, :], _qk1[64:128, :]]
            kt_ = [_qk2[0:64, :], _qk2[64:128, :]]
            vh_ = [_vh2[0:64, :], _vh2[64:128, :]]
            vt = [p.sbuf(uid + f"_vt{i}", [128, TT, 65], BF16) for i in range(2)]
            esc = [p.sbuf(uid + f"_esc{i}", [128, 512], BF16) for i in range(2)]
            tsc = p.sbuf(uid + "_tsc", [128, 512], F32)
            bt8 = [p.sbuf(uid + f"_bt8{i}", [128, 8, 512], BF16) for i in range(2)]
            rcq = p.sbuf(uid + "_rcq", [1, 512], F32)
            rb = p.sbuf(uid + "_rb", [64, 512], F32)
            pslab = [self.pbank(0), self.pbank(1)]
            psc = [self.pbank(2), self.pbank(3)]
            pat = self.pbank(4)
            pbc = self.pbank(5)
            ptr = self.pbank_bf(6)
            wq_h = self.hbm["l0a_win"]

            def emit_slab(slab_idx, wb, dst_top, dst_bot, evac_done_prev):
                p.wait("gpsimd", spe, evac_done_prev)
                p.dma("gpsimd", wslab[wb][:, :, :],
                      wq_h[:, slab_idx * 128:(slab_idx + 1) * 128]
                      .rearrange("(kc p) m -> p kc m", p=128), swl)
                p.wait("tensor", swl, swl.n)
                for r in range(8):
                    pb = pslab[r % 2]
                    if r >= 2:
                        p.wait("tensor", sse, sse.n - 2)
                    for kc in range(8):
                        p.run("tensor", lambda t, wb=wb, r=r, kc=kc, pb=pb: t.matmul(
                            pb[:, :], wslab[wb][:, kc, :],
                            h_fm[:, kc, r * 512:(r + 1) * 512],
                            start=(kc == 0), stop=(kc == 7)),
                            inc=(spe if kc == 7 else None))
                    p.wait("scalar", spe, spe.n)
                    p.run("scalar", lambda s, r=r, pb=pb, dst_top=dst_top:
                          s.activation(dst_top[:, r * 512:(r + 1) * 512], pb[0:64, :],
                                       AF.Copy), inc=sse)
                    p.run("scalar", lambda s, r=r, pb=pb, dst_bot=dst_bot:
                          s.activation(dst_bot[:, r * 512:(r + 1) * 512], pb[64:128, :],
                                       AF.Copy), inc=sse)
                return spe.n

            p.wait("tensor", smt, smt.n)
            qk_used = {0: 0, 1: 0}
            slab_mm_done = 0
            vt_used = 0
            bt_used = {0: 0, 1: 0}
            for h in range(NH):
                hb = h % 2
                p.wait("scalar", spe, qk_used[hb])
                slab_mm_done = emit_slab(h, hb, qt_[hb], kt_[hb], slab_mm_done)
                if h % 2 == 0:
                    p.wait("scalar", spe, vt_used)
                    slab_mm_done = emit_slab(16 + h // 2, 1 - hb, vh_[0], vh_[1], slab_mm_done)
                    p.wait("tensor", sse, sse.n)
                    for half in range(2):
                        for r4 in range(4):
                            if not (half == 0 and r4 == 0):
                                p.wait("tensor", sve, sve.n)
                            for j in range(8):
                                ltk = r4 * 8 + j
                                p.run("tensor", lambda t, half=half, ltk=ltk, j=j: t.transpose(
                                    ptr[:, j * 64:(j + 1) * 64],
                                    vh_[half][:, ltk * 128:(ltk + 1) * 128],
                                    self.identb[half * 64:half * 64 + 64,
                                                half * 64:half * 64 + 64]),
                                    inc=(spe if j == 7 else None))
                            p.wait("vector", spe, spe.n)
                            p.run("vector", lambda v, half=half, r4=r4: v.tensor_copy(
                                vt[half][:, r4 * 8:(r4 + 1) * 8, 0:64],
                                ptr[:, 0:512].rearrange("p (t d) -> p t d", t=8)), inc=sve)
                    p.run("vector", lambda v: v.memset(vt[0][:, :, 64:65], 1.0))
                    p.run("vector", lambda v: v.memset(vt[1][:, :, 64:65], 1.0), inc=sve)
                    p.wait("tensor", sve, sve.n)
                p.wait("tensor", sse, sse.n)
                exp_done = {0: 0, 1: 0}
                add_done = {0: 0, 1: 0}
                pv_done = {0: 0, 1: 0}
                for qt in range(8):
                    for kcg in range(4):
                        bb = kcg % 2
                        j0 = 4096 + qt * 512 - kcg * 1024
                        if bt_used[bb]:
                            p.wait("sync", sve, bt_used[bb])
                        p.dma("sync", bt8[bb][:, :, :],
                              bass.AP(mat3.tensor, h * 8 * 8192 + j0,
                                      [[NH * 8 * 8192, 128], [8192, 8], [1, 512]]), sld)
                        for ci in range(8):
                            kc = kcg * 8 + ci
                            eb = kc % 2
                            pb = psc[eb]
                            if add_done[eb]:
                                p.wait("tensor", sve, add_done[eb])
                            p.run("tensor", lambda t, hb=hb, kc=kc, qt=qt, pb=pb: t.matmul(
                                pb[:, :], kt_[hb][:, kc * 128:(kc + 1) * 128],
                                qt_[hb][:, qt * 512:(qt + 1) * 512],
                                start=True, stop=True), inc=spe)
                            p.wait("vector", spe, spe.n)
                            p.wait("vector", sld, sld.n)
                            if exp_done[eb]:
                                p.wait("vector", sse, exp_done[eb])
                            p.run("vector", lambda v, bb=bb, ci=ci, pb=pb: v.tensor_add(
                                tsc[:, :], pb[:, :], bt8[bb][:, ci, :]), inc=sve)
                            add_done[eb] = sve.n
                            if ci == 7:
                                bt_used[bb] = sve.n
                            p.wait("scalar", sve, add_done[eb])
                            if pv_done[eb]:
                                p.wait("scalar", spe, pv_done[eb])
                            p.run("scalar", lambda s, eb=eb: s.activation(
                                esc[eb][:, :], tsc[:, :], AF.Exp,
                                bias=self.cc("neg_shift"), scale=0.125), inc=sse)
                            exp_done[eb] = sse.n
                            p.wait("tensor", sse, exp_done[eb])
                            p.run("tensor", lambda t, hb=hb, kc=kc, eb=eb: t.matmul(
                                pat[0:65, :], vt[hb][:, kc, :], esc[eb][:, :],
                                start=(kc == 0), stop=(kc == TT - 1)), inc=spe)
                            pv_done[eb] = spe.n
                    p.wait("vector", spe, spe.n)
                    p.run("vector", lambda v: v.tensor_copy(rcq[0:1, :], pat[64:65, :]))
                    p.hz("vector")
                    p.run("vector", lambda v: v.reciprocal(rcq[0:1, :], rcq[0:1, :]), inc=sve)
                    p.wait("tensor", sve, sve.n)
                    p.run("tensor", lambda t: t.matmul(
                        pbc[0:64, :], self.ones_row[0:1, 0:64], rcq[0:1, :],
                        start=True, stop=True), inc=spe)
                    p.wait("scalar", spe, spe.n)
                    p.run("scalar", lambda s: s.activation(rb[:, :], pbc[0:64, :], AF.Copy), inc=sse)
                    p.wait("vector", sse, sse.n)
                    p.run("vector", lambda v, h=h, qt=qt: v.tensor_mul(
                        attn_fm[(h % 2) * 64:(h % 2) * 64 + 64, h // 2, qt * 512:(qt + 1) * 512],
                        pat[0:64, :], rb[:, :]), inc=sve)
                    p.wait("tensor", sve, sve.n)
                qk_used[hb] = spe.n
                if h % 2 == 1:
                    vt_used = spe.n
        p.barrier()

        with p.scope():
            wg2 = p.sbuf(uid + "_wg2", [128, 8, 2048], BF16)
            bga = p.sbuf(uid + "_bga", [128, D], F32)
            bgg = p.sbuf(uid + "_bgg", [128, D], F32)
            xr = [p.sbuf(uid + f"_xr{i}", [128, D], F32) for i in range(2)]
            asb = p.sbuf(uid + "_asb", [128, D], F32)
            gsb = p.sbuf(uid + "_gsb", [128, D], F32)
            gt = p.sbuf(uid + "_gt", [128, D], F32)
            xo = [p.sbuf(uid + f"_xo{i}", [128, D], F32) for i in range(2)]
            p.dma("gpsimd", wg2[:, :, :], self.hbm["l0a_wgate"].rearrange("(c p) n -> p c n", p=128), swl)
            p.dma("gpsimd", bga[:, :], self.hbm["l0a_bga_bc"][:, :], swl)
            p.dma("gpsimd", bgg[:, :], self.hbm["l0a_bgg_bc"][:, :], swl)
            pa = [self.pbank(0), self.pbank(1)]
            pg = [self.pbank(2), self.pbank(3)]
            p.wait("tensor", swl, swl.n)
            st_done = {0: 0, 1: 0}
            ga_done = 0
            gg_done = 0
            for tc in range(TT):
                b = tc % 2
                if tc >= 2:
                    p.wait("sync", sst, st_done[b])
                p.dma("sync", xr[b][:, :], x_in[tc * 128:(tc + 1) * 128, :], sld)
                if ga_done:
                    p.wait("tensor", sve, ga_done)
                    p.wait("tensor", sve, gg_done)
                for nb in range(2):
                    for c in range(8):
                        p.run("tensor", lambda t, tc=tc, nb=nb, c=c: t.matmul(
                            pa[nb][:, :], attn_fm[:, c, tc * 128:(tc + 1) * 128],
                            wg2[:, c, nb * 512:(nb + 1) * 512],
                            start=(c == 0), stop=(c == 7)), inc=(spe if c == 7 else None))
                for nb in range(2):
                    for c in range(8):
                        p.run("tensor", lambda t, tc=tc, nb=nb, c=c: t.matmul(
                            pg[nb][:, :], attn_fm[:, c, tc * 128:(tc + 1) * 128],
                            wg2[:, c, 1024 + nb * 512:1024 + (nb + 1) * 512],
                            start=(c == 0), stop=(c == 7)), inc=(spe if c == 7 else None))
                p.wait("vector", spe, spe.n - 2)
                for nb in range(2):
                    p.run("vector", lambda v, nb=nb: v.tensor_add(
                        asb[:, nb * 512:(nb + 1) * 512], pa[nb][:, :],
                        bga[:, nb * 512:(nb + 1) * 512]), inc=sve)
                ga_done = sve.n
                p.wait("vector", spe, spe.n)
                for nb in range(2):
                    p.run("vector", lambda v, nb=nb: v.tensor_add(
                        gt[:, nb * 512:(nb + 1) * 512], pg[nb][:, :],
                        bgg[:, nb * 512:(nb + 1) * 512]), inc=sve)
                gg_done = sve.n
                p.wait("scalar", sve, sve.n)
                p.run("scalar", lambda s: s.activation(gsb[:, :], gt[:, :], AF.Sigmoid), inc=sse)
                p.wait("vector", sse, sse.n)
                p.wait("vector", sld, sld.n)
                p.run("vector", lambda v: v.tensor_mul(asb[:, :], asb[:, :], gsb[:, :]))
                p.run("vector", lambda v, b=b: v.tensor_add(xo[b][:, :], xr[b][:, :], asb[:, :]), inc=sve)
                p.wait("sync", sve, sve.n)
                p.dma("sync", x_out[tc * 128:(tc + 1) * 128, :], xo[b][:, :], sst)
                st_done[b] = sst.n
        p.barrier()
        return sst



STAGE_SEQ = ("l0ffn", "l0attn", "l0ffn", "l1ffn", "l1attn", "l1ffn")


def build_program(g, stages):
    nc = bass.Bass("TRN2", target_bir_lowering=False, debug=False)
    x = nc.dram_tensor("x", [S, D], F32, kind="ExternalInput").ap()
    y = nc.dram_tensor("y", [S, D], F32, kind="ExternalOutput").ap()
    b = Builder(nc, g)
    scratch = [nc.dram_tensor(f"res{i}", [S, D], F32, kind="Internal").ap()
               for i in range(max(0, len(stages) - 1))]
    last = None
    for i, st in enumerate(stages):
        src = x if i == 0 else scratch[i - 1]
        dst = y if i == len(stages) - 1 else scratch[i]
        uid = f"s{i}"
        fn = {"l0ffn": b.l0ffn, "l0attn": b.l0attn,
              "l1ffn": b.l1ffn, "l1attn": b.l1attn}[st]
        with b.p.scope():
            last = fn(src, dst, uid)
    b.p.wait("sync", last, last.n)
    with b.p.es:
        with nc.Block() as block:
            b.p.emit_into(block)
    return nc


def make_in_maps(g, xs):
    in_maps = []
    for xc in xs:
        m = {"x": np.ascontiguousarray(xc)}
        for name, arr in g.items():
            if not name.startswith("_"):
                m["g_" + name] = arr
        in_maps.append(m)
    return in_maps


def kernel(**inputs):
    g = prep_host_inputs(inputs)
    x = np.asarray(inputs["L0_attn_x"])
    nc = build_program(g, list(STAGE_SEQ))
    res = run_bass_kernel_spmd(nc, make_in_maps(g, [x[c] for c in range(8)]),
                               core_ids=list(range(8)))
    return np.stack([np.asarray(res.results[c]["y"]) for c in range(8)],
                    axis=0).astype(np.float32)


# version 08
# speedup vs baseline: 1.1389x; 1.1389x over previous
import numpy as np
import ml_dtypes
from contextlib import ExitStack

import concourse.bass as bass
import concourse.mybir as mybir
from concourse.bass_utils import run_bass_kernel_spmd

F32 = mybir.dt.float32
BF16 = mybir.dt.bfloat16
AF = mybir.ActivationFunctionType
ALU = mybir.AluOpType
AX = mybir.AxisListType

S = 4096
D = 1024
NH = 16
HD = 64
TT = S // 128

LN_EPS = 1e-5
RMS_EPS = 1.1920929e-07
EPS_LIN = 1e-6
SHIFT = 20.0

ENGINES = ("sync", "gpsimd", "tensor", "vector", "scalar")
DEBUG_SERIAL = False
_DBG_TENSORS = {}



def _bc(v, rows=128):
    v = np.ascontiguousarray(np.asarray(v, dtype=np.float32).reshape(-1))
    return np.ascontiguousarray(np.broadcast_to(v[None, :], (rows, v.size)))


def prep_host_inputs(inp):
    g = {}
    g["ident_bf16"] = np.eye(128, dtype=ml_dtypes.bfloat16)
    g["ident_f32"] = np.eye(128, dtype=np.float32)
    cc = {"ln_eps": LN_EPS, "rms_eps": RMS_EPS, "eps_lin": EPS_LIN,
          "neg_shift": -SHIFT, "one": 1.0, "zero": 0.0}
    g["_ccols_order"] = sorted(cc)
    g["ccols"] = np.ascontiguousarray(
        np.stack([np.full(128, cc[k], dtype=np.float32) for k in g["_ccols_order"]], axis=1))
    g["ones_col_bf16"] = np.ones((128, 1), dtype=ml_dtypes.bfloat16)
    g["ones_row_f32"] = np.ones((1, 128), dtype=np.float32)

    i = np.arange(0, HD, 2, dtype=np.float64)
    inv_freq = 1.0 / (10000.0 ** (i / HD))
    ang = np.outer(np.arange(S, dtype=np.float64), inv_freq)
    cos64 = np.repeat(np.cos(ang), 2, axis=1).astype(np.float32)
    sin = np.sin(ang).astype(np.float32)
    sgnsin64 = np.empty((S, HD), dtype=np.float32)
    sgnsin64[:, 0::2] = -sin
    sgnsin64[:, 1::2] = sin
    g["rope_cos"] = np.ascontiguousarray(np.tile(cos64, (1, 2 * NH)))
    g["rope_sgnsin"] = np.ascontiguousarray(np.tile(sgnsin64, (1, 2 * NH)))

    Wqkv = np.asarray(inp["L1_attn_Wqkv"])
    bqkv = np.asarray(inp["L1_attn_bqkv"])
    WqkvT = Wqkv.T
    cols = []
    for h in range(NH):
        cols.extend(range(h * HD, h * HD + HD))
        cols.extend(range(D + h * HD, D + h * HD + HD))
    for vp in range(NH // 2):
        cols.extend(range(2 * D + vp * 128, 2 * D + vp * 128 + 128))
    cols = np.array(cols)
    g["l1a_wqkvT"] = np.ascontiguousarray(WqkvT[:, cols])
    g["l1a_bqkv_bc"] = _bc(bqkv[cols])
    gq = np.asarray(inp["L1_attn_gq"]); gk = np.asarray(inp["L1_attn_gk"])
    gqk = np.empty(2 * D, dtype=np.float32)
    for h in range(NH):
        gqk[h * 128:h * 128 + HD] = gq[h * HD:(h + 1) * HD]
        gqk[h * 128 + HD:h * 128 + 128] = gk[h * HD:(h + 1) * HD]
    g["l1a_gqk_bc"] = _bc(gqk)
    g["l1a_woutT"] = np.ascontiguousarray(np.asarray(inp["L1_attn_Wout"]).T)
    g["l1a_bout_bc"] = _bc(inp["L1_attn_bout"])
    g["l1a_gout_bc"] = _bc(inp["L1_attn_gout"])

    g["l0f_gатe"] = None; del g["l0f_gатe"]
    g["l0f_wg"] = np.ascontiguousarray(np.concatenate(
        [np.asarray(inp["L0_ffn_gate_proj_w"]),
         np.asarray(inp["L0_ffn_shared_gate_w"])], axis=0))
    g["l0f_wu"] = np.ascontiguousarray(np.concatenate(
        [np.asarray(inp["L0_ffn_up_proj_w"]),
         np.asarray(inp["L0_ffn_shared_up_w"])], axis=0))
    g["l0f_wd"] = np.ascontiguousarray(np.concatenate(
        [np.asarray(inp["L0_ffn_down_proj_w"]),
         np.asarray(inp["L0_ffn_shared_down_w"])], axis=0))
    g["l0f_gatewT"] = np.ascontiguousarray(np.asarray(inp["L0_ffn_gate_w"]).T)
    g["l0f_beta_bc"] = _bc(inp["L0_ffn_beta"])

    w_in = np.asarray(inp["L0_attn_w_in"])
    cols = []
    for h in range(NH):
        cols.extend(range(2 * D + h * HD, 2 * D + h * HD + HD))
        cols.extend(range(h * HD, h * HD + HD))
    for vp in range(NH // 2):
        cols.extend(range(D + vp * 128, D + vp * 128 + 128))
    cols = np.array(cols)
    g["l0a_win"] = np.ascontiguousarray(w_in[:, cols])
    g["l0a_wgate"] = np.ascontiguousarray(np.asarray(inp["L0_attn_w_gate"]))
    bgate = np.asarray(inp["L0_attn_b_gate"])
    g["l0a_bga_bc"] = _bc(bgate[0:D])
    g["l0a_bgg_bc"] = _bc(bgate[D:2 * D])
    rep = lambda a: np.ascontiguousarray(np.repeat(np.asarray(a, np.float32), 8, axis=0))
    g["l0a_amp"] = rep(inp["L0_attn_amplitudes"])
    g["l0a_sharp"] = rep(inp["L0_attn_sharpness"])
    g["l0a_off"] = rep(inp["L0_attn_offsets"])
    pgrid = np.arange(128) % 8
    g["l0a_relgrid"] = np.ascontiguousarray(
        (4095 - 1024 * pgrid[:, None] - np.arange(1024)[None, :]).astype(np.float32))

    g["l1f_ln1g_bc"] = _bc(inp["L1_ffn_ln1_g"])
    g["l1f_ln1b_bc"] = _bc(inp["L1_ffn_ln1_b"])
    g["l1f_ln2g_bc"] = _bc(inp["L1_ffn_ln2_g"])
    g["l1f_ln2b_bc"] = _bc(inp["L1_ffn_ln2_b"])
    wqkv = np.asarray(inp["L1_ffn_w_qkv"])
    bqkv2 = np.asarray(inp["L1_ffn_b_qkv"])
    cols = []
    for h in range(NH):
        cols.extend(range(h * HD, h * HD + HD))
        cols.extend(range(D + h * HD, D + h * HD + HD))
    for vp in range(NH // 2):
        cols.extend(range(2 * D + vp * 128, 2 * D + vp * 128 + 128))
    cols = np.array(cols)
    g["l1f_wqkv"] = np.ascontiguousarray(wqkv[:, cols])
    g["l1f_bqkv"] = np.ascontiguousarray(bqkv2[cols].reshape(24, 128).T)
    g["l1f_wo"] = np.ascontiguousarray(np.asarray(inp["L1_ffn_w_o"]))
    g["l1f_bo_bc"] = _bc(inp["L1_ffn_b_o"])
    g["l1f_gatew"] = np.ascontiguousarray(np.asarray(inp["L1_ffn_gate_w"]))
    g["l1f_gateb_bc"] = _bc(inp["L1_ffn_gate_b"])
    g["l1f_W1"] = np.ascontiguousarray(np.asarray(inp["L1_ffn_W1"]))
    g["l1f_W2"] = np.ascontiguousarray(np.asarray(inp["L1_ffn_W2"]))
    b1 = np.asarray(inp["L1_ffn_b1"])
    g["l1f_b1T"] = np.ascontiguousarray(b1.reshape(8, 32, 128).transpose(0, 2, 1))
    g["l1f_b2"] = np.ascontiguousarray(np.asarray(inp["L1_ffn_b2"]))
    return g


def prep_core_inputs(inp, core):
    x = np.asarray(inp["L0_attn_x"])
    return {"x": np.ascontiguousarray(x[core])}



class Sem:
    __slots__ = ("h", "n")
    def __init__(self, handle):
        self.h = handle
        self.n = 0


class Prog:

    def __init__(self, nc):
        self.nc = nc
        self.es = ExitStack()
        self._scopes = [self.es]
        self.q = {e: [] for e in ENGINES}
        self._sems = {}
        self._waited = {e: {} for e in ENGINES}
        self._hzt = self.es.enter_context(nc.sbuf_tensor("hzt", [128, 1], F32))

    def scope(self):
        import contextlib
        prog = self

        @contextlib.contextmanager
        def _cm():
            st = ExitStack()
            prog._scopes.append(st)
            try:
                yield st
            finally:
                prog._scopes.pop()
                st.close()
        return _cm()

    def barrier(self):
        for e in ENGINES:
            for sem in list(self._sems.values()):
                if sem.n > 0:
                    self.wait(e, sem, sem.n)

    def sem(self, name):
        if name not in self._sems:
            self._sems[name] = Sem(self.es.enter_context(self.nc.semaphore(name)))
        return self._sems[name]

    def sbuf(self, name, shape, dtype):
        return self._scopes[-1].enter_context(self.nc.sbuf_tensor(name, shape, dtype))

    def psum(self, name, shape, dtype=F32):
        return self.es.enter_context(self.nc.psum_tensor(name, shape, dtype))

    def wait(self, eng, sem, val=None):
        if val is None:
            val = sem.n
        if val <= self._waited[eng].get(sem, -1):
            return
        self._waited[eng][sem] = val
        self.q[eng].append(lambda e, s=sem.h, v=val: e.wait_ge(s, v))

    def hz(self, eng):
        sem = self.sem("hz_" + eng)
        eng_dummy = {"vector": lambda e: e.memset(self._hzt[:, :], 0.0),
                     "scalar": lambda e: e.memzero(self._hzt[:, :]),
                     "gpsimd": lambda e: e.memset(self._hzt[:, :], 0.0)}[eng]
        self.run(eng, eng_dummy, inc=sem)
        self.wait(eng, sem, sem.n)

    def run(self, eng, fn, inc=None, inc_by=1):
        if inc is not None:
            inc.n += inc_by
            self.q[eng].append((fn, inc, inc_by))
        else:
            self.q[eng].append(fn)

    def dma(self, eng, out, in_, sem, **kw):
        def fn(e, out=out, in_=in_, kw=kw):
            return e.dma_start(out=out, in_=in_, **kw)
        self.run(eng, fn, inc=sem, inc_by=16)

    def emit_into(self, block):
        for eng in ENGINES:
            items = self.q[eng]
            if not items:
                continue
            def body(e, items=items):
                for it in items:
                    if isinstance(it, tuple):
                        fn, sem, k = it
                        fn(e).then_inc(sem.h, k)
                    else:
                        it(e)
            getattr(block, eng)(body)
            self.q[eng] = []


class Builder:

    def __init__(self, nc, g):
        self.nc = nc
        self.p = Prog(nc)
        self.g = g
        self.hbm = {}
        for name, arr in g.items():
            if name.startswith("_"):
                continue
            dt = BF16 if arr.dtype == ml_dtypes.bfloat16 else F32
            self.hbm[name] = nc.dram_tensor("g_" + name, list(arr.shape), dt,
                                            kind="ExternalInput").ap()
        self.ccols_order = g["_ccols_order"]
        p = self.p
        self.identb = p.sbuf("c_identb", [128, 128], BF16)
        self.identf = p.sbuf("c_identf", [128, 128], F32)
        self.ones_b = p.sbuf("c_ones_b", [128, 1], BF16)
        self.ones_row = p.sbuf("c_ones_row", [1, 128], F32)
        self.ccols = p.sbuf("c_ccols", [128, len(self.ccols_order)], F32)
        ld = p.sem("c_ld")
        p.dma("sync", self.identb[:, :], self.hbm["ident_bf16"][:, :], ld)
        p.dma("sync", self.identf[:, :], self.hbm["ident_f32"][:, :], ld)
        p.dma("sync", self.ones_b[:, :], self.hbm["ones_col_bf16"][:, :], ld)
        p.dma("sync", self.ones_row[:, :], self.hbm["ones_row_f32"][:, :], ld)
        p.dma("sync", self.ccols[:, :], self.hbm["ccols"][:, :], ld)
        for e in ("tensor", "vector", "scalar", "gpsimd"):
            p.wait(e, ld)
        self.fmA = p.sbuf("ar_fmA", [128, 8, S], BF16)
        self.fmB = p.sbuf("ar_fmB", [128, 8, S], BF16)
        self.c_tm = p.sbuf("ar_ctm", [128, TT, 9], F32)
        self.pball = p.psum("pball", [128, 4096], F32)
        self.pb = [self.pball[:, i * 512:(i + 1) * 512] for i in range(8)]
        self.dump_tags = set()
        self.dumps = {}

    def maybe_dump(self, tag, ap, shape, dtype):
        if tag not in self.dump_tags:
            return
        out = self.nc.dram_tensor("dbg_" + tag, list(shape), dtype,
                                  kind="ExternalOutput").ap()
        self.p.barrier()
        sdbg = self.p.sem("dbg")
        self.p.dma("gpsimd", out, ap, sdbg)
        self.p.wait("gpsimd", sdbg)
        self.dumps[tag] = out

    def pbank(self, i):
        return self.pb[i]

    def pbank_bf(self, i):
        return self.pb[i].bitcast(BF16)

    def pspan(self, i, n):
        return self.pball[:, i * 512:(i + n) * 512]

    def cc(self, name):
        i = self.ccols_order.index(name)
        return self.ccols[:, i:i + 1]


    def ln_to_fm(self, x_in, xn_fm, uid, xn_tm_hbm=None, g_bc=None, b_bc=None,
                 banks=(0, 1)):
        p = self.p
        with p.scope():
            sld = p.sem(uid + "_ld")
            sve = p.sem(uid + "_ve")
            sse = p.sem(uid + "_se")
            spe = p.sem(uid + "_pe")
            sst = p.sem(uid + "_st")

            xt = [p.sbuf(uid + f"_xt{i}", [128, D], F32) for i in range(2)]
            xc = [p.sbuf(uid + f"_xc{i}", [128, D], F32) for i in range(2)]
            sq = p.sbuf(uid + "_sq", [128, D], F32)
            st_ = p.sbuf(uid + "_stats", [128, 4], F32)
            xnb = [p.sbuf(uid + f"_xnb{i}", [128, D], BF16) for i in range(2)]
            xn32 = p.sbuf(uid + "_xn32", [128, D], F32) if xn_tm_hbm is not None else None
            ps = [self.pbank_bf(banks[0]), self.pbank_bf(banks[1])]
            if g_bc is not None:
                gb = p.sbuf(uid + "_gb", [128, D], F32)
                bb = p.sbuf(uid + "_bb", [128, D], F32)
                p.dma("sync", gb[:, :], g_bc, sld)
                p.dma("sync", bb[:, :], b_bc, sld)

            done = {0: 0, 1: 0}
            for tt in range(TT):
                b = tt % 2
                if tt >= 2:
                    p.wait("sync", sve, done[b])
                p.dma("sync", xt[b][:, :], x_in[tt * 128:(tt + 1) * 128, :], sld)
                p.wait("vector", sld, sld.n)
                p.run("vector", lambda v, b=b: v.reduce_sum(st_[:, 1:2], xt[b][:, :], axis=AX.X))
                p.hz("vector")
                p.run("vector", lambda v: v.tensor_scalar(st_[:, 2:3], st_[:, 1:2], 1.0 / D, None, ALU.mult))
                p.hz("vector")
                p.run("vector", lambda v, b=b: v.tensor_scalar(xc[b][:, :], xt[b][:, :], st_[:, 2:3], None, ALU.subtract), inc=sve)
                p.wait("scalar", sve, sve.n)
                p.run("scalar", lambda s, b=b: s.activation(sq[:, :], xc[b][:, :], AF.Square, accum_out=st_[:, 0:1]))
                p.hz("scalar")
                p.run("scalar", lambda s: s.activation(st_[:, 3:4], st_[:, 0:1], AF.Sqrt,
                                                       bias=self.cc("ln_eps"), scale=1.0 / D), inc=sse)
                p.wait("vector", sse, sse.n)
                p.run("vector", lambda v: v.reciprocal(st_[:, 3:4], st_[:, 3:4]))
                p.hz("vector")
                if g_bc is None:
                    p.run("vector", lambda v, b=b: v.tensor_scalar(
                        xnb[b][:, :], xc[b][:, :], st_[:, 3:4], None, ALU.mult), inc=sve)
                    if xn32 is not None:
                        p.wait("vector", sst, sst.n)
                        p.run("vector", lambda v, b=b: v.tensor_scalar(
                            xn32[:, :], xc[b][:, :], st_[:, 3:4], None, ALU.mult), inc=sve)
                else:
                    p.run("vector", lambda v, b=b: v.scalar_tensor_tensor(
                        sq[:, :], xc[b][:, :], st_[:, 3:4], gb[:, :], ALU.mult, ALU.mult))
                    p.run("vector", lambda v, b=b: v.tensor_add(xnb[b][:, :], sq[:, :], bb[:, :]), inc=sve)
                    if xn32 is not None:
                        p.wait("vector", sst, sst.n)
                        p.run("vector", lambda v: v.tensor_add(xn32[:, :], sq[:, :], bb[:, :]), inc=sve)
                if xn32 is not None:
                    p.wait("sync", sve, sve.n)
                    p.dma("sync", xn_tm_hbm[tt * 128:(tt + 1) * 128, :], xn32[:, :], sst)
                p.wait("tensor", sve, sve.n)
                if tt >= 2:
                    p.wait("tensor", sve, done[b])
                for c in range(8):
                    p.run("tensor", lambda t, b=b, c=c: t.transpose(
                        ps[b][:, c * 128:(c + 1) * 128], xnb[b][:, c * 128:(c + 1) * 128],
                        self.identb[:, :]), inc=(spe if c == 7 else None))
                p.wait("vector", spe, spe.n)
                p.run("vector", lambda v, b=b, tt=tt: v.tensor_copy(
                    xn_fm[:, :, tt * 128:(tt + 1) * 128],
                    ps[b][:, :].rearrange("p (c t) -> p c t", c=8)), inc=sve)
                done[b] = sve.n
                if DEBUG_SERIAL:
                    p.barrier()
        p.barrier()


    def l1attn(self, x_in, x_out, uid):
        nc, p = self.nc, self.p
        qf_s = nc.dram_tensor(uid + "_qf", [NH, S, HD], BF16, kind="Internal").ap()
        kf_s = nc.dram_tensor(uid + "_kf", [NH, S, HD], BF16, kind="Internal").ap()
        v_s = nc.dram_tensor(uid + "_v", [NH, S, HD], BF16, kind="Internal").ap()

        xn_fm = self.fmA
        self.ln_to_fm(x_in, xn_fm, uid + "ln", banks=(0, 1))
        self.maybe_dump("xnfm", xn_fm[:, :, :], [128, 8, S], BF16)

        swl = p.sem(uid + "_wl")
        sld = p.sem(uid + "_ld")
        spe = p.sem(uid + "_pe")
        sve = p.sem(uid + "_ve")
        sse = p.sem(uid + "_se")
        sst = p.sem(uid + "_st")

        flat = self.fmB[:, :, :].rearrange("p c t -> p (c t)")

        def carve32(lo_k, n_elems_f32):
            return flat[:, lo_k:lo_k + 2 * n_elems_f32].bitcast(F32)

        qkv = carve32(0, 3072)
        ta = carve32(6144, 2048)
        tb = carve32(10240, 2048)
        tc = carve32(14336, 2048)
        ct = carve32(18432, 2048)
        stb = carve32(22528, 2048)
        outb = flat[:, 26624:29696]

        with p.scope():
            wq3 = p.sbuf(uid + "_wq", [128, 8, 3072], BF16)
            wq = wq3[:, :, :]
            bq_bc = p.sbuf(uid + "_bqbc", [128, 3072], F32)
            gqk = p.sbuf(uid + "_gqk", [128, 2 * D], F32)
            p.dma("gpsimd", wq,
                  self.hbm["l1a_wqkvT"].rearrange("(kc p) n -> p kc n", p=128), swl)
            p.dma("gpsimd", bq_bc[:, :], self.hbm["l1a_bqkv_bc"][:, :], swl)
            p.dma("gpsimd", gqk[:, :], self.hbm["l1a_gqk_bc"][:, :], swl)
            rst = p.sbuf(uid + "_rst", [128, 4], F32)
            sqscr = p.sbuf(uid + "_sqscr", [128, 2 * D], F32)
            pq = [self.pbank(2), self.pbank(3)]

            def slabs(apx, lo, hi):
                return apx[:, 0:2048].rearrange("p (h td) -> p h td", h=NH)[:, :, lo:hi]

            evac_done = {0: 0, 1: 0}
            st_done = 0
            p.wait("tensor", swl, swl.n)
            for tt in range(TT):
                if tt >= 1:
                    p.wait("sync", sve, ve_done)
                p.dma("sync", ct, self.hbm["rope_cos"][tt * 128:(tt + 1) * 128, :], sld)
                p.dma("sync", stb, self.hbm["rope_sgnsin"][tt * 128:(tt + 1) * 128, :], sld)
                for nb in range(6):
                    pbix = nb % 2
                    if evac_done[pbix]:
                        p.wait("tensor", sve, evac_done[pbix])
                    for kc in range(8):
                        p.run("tensor", lambda t, tt=tt, nb=nb, kc=kc, pbix=pbix: t.matmul(
                            pq[pbix][:, :], xn_fm[:, kc, tt * 128:(tt + 1) * 128],
                            wq[:, kc, nb * 512:(nb + 1) * 512],
                            start=(kc == 0), stop=(kc == 7)),
                            inc=(spe if kc == 7 else None))
                    p.wait("vector", spe, spe.n)
                    if nb == 0:
                        p.wait("vector", sst, st_done)
                    p.run("vector", lambda v, nb=nb, pbix=pbix: v.tensor_add(
                        qkv[:, nb * 512:(nb + 1) * 512], pq[pbix][:, :],
                        bq_bc[:, nb * 512:(nb + 1) * 512]), inc=sve)
                    evac_done[pbix] = sve.n
                p.wait("scalar", sve, sve.n)
                p.run("scalar", lambda s: s.activation(
                    slabs(sqscr, 0, HD), slabs(qkv, 0, HD), AF.Square, accum_out=rst[:, 0:1]))
                p.run("scalar", lambda s: s.activation(
                    slabs(sqscr, HD, 128), slabs(qkv, HD, 128), AF.Square, accum_out=rst[:, 2:3]))
                p.hz("scalar")
                p.run("scalar", lambda s: s.activation(rst[:, 1:2], rst[:, 0:1], AF.Sqrt,
                                                       bias=self.cc("rms_eps"), scale=1.0 / D))
                p.run("scalar", lambda s: s.activation(rst[:, 3:4], rst[:, 2:3], AF.Sqrt,
                                                       bias=self.cc("rms_eps"), scale=1.0 / D), inc=sse)
                p.wait("vector", sse, sse.n)
                p.wait("vector", sld, sld.n)
                p.run("vector", lambda v: v.reciprocal(rst[:, 1:2], rst[:, 1:2]))
                p.run("vector", lambda v: v.reciprocal(rst[:, 3:4], rst[:, 3:4]))
                p.hz("vector")
                p.run("vector", lambda v: v.scalar_tensor_tensor(
                    slabs(ta, 0, HD), slabs(qkv, 0, HD), rst[:, 1:2],
                    slabs(gqk, 0, HD), ALU.mult, ALU.mult))
                p.run("vector", lambda v: v.scalar_tensor_tensor(
                    slabs(ta, HD, 128), slabs(qkv, HD, 128), rst[:, 3:4],
                    slabs(gqk, HD, 128), ALU.mult, ALU.mult))
                p.hz("vector")
                p.run("vector", lambda v: v.tensor_mul(tc[:, :], ta[:, :], ct[:, :]))
                p.run("vector", lambda v: v.tensor_copy(tb[:, 0:2048:2], ta[:, 1:2048:2]))
                p.run("vector", lambda v: v.tensor_copy(tb[:, 1:2048:2], ta[:, 0:2048:2]))
                p.run("vector", lambda v: v.tensor_mul(ta[:, :], tb[:, :], stb[:, :]))
                p.run("vector", lambda v: v.tensor_add(tb[:, :], tc[:, :], ta[:, :]))
                p.wait("vector", sst, st_done)
                p.run("vector", lambda v: v.tensor_scalar(
                    outb[:, 0:2048], tb[:, :], 0.0, None, ALU.max))
                p.run("vector", lambda v: v.tensor_copy(
                    outb[:, 2048:3072], qkv[:, 2048:3072]), inc=sve)
                ve_done = sve.n
                p.wait("scalar", sve, sve.n)
                for half in range(2):
                    r0 = half * 64
                    src_qk = outb[r0:r0 + 64, 0:2048].rearrange("p (h td) -> p h td", h=NH)
                    src_v = outb[r0:r0 + 64, 2048:3072].rearrange("p (vp two d) -> p vp two d", vp=8, two=2)
                    t0 = tt * 128 + r0
                    p.dma("scalar", qf_s[:, t0:t0 + 64, :].rearrange("h p d -> p h d"),
                          src_qk[:, :, 0:HD], sst)
                    p.dma("scalar", kf_s[:, t0:t0 + 64, :].rearrange("h p d -> p h d"),
                          src_qk[:, :, HD:128], sst)
                    p.dma("scalar", v_s[:, t0:t0 + 64, :].rearrange(
                        "(vp two) p d -> p vp two d", two=2), src_v, sst)
                st_done = sst.n
        p.barrier()
        self.maybe_dump("qfs", qf_s[:, :, :], [NH, S, HD], BF16)
        self.maybe_dump("kfs", kf_s[:, :, :], [NH, S, HD], BF16)
        self.maybe_dump("vs", v_s[:, :, :], [NH, S, HD], BF16)

        attn_fm = self.fmB
        with p.scope():
            kh = [p.sbuf(uid + f"_kh{i}", [128, TT, HD], BF16) for i in range(2)]
            vh = [p.sbuf(uid + f"_vh{i}", [128, TT, HD], BF16) for i in range(2)]
            qh = [p.sbuf(uid + f"_qh{i}", [128, TT, HD], BF16) for i in range(2)]
            vkt = p.sbuf(uid + "_vkt", [64, 65], BF16)
            qfm = p.sbuf(uid + "_qfm", [64, TT, 128], BF16)
            atm = p.sbuf(uid + "_atm", [128, TT, HD], BF16)
            rcp = p.sbuf(uid + "_rcp", [128, 4], F32)
            pvk = self.pbank(4)
            pqt = self.pbank_bf(5)
            pres = [self.pbank(6), self.pbank(7)]
            pat = self.pbank_bf(1)

            head_ve = {0: 0, 1: 0}
            head_pe = {0: 0, 1: 0}
            for h in range(NH):
                hb = h % 2
                if h >= 2:
                    p.wait("gpsimd", sve, head_ve[hb])
                    p.wait("gpsimd", spe, head_pe[hb])
                p.dma("gpsimd", kh[hb][:, :, :], kf_s[h].rearrange("(t p) d -> p t d", p=128), swl)
                p.dma("gpsimd", vh[hb][:, :, :], v_s[h].rearrange("(t p) d -> p t d", p=128), swl)
                p.dma("gpsimd", qh[hb][:, :, :], qf_s[h].rearrange("(t p) d -> p t d", p=128), swl)
                p.wait("tensor", swl, swl.n)
                for lt in range(TT):
                    p.run("tensor", lambda t, hb=hb, lt=lt: t.matmul(
                        pvk[0:64, 0:HD], kh[hb][:, lt, :], vh[hb][:, lt, :],
                        start=(lt == 0), stop=(lt == TT - 1)))
                    p.run("tensor", lambda t, hb=hb, lt=lt: t.matmul(
                        pvk[0:64, HD:65], kh[hb][:, lt, :], self.ones_b[:, :],
                        start=False, stop=(lt == TT - 1)),
                        inc=(spe if lt == TT - 1 else None))
                p.wait("scalar", spe, spe.n)
                p.run("scalar", lambda s: s.activation(vkt[:, :], pvk[0:64, 0:65], AF.Copy), inc=sse)
                for r in range(4):
                    for j in range(8):
                        lt = r * 8 + j
                        p.run("tensor", lambda t, hb=hb, lt=lt, j=j: t.transpose(
                            pqt[0:64, j * 128:(j + 1) * 128], qh[hb][:, lt, :],
                            self.identb[:, :]), inc=(spe if j == 7 else None))
                    p.wait("vector", spe, spe.n)
                    p.run("vector", lambda v, r=r: v.tensor_copy(
                        qfm[:, r * 8:(r + 1) * 8, :].rearrange("p t n -> p (t n)"),
                        pqt[0:64, :]), inc=sve)
                    p.wait("tensor", sve, sve.n)
                p.wait("tensor", sse, sse.n)
                res_done = {0: 0, 1: 0}
                for ltg in range(TT // 4):
                    pbix = ltg % 2
                    if res_done[pbix]:
                        p.wait("tensor", sve, res_done[pbix])
                    for i in range(4):
                        p.run("tensor", lambda t, ltg=ltg, i=i, pbix=pbix: t.matmul(
                            pres[pbix][:, i * 128:i * 128 + 65], qfm[:, ltg * 4 + i, :],
                            vkt[:, :], start=True, stop=True),
                            inc=(spe if i == 3 else None))
                    p.wait("vector", spe, spe.n)
                    p.run("vector", lambda v, pbix=pbix: v.tensor_scalar(
                        rcp[:, 0:4],
                        pres[pbix].rearrange("p (a b) -> p a b", b=128)[:, :, HD:HD + 1]
                        .rearrange("p a b -> p (a b)"),
                        self.cc("eps_lin"), None, ALU.add))
                    p.hz("vector")
                    p.run("vector", lambda v: v.reciprocal(rcp[:, 0:4], rcp[:, 0:4]))
                    p.hz("vector")
                    for i in range(4):
                        p.run("vector", lambda v, ltg=ltg, i=i, pbix=pbix: v.tensor_scalar(
                            atm[:, ltg * 4 + i, :], pres[pbix][:, i * 128:i * 128 + HD],
                            rcp[:, i:i + 1], None, ALU.mult), inc=(sve if i == 3 else None))
                    res_done[pbix] = sve.n
                p.wait("tensor", sve, sve.n)
                for r in range(4):
                    for j in range(8):
                        lt = r * 8 + j
                        p.run("tensor", lambda t, lt=lt, j=j: t.transpose(
                            pat[0:64, j * 128:(j + 1) * 128], atm[:, lt, :],
                            self.identb[:, :]), inc=(spe if j == 7 else None))
                    p.wait("vector", spe, spe.n)
                    p.run("vector", lambda v, h=h, r=r: v.tensor_copy(
                        attn_fm[(h % 2) * 64:(h % 2) * 64 + 64, h // 2,
                                r * 1024:(r + 1) * 1024], pat[0:64, :]), inc=sve)
                    p.wait("tensor", sve, sve.n)
                head_ve[hb] = sve.n
                head_pe[hb] = spe.n
        p.barrier()
        self.maybe_dump("attnfm", attn_fm[:, :, :], [128, 8, S], BF16)

        with p.scope():
            wo3 = p.sbuf(uid + "_wo", [128, 8, 1024], BF16)
            wo = wo3[:, :, :]
            p.dma("gpsimd", wo, self.hbm["l1a_woutT"].rearrange("(c p) n -> p c n", p=128), swl)
            bout = p.sbuf(uid + "_bout", [128, D], F32)
            gout = p.sbuf(uid + "_gout", [128, D], F32)
            p.dma("gpsimd", bout[:, :], self.hbm["l1a_bout_bc"][:, :], swl)
            p.dma("gpsimd", gout[:, :], self.hbm["l1a_gout_bc"][:, :], swl)

            xr = [p.sbuf(uid + f"_xr{i}", [128, D], F32) for i in range(2)]
            o32 = [p.sbuf(uid + f"_o32{i}", [128, D], F32) for i in range(2)]
            sq2 = p.sbuf(uid + "_sq2", [128, D], F32)
            st2 = p.sbuf(uid + "_st2", [128, 2], F32)
            po = [self.pbank(2), self.pbank(3)]
            xo = [p.sbuf(uid + f"_xo{i}", [128, D], F32) for i in range(2)]

            p.wait("tensor", swl, swl.n)
            e_done = {0: 0, 1: 0}
            po_done = {0: 0, 1: 0}
            for tt in range(TT):
                b = tt % 2
                if tt >= 2:
                    p.wait("sync", sst, e_done[b])
                p.dma("sync", xr[b][:, :], x_in[tt * 128:(tt + 1) * 128, :], sld)
                for nb in range(2):
                    if po_done[nb]:
                        p.wait("tensor", sve, po_done[nb])
                    for c in range(8):
                        p.run("tensor", lambda t, tt=tt, nb=nb, c=c: t.matmul(
                            po[nb][:, :], attn_fm[:, c, tt * 128:(tt + 1) * 128],
                            wo[:, c, nb * 512:(nb + 1) * 512],
                            start=(c == 0), stop=(c == 7)), inc=(spe if c == 7 else None))
                    p.wait("vector", spe, spe.n)
                    p.run("vector", lambda v, b=b, nb=nb: v.tensor_add(
                        o32[b][:, nb * 512:(nb + 1) * 512], po[nb][:, :],
                        bout[:, nb * 512:(nb + 1) * 512]), inc=sve)
                    po_done[nb] = sve.n
                p.wait("scalar", sve, sve.n)
                p.run("scalar", lambda s, b=b: s.activation(sq2[:, :], o32[b][:, :], AF.Square,
                                                            accum_out=st2[:, 0:1]))
                p.hz("scalar")
                p.run("scalar", lambda s: s.activation(st2[:, 1:2], st2[:, 0:1], AF.Sqrt,
                                                       bias=self.cc("rms_eps"), scale=1.0 / D), inc=sse)
                p.wait("vector", sse, sse.n)
                p.wait("vector", sld, sld.n)
                p.run("vector", lambda v: v.reciprocal(st2[:, 1:2], st2[:, 1:2]))
                p.hz("vector")
                p.run("vector", lambda v, b=b: v.scalar_tensor_tensor(
                    sq2[:, :], o32[b][:, :], st2[:, 1:2], gout[:, :], ALU.mult, ALU.mult))
                p.run("vector", lambda v, b=b: v.tensor_add(xo[b][:, :], xr[b][:, :], sq2[:, :]), inc=sve)
                p.wait("sync", sve, sve.n)
                p.dma("sync", x_out[tt * 128:(tt + 1) * 128, :], xo[b][:, :], sst)
                e_done[b] = sst.n
        p.barrier()
        return sst


    def l0ffn(self, x_in, x_out, uid):
        nc, p = self.nc, self.p
        NE = 9
        DFF = 2048
        xn_fm = self.fmA
        self.ln_to_fm(x_in, xn_fm, uid + "ln", banks=(0, 1))

        swl = p.sem(uid + "_wl")
        sld = p.sem(uid + "_ld")
        spe = p.sem(uid + "_pe")
        sve = p.sem(uid + "_ve")
        sse = p.sem(uid + "_se")
        sst = p.sem(uid + "_st")

        acc = self.fmB[:, :, :].rearrange("p c t -> p (c t)").bitcast(F32)

        with p.scope():
            gw = p.sbuf(uid + "_gw", [128, 8, 8], BF16)
            beta = p.sbuf(uid + "_beta", [128, 8], F32)
            z = p.sbuf(uid + "_z", [128, 8], F32)
            cmp_ = p.sbuf(uid + "_cmp", [128, 8], F32)
            rank = p.sbuf(uid + "_rank", [128, 8], F32)
            sig = p.sbuf(uid + "_sig", [128, 8], F32)
            pl = self.pbank(6)
            p.dma("gpsimd", gw[:, :, :],
                  self.hbm["l0f_gatewT"].rearrange("(kc p) e -> p kc e", p=128), swl)
            p.dma("gpsimd", beta[:, :], self.hbm["l0f_beta_bc"][:, :], swl)
            p.run("vector", lambda v: v.memset(self.c_tm[:, :, 8:9], 1.0), inc=sve)
            p.wait("tensor", swl, swl.n)
            prev_se = 0
            for tc in range(TT):
                if prev_se:
                    p.wait("tensor", sse, prev_se)
                for kc in range(8):
                    p.run("tensor", lambda t, tc=tc, kc=kc: t.matmul(
                        pl[:, 0:8], xn_fm[:, kc, tc * 128:(tc + 1) * 128],
                        gw[:, kc, :], start=(kc == 0), stop=(kc == 7)),
                        inc=(spe if kc == 7 else None))
                p.wait("vector", spe, spe.n)
                p.run("vector", lambda v: v.tensor_add(z[:, :], pl[:, 0:8], beta[:, :]), inc=sve)
                p.hz("vector")
                for j in range(8):
                    p.run("vector", lambda v, j=j: v.tensor_scalar(
                        cmp_[:, :], z[:, :], z[:, j:j + 1], None, ALU.is_lt))
                    if j == 0:
                        p.hz("vector")
                        p.run("vector", lambda v: v.tensor_copy(rank[:, :], cmp_[:, :]))
                    else:
                        p.hz("vector")
                        p.run("vector", lambda v: v.tensor_add(rank[:, :], rank[:, :], cmp_[:, :]))
                    p.hz("vector")
                p.run("vector", lambda v: v.tensor_scalar(
                    cmp_[:, :], rank[:, :], 2.0, None, ALU.is_lt), inc=sve)
                p.wait("scalar", spe, spe.n)
                p.wait("scalar", sve, sve.n)
                p.run("scalar", lambda s: s.activation(sig[:, :], pl[:, 0:8], AF.Sigmoid), inc=sse)
                prev_se = sse.n
                p.wait("vector", sse, sse.n)
                p.run("vector", lambda v, tc=tc: v.tensor_mul(
                    self.c_tm[:, tc, 0:8], sig[:, :], cmp_[:, :]), inc=sve)
        p.barrier()

        TSN = 2
        NQ = 4
        NTT = 4
        with p.scope():
            wg = [p.sbuf(uid + f"_wg{i}", [128, 8, 512], BF16) for i in range(2)]
            wu = [p.sbuf(uid + f"_wu{i}", [128, 8, 512], BF16) for i in range(2)]
            wd = [p.sbuf(uid + f"_wd{i}", [128, 4, 1024], BF16) for i in range(2)]
            hbf = p.sbuf(uid + "_hbf", [128, 4, 512], BF16)
            sg = p.sbuf(uid + "_sg", [128, 512], F32)
            xr = [p.sbuf(uid + f"_xr{i}", [128, D], F32) for i in range(2)]
            xo = [p.sbuf(uid + f"_xo{i}", [128, D], F32) for i in range(2)]

            wdone = {0: 0, 1: 0}
            st_done = {0: 0, 1: 0}
            out_done = {0: 0, 1: 0}
            gu_done = {}
            for ts in range(TSN):
                p.wait("vector", sst, sst.n)
                p.run("vector", lambda v: v.memset(acc[:, :], 0.0), inc=sve)
                p.wait("tensor", sve, sve.n)
                for e in range(NE):
                    for q in range(NQ):
                        it = (ts * NE + e) * NQ + q
                        wb = it % 2
                        if it >= 2:
                            p.wait("gpsimd", spe, wdone[wb])
                        p.dma("gpsimd", wg[wb][:, :, :],
                              self.hbm["l0f_wg"][e][:, q * 512:(q + 1) * 512]
                              .rearrange("(kc p) n -> p kc n", p=128), swl)
                        p.dma("gpsimd", wu[wb][:, :, :],
                              self.hbm["l0f_wu"][e][:, q * 512:(q + 1) * 512]
                              .rearrange("(kc p) n -> p kc n", p=128), swl)
                        p.dma("gpsimd", wd[wb][:, :, :],
                              self.hbm["l0f_wd"][e][q * 512:(q + 1) * 512, :]
                              .rearrange("(dc p) n -> p dc n", p=128), swl)
                        p.wait("tensor", swl, swl.n)
                        for tt in range(NTT):
                            t0 = ts * 2048 + tt * 512
                            for dc in range(4):
                                bA = self.pbank((dc % 2) * 2)
                                bB = self.pbank((dc % 2) * 2 + 1)
                                key = dc % 2
                                if gu_done.get(key):
                                    p.wait("tensor", sve, gu_done[key][0])
                                    p.wait("tensor", sse, gu_done[key][1])
                                for kc in range(8):
                                    p.run("tensor", lambda t, wb=wb, dc=dc, kc=kc, t0=t0, bA=bA: t.matmul(
                                        bA[:, :], wg[wb][:, kc, dc * 128:(dc + 1) * 128],
                                        xn_fm[:, kc, t0:t0 + 512],
                                        start=(kc == 0), stop=(kc == 7)),
                                        inc=(spe if kc == 7 else None))
                                for kc in range(8):
                                    p.run("tensor", lambda t, wb=wb, dc=dc, kc=kc, t0=t0, bB=bB: t.matmul(
                                        bB[:, :], wu[wb][:, kc, dc * 128:(dc + 1) * 128],
                                        xn_fm[:, kc, t0:t0 + 512],
                                        start=(kc == 0), stop=(kc == 7)),
                                        inc=(spe if kc == 7 else None))
                                p.wait("scalar", spe, spe.n - 1)
                                p.wait("scalar", sve, sve.n)
                                p.run("scalar", lambda s, bA=bA: s.activation(
                                    sg[:, :], bA[:, :], AF.Silu), inc=sse)
                                p.wait("vector", sse, sse.n)
                                p.wait("vector", spe, spe.n)
                                p.run("vector", lambda v, dc=dc, bB=bB: v.tensor_mul(
                                    hbf[:, dc, :], sg[:, :], bB[:, :]), inc=sve)
                                gu_done[dc % 2] = (sve.n, sse.n)
                            p.wait("tensor", sve, sve.n)
                            for tck in range(4):
                                ob = 4 + (tck % 2) * 2
                                pO = [self.pbank(ob), self.pbank(ob + 1)]
                                if out_done[tck % 2]:
                                    p.wait("tensor", sve, out_done[tck % 2])
                                for dc in range(4):
                                    for nb in range(2):
                                        p.run("tensor", lambda t, wb=wb, dc=dc, nb=nb, tck=tck, pO=pO: t.matmul(
                                            pO[nb][:, :],
                                            hbf[:, dc, tck * 128:(tck + 1) * 128],
                                            wd[wb][:, dc, nb * 512:(nb + 1) * 512],
                                            start=(dc == 0), stop=(dc == 3)),
                                            inc=(spe if (dc == 3 and nb == 1) else None))
                                p.wait("vector", spe, spe.n)
                                gtc = (t0 // 128) + tck
                                for nb in range(2):
                                    p.run("vector", lambda v, e=e, nb=nb, gtc=gtc, pO=pO: v.scalar_tensor_tensor(
                                        acc[:, (gtc % 16) * 1024 + nb * 512:(gtc % 16) * 1024 + (nb + 1) * 512],
                                        pO[nb][:, :], self.c_tm[:, gtc, e:e + 1],
                                        acc[:, (gtc % 16) * 1024 + nb * 512:(gtc % 16) * 1024 + (nb + 1) * 512],
                                        ALU.mult, ALU.add), inc=sve)
                                out_done[tck % 2] = sve.n
                        wdone[wb] = spe.n
                for tc in range(16):
                    b = tc % 2
                    gtc = ts * 16 + tc
                    if tc >= 2:
                        p.wait("sync", sst, st_done[b])
                    p.dma("sync", xr[b][:, :], x_in[gtc * 128:(gtc + 1) * 128, :], sld)
                    p.wait("vector", sld, sld.n)
                    p.wait("vector", sst, st_done[b])
                    p.run("vector", lambda v, b=b, tc=tc: v.scalar_tensor_tensor(
                        xo[b][:, :], acc[:, tc * 1024:(tc + 1) * 1024], 0.5,
                        xr[b][:, :], ALU.mult, ALU.add), inc=sve)
                    p.wait("sync", sve, sve.n)
                    p.dma("sync", x_out[gtc * 128:(gtc + 1) * 128, :], xo[b][:, :], sst)
                    st_done[b] = sst.n
        p.barrier()
        return sst

    def l1ffn(self, x_in, x_out, uid):
        nc, p = self.nc, self.p
        xin_s = nc.dram_tensor(uid + "_xin", [S, D], F32, kind="Internal").ap()
        xmid_s = nc.dram_tensor(uid + "_xmid", [S, D], F32, kind="Internal").ap()

        swl = p.sem(uid + "_wl")
        sld = p.sem(uid + "_ld")
        spe = p.sem(uid + "_pe")
        sve = p.sem(uid + "_ve")
        sse = p.sem(uid + "_se")
        sst = p.sem(uid + "_st")

        self.ln_to_fm(x_in, self.fmA, uid + "lnA", xn_tm_hbm=xin_s)
        self.ln_to_fm(xin_s, self.fmA, uid + "lnB",
                      g_bc=self.hbm["l1f_ln1g_bc"][:, :], b_bc=self.hbm["l1f_ln1b_bc"][:, :])
        h_fm = self.fmA
        attn_fm = self.fmB
        self.maybe_dump("hfm", self.fmA[:, :, :], [128, 8, S], BF16)

        with p.scope():
            wslab = [p.sbuf(uid + f"_ws{i}", [128, 8, 128], BF16) for i in range(2)]
            bq = p.sbuf(uid + "_bq", [128, 24], F32)
            qt_ = [p.sbuf(uid + f"_qt{i}", [64, 4096], BF16) for i in range(2)]
            kt_ = [p.sbuf(uid + f"_kt{i}", [64, 4096], BF16) for i in range(2)]
            vh_ = [p.sbuf(uid + f"_vh{i}", [64, 4096], BF16) for i in range(2)]
            vt = [p.sbuf(uid + f"_vt{i}", [128, TT, 65], BF16) for i in range(2)]
            esc = [p.sbuf(uid + f"_esc{i}", [128, 2048], BF16) for i in range(2)]
            rcq = p.sbuf(uid + "_rcq", [1, 512], F32)
            rb = p.sbuf(uid + "_rb", [64, 512], F32)
            p.dma("gpsimd", bq[:, :], self.hbm["l1f_bqkv"][:, :], swl)
            pslab = [self.pbank(0), self.pbank(1)]
            pat = self.pbank(6)
            pbc = self.pbank(7)
            ptr = self.pbank_bf(0)
            wq_h = self.hbm["l1f_wqkv"]

            def emit_slab(slab_idx, wb, dst_top, dst_bot, evac_done_prev):
                p.wait("gpsimd", spe, evac_done_prev)
                p.dma("gpsimd", wslab[wb][:, :, :],
                      wq_h[:, slab_idx * 128:(slab_idx + 1) * 128]
                      .rearrange("(kc p) m -> p kc m", p=128), swl)
                p.wait("tensor", swl, swl.n)
                for r in range(8):
                    pb = pslab[r % 2]
                    if r >= 2:
                        p.wait("tensor", sse, sse.n - 2)
                    for kc in range(8):
                        p.run("tensor", lambda t, wb=wb, r=r, kc=kc, pb=pb: t.matmul(
                            pb[:, :], wslab[wb][:, kc, :],
                            h_fm[:, kc, r * 512:(r + 1) * 512],
                            start=(kc == 0), stop=(kc == 7)),
                            inc=(spe if kc == 7 else None))
                    p.wait("scalar", spe, spe.n)
                    p.run("scalar", lambda s, r=r, pb=pb, dst_top=dst_top, slab_idx=slab_idx:
                          s.activation(dst_top[:, r * 512:(r + 1) * 512], pb[0:64, :],
                                       AF.Identity, bias=bq[0:64, slab_idx:slab_idx + 1]), inc=sse)
                    p.run("scalar", lambda s, r=r, pb=pb, dst_bot=dst_bot, slab_idx=slab_idx:
                          s.activation(dst_bot[:, r * 512:(r + 1) * 512], pb[64:128, :],
                                       AF.Identity, bias=bq[64:128, slab_idx:slab_idx + 1]), inc=sse)
                return spe.n

            qk_used = {0: 0, 1: 0}
            slab_mm_done = 0
            vt_used = 0
            for h in range(NH):
                hb = h % 2
                p.wait("scalar", spe, qk_used[hb])
                slab_mm_done = emit_slab(h, hb, qt_[hb], kt_[hb], slab_mm_done)
                if h % 2 == 0:
                    p.wait("scalar", spe, vt_used)
                    slab_mm_done = emit_slab(16 + h // 2, 1 - hb, vh_[0], vh_[1], slab_mm_done)
                    p.wait("tensor", sse, sse.n)
                    for half in range(2):
                        for r4 in range(4):
                            if not (half == 0 and r4 == 0):
                                p.wait("tensor", sve, sve.n)
                            for j in range(8):
                                ltk = r4 * 8 + j
                                p.run("tensor", lambda t, half=half, ltk=ltk, j=j: t.transpose(
                                    ptr[:, j * 64:(j + 1) * 64],
                                    vh_[half][:, ltk * 128:(ltk + 1) * 128],
                                    self.identb[0:64, 0:64]), inc=(spe if j == 7 else None))
                            p.wait("vector", spe, spe.n)
                            p.run("vector", lambda v, half=half, r4=r4: v.tensor_copy(
                                vt[half][:, r4 * 8:(r4 + 1) * 8, 0:64],
                                ptr[:, 0:512].rearrange("p (t d) -> p t d", t=8)), inc=sve)
                    p.run("vector", lambda v: v.memset(vt[0][:, :, 64:65], 1.0))
                    p.run("vector", lambda v: v.memset(vt[1][:, :, 64:65], 1.0), inc=sve)
                    p.wait("tensor", sve, sve.n)
                p.wait("tensor", sse, sse.n)
                exp_done = 0
                pv_done = {0: 0, 1: 0}
                pscore = self.pspan(2, 4)
                for qt in range(8):
                    for gg in range(8):
                        eb = gg % 2
                        if exp_done:
                            p.wait("tensor", sse, exp_done)
                        for i in range(4):
                            kc = gg * 4 + i
                            p.run("tensor", lambda t, hb=hb, kc=kc, qt=qt, i=i: t.matmul(
                                pscore[:, i * 512:(i + 1) * 512],
                                kt_[hb][:, kc * 128:(kc + 1) * 128],
                                qt_[hb][:, qt * 512:(qt + 1) * 512],
                                start=True, stop=True), inc=(spe if i == 3 else None))
                        p.wait("scalar", spe, spe.n)
                        if pv_done[eb]:
                            p.wait("scalar", spe, pv_done[eb])
                        p.run("scalar", lambda s, eb=eb: s.activation(
                            esc[eb][:, :], pscore[:, :], AF.Exp,
                            bias=self.cc("neg_shift"), scale=0.125), inc=sse)
                        exp_done = sse.n
                        p.wait("tensor", sse, exp_done)
                        for i in range(4):
                            kc = gg * 4 + i
                            p.run("tensor", lambda t, hb=hb, kc=kc, eb=eb, i=i: t.matmul(
                                pat[0:65, :], vt[hb][:, kc, :],
                                esc[eb][:, i * 512:(i + 1) * 512],
                                start=(kc == 0), stop=(kc == TT - 1)),
                                inc=(spe if i == 3 else None))
                        pv_done[eb] = spe.n
                    p.wait("vector", spe, spe.n)
                    p.run("vector", lambda v: v.tensor_copy(rcq[0:1, :], pat[64:65, :]))
                    p.hz("vector")
                    p.run("vector", lambda v: v.reciprocal(rcq[0:1, :], rcq[0:1, :]), inc=sve)
                    p.wait("tensor", sve, sve.n)
                    p.run("tensor", lambda t: t.matmul(
                        pbc[0:64, :], self.ones_row[0:1, 0:64], rcq[0:1, :],
                        start=True, stop=True), inc=spe)
                    p.wait("scalar", spe, spe.n)
                    p.run("scalar", lambda s: s.activation(rb[:, :], pbc[0:64, :], AF.Copy), inc=sse)
                    p.wait("vector", sse, sse.n)
                    p.run("vector", lambda v, h=h, qt=qt: v.tensor_mul(
                        attn_fm[(h % 2) * 64:(h % 2) * 64 + 64, h // 2, qt * 512:(qt + 1) * 512],
                        pat[0:64, :], rb[:, :]), inc=sve)
                    p.wait("tensor", sve, sve.n)
                qk_used[hb] = spe.n
                if h % 2 == 1:
                    vt_used = spe.n
        p.barrier()
        self.maybe_dump("attnfm", attn_fm[:, :, :], [128, 8, S], BF16)

        with p.scope():
            wo = p.sbuf(uid + "_wo", [128, 8, 1024], BF16)
            bo = p.sbuf(uid + "_bo", [128, D], F32)
            g2 = p.sbuf(uid + "_g2", [128, D], F32)
            b2c = p.sbuf(uid + "_b2c", [128, D], F32)
            gwl = p.sbuf(uid + "_gwl", [128, 8, 8], BF16)
            gbl = p.sbuf(uid + "_gbl", [128, 8], F32)
            xi = [p.sbuf(uid + f"_xi{i}", [128, D], F32) for i in range(2)]
            xm = [p.sbuf(uid + f"_xm{i}", [128, D], F32) for i in range(2)]
            xc2 = p.sbuf(uid + "_xc2", [128, D], F32)
            sqd = p.sbuf(uid + "_sqd", [128, D], F32)
            h2t = p.sbuf(uid + "_h2t", [128, D], BF16)
            st_ = p.sbuf(uid + "_st2", [128, 4], F32)
            zz = p.sbuf(uid + "_zz", [128, 8], F32)
            cmp_ = p.sbuf(uid + "_cmp", [128, 8], F32)
            rank = p.sbuf(uid + "_rank", [128, 8], F32)
            texp = p.sbuf(uid + "_texp", [128, 8], F32)
            tm_ = p.sbuf(uid + "_tm", [128, 8], F32)
            p.dma("gpsimd", wo[:, :, :], self.hbm["l1f_wo"].rearrange("(c p) n -> p c n", p=128), swl)
            p.dma("gpsimd", bo[:, :], self.hbm["l1f_bo_bc"][:, :], swl)
            p.dma("gpsimd", g2[:, :], self.hbm["l1f_ln2g_bc"][:, :], swl)
            p.dma("gpsimd", b2c[:, :], self.hbm["l1f_ln2b_bc"][:, :], swl)
            p.dma("gpsimd", gwl[:, :, :], self.hbm["l1f_gatew"].rearrange("(kc p) e -> p kc e", p=128), swl)
            p.dma("gpsimd", gbl[:, :], self.hbm["l1f_gateb_bc"][:, :], swl)
            po = [self.pbank(0), self.pbank(1)]
            pt2 = self.pbank_bf(2)
            plg = self.pbank(3)
            p.wait("tensor", swl, swl.n)
            st_done = {0: 0, 1: 0}
            po_done = {0: 0, 1: 0}
            lg_done = 0
            for tc in range(TT):
                b = tc % 2
                if tc >= 2:
                    p.wait("sync", sst, st_done[b])
                p.dma("sync", xi[b][:, :], xin_s[tc * 128:(tc + 1) * 128, :], sld)
                for nb in range(2):
                    if po_done[nb]:
                        p.wait("tensor", sve, po_done[nb])
                    for c in range(8):
                        p.run("tensor", lambda t, tc=tc, nb=nb, c=c: t.matmul(
                            po[nb][:, :], attn_fm[:, c, tc * 128:(tc + 1) * 128],
                            wo[:, c, nb * 512:(nb + 1) * 512],
                            start=(c == 0), stop=(c == 7)), inc=(spe if c == 7 else None))
                    p.wait("vector", spe, spe.n)
                    p.run("vector", lambda v, nb=nb: v.tensor_add(
                        sqd[:, nb * 512:(nb + 1) * 512], po[nb][:, :],
                        bo[:, nb * 512:(nb + 1) * 512]), inc=sve)
                    po_done[nb] = sve.n
                p.wait("vector", sld, sld.n)
                p.run("vector", lambda v, b=b: v.tensor_add(xm[b][:, :], sqd[:, :], xi[b][:, :]), inc=sve)
                p.wait("sync", sve, sve.n)
                p.dma("sync", xmid_s[tc * 128:(tc + 1) * 128, :], xm[b][:, :], sst)
                st_done[b] = sst.n
                p.run("vector", lambda v, b=b: v.reduce_sum(st_[:, 1:2], xm[b][:, :], axis=AX.X))
                p.hz("vector")
                p.run("vector", lambda v: v.tensor_scalar(st_[:, 2:3], st_[:, 1:2], 1.0 / D, None, ALU.mult))
                p.hz("vector")
                p.run("vector", lambda v, b=b: v.tensor_scalar(xc2[:, :], xm[b][:, :], st_[:, 2:3], None, ALU.subtract), inc=sve)
                p.wait("scalar", sve, sve.n)
                p.run("scalar", lambda s: s.activation(sqd[:, :], xc2[:, :], AF.Square, accum_out=st_[:, 0:1]))
                p.hz("scalar")
                p.run("scalar", lambda s: s.activation(st_[:, 3:4], st_[:, 0:1], AF.Sqrt,
                                                       bias=self.cc("ln_eps"), scale=1.0 / D), inc=sse)
                p.wait("vector", sse, sse.n)
                p.run("vector", lambda v: v.reciprocal(st_[:, 3:4], st_[:, 3:4]))
                p.hz("vector")
                p.run("vector", lambda v: v.scalar_tensor_tensor(
                    sqd[:, :], xc2[:, :], st_[:, 3:4], g2[:, :], ALU.mult, ALU.mult))
                p.run("vector", lambda v: v.tensor_add(h2t[:, :], sqd[:, :], b2c[:, :]), inc=sve)
                p.wait("tensor", sve, sve.n)
                for c in range(8):
                    p.run("tensor", lambda t, c=c: t.transpose(
                        pt2[:, c * 128:(c + 1) * 128], h2t[:, c * 128:(c + 1) * 128],
                        self.identb[:, :]), inc=(spe if c == 7 else None))
                p.wait("vector", spe, spe.n)
                p.run("vector", lambda v, tc=tc: v.tensor_copy(
                    self.fmA[:, :, tc * 128:(tc + 1) * 128],
                    pt2[:, :].rearrange("p (c t) -> p c t", c=8)), inc=sve)
                if lg_done:
                    p.wait("tensor", sse, lg_done)
                p.wait("tensor", sve, sve.n)
                for kc in range(8):
                    p.run("tensor", lambda t, tc=tc, kc=kc: t.matmul(
                        plg[:, 0:8], self.fmA[:, kc, tc * 128:(tc + 1) * 128], gwl[:, kc, :],
                        start=(kc == 0), stop=(kc == 7)), inc=(spe if kc == 7 else None))
                p.wait("vector", spe, spe.n)
                p.run("vector", lambda v: v.tensor_add(zz[:, :], plg[:, 0:8], gbl[:, :]))
                p.hz("vector")
                p.run("vector", lambda v: v.reduce_max(st_[:, 0:1], zz[:, :], axis=AX.X, negate=True), inc=sve)
                p.hz("vector")
                for j in range(8):
                    p.run("vector", lambda v, j=j: v.tensor_scalar(
                        cmp_[:, :], zz[:, :], zz[:, j:j + 1], None, ALU.is_lt))
                    p.hz("vector")
                    if j == 0:
                        p.run("vector", lambda v: v.tensor_copy(rank[:, :], cmp_[:, :]))
                    else:
                        p.run("vector", lambda v: v.tensor_add(rank[:, :], rank[:, :], cmp_[:, :]))
                    p.hz("vector")
                p.run("vector", lambda v: v.tensor_scalar(
                    rank[:, :], rank[:, :], 2.0, None, ALU.is_lt), inc=sve)
                p.wait("scalar", sve, sve.n)
                p.run("scalar", lambda s: s.activation(texp[:, :], zz[:, :], AF.Exp,
                                                       bias=st_[:, 0:1]), inc=sse)
                p.wait("vector", sse, sse.n)
                p.run("vector", lambda v: v.tensor_mul(tm_[:, :], texp[:, :], rank[:, :]))
                p.hz("vector")
                p.run("vector", lambda v: v.reduce_sum(st_[:, 1:2], tm_[:, :], axis=AX.X))
                p.hz("vector")
                p.run("vector", lambda v: v.reciprocal(st_[:, 1:2], st_[:, 1:2]))
                p.hz("vector")
                p.run("vector", lambda v, tc=tc: v.tensor_scalar(
                    self.c_tm[:, tc, 0:8], tm_[:, :], st_[:, 1:2], None, ALU.mult), inc=sve)
                lg_done = sve.n
        p.barrier()
        self.maybe_dump("xmid", xmid_s, [S, D], F32)
        self.maybe_dump("h2fm", self.fmA[:, :, :], [128, 8, S], BF16)
        self.maybe_dump("ctm", self.c_tm[:, :, :], [128, TT, 9], F32)

        ewt = p.sbuf(uid + "_ewt", [8, TT, 128], BF16)
        with p.scope():
            pew = self.pbank(5)
            for tc in range(TT):
                if tc:
                    p.wait("tensor", sse, sse.n)
                p.run("tensor", lambda t, tc=tc: t.transpose(
                    pew[0:8, 0:128], self.c_tm[:, tc, 0:8], self.identf[:, :]), inc=spe)
                p.wait("scalar", spe, spe.n)
                p.run("scalar", lambda s, tc=tc: s.activation(
                    ewt[:, tc, :], pew[0:8, 0:128], AF.Copy), inc=sse)
        p.barrier()

        NE2 = 8
        NQ = 4
        acc = self.fmB[:, :, :].rearrange("p c t -> p (c t)").bitcast(F32)
        with p.scope():
            w1q = [p.sbuf(uid + f"_w1q{i}", [128, 8, 1024], BF16) for i in range(2)]
            w2q_s = p.sbuf(uid + "_w2q", [128, 8, 1024], BF16)
            w2q = [w2q_s, w2q_s]
            b1s = p.sbuf(uid + "_b1s", [128, 32], F32)
            b2s = p.sbuf(uid + "_b2s", [8, 1024], BF16)
            hid = p.sbuf(uid + "_hid", [128, 8, 512], BF16)
            _w1f0 = w1q[0][:, :, :].rearrange("p a b -> p (a b)")
            _w1f1 = w1q[1][:, :, :].rearrange("p a b -> p (a b)")
            xr = [_w1f0[:, 0:2048].bitcast(F32)] * 2
            xml = [_w1f0[:, 2048:4096].bitcast(F32)] * 2
            t1 = _w1f1[:, 0:2048].bitcast(F32)
            xo = [_w1f1[:, 2048:4096].bitcast(F32)] * 2
            p.dma("gpsimd", b2s[:, :], self.hbm["l1f_b2"][:, :], swl)
            wdone = {0: 0, 1: 0}
            w2_done = 0
            b1_done = 0
            relu_done = {0: 0, 1: 0, 2: 0, 3: 0}
            moe_out_done = {0: 0, 1: 0}
            st_done = {0: 0, 1: 0}
            for ts in range(2):
                p.wait("vector", sst, sst.n)
                p.run("vector", lambda v: v.memset(acc[:, :], 0.0), inc=sve)
                p.wait("gpsimd", sst, sst.n)
                p.wait("gpsimd", sve, sve.n)
                for e in range(NE2):
                    p.wait("gpsimd", sse, b1_done)
                    p.dma("gpsimd", b1s[:, :], self.hbm["l1f_b1T"][e], swl)
                    for qr in range(NQ):
                        it = (ts * NE2 + e) * NQ + qr
                        wb = it % 2
                        if it >= 2:
                            p.wait("gpsimd", spe, wdone[wb])
                        p.dma("gpsimd", w1q[wb][:, :, :],
                              self.hbm["l1f_W1"][e][:, qr * 1024:(qr + 1) * 1024]
                              .rearrange("(kc p) n -> p kc n", p=128), swl)
                        if it >= 1:
                            p.wait("gpsimd", spe, w2_done)
                        p.dma("gpsimd", w2q_s[:, :, :],
                              self.hbm["l1f_W2"][e][qr * 1024:(qr + 1) * 1024, :]
                              .rearrange("(dc p) n -> p dc n", p=128), swl)
                        p.wait("tensor", swl, swl.n)
                        for tt in range(4):
                            t0 = tt * 512
                            for dc in range(8):
                                hb_ = self.pbank(dc % 4)
                                if relu_done[dc % 4]:
                                    p.wait("tensor", sse, relu_done[dc % 4])
                                for kc in range(8):
                                    p.run("tensor", lambda t, wb=wb, dc=dc, kc=kc, ts=ts, t0=t0, hb_=hb_: t.matmul(
                                        hb_[:, :], w1q[wb][:, kc, dc * 128:(dc + 1) * 128],
                                        self.fmA[:, kc, ts * 2048 + t0:ts * 2048 + t0 + 512],
                                        start=(kc == 0), stop=(kc == 7)),
                                        inc=(spe if kc == 7 else None))
                                p.wait("scalar", spe, spe.n)
                                p.run("scalar", lambda s, dc=dc, qr=qr, hb_=hb_: s.activation(
                                    hid[:, dc, :], hb_[:, :], AF.Relu,
                                    bias=b1s[:, qr * 8 + dc:qr * 8 + dc + 1]), inc=sse)
                                relu_done[dc % 4] = sse.n
                            b1_done = sse.n
                            p.wait("tensor", sse, sse.n)
                            for tck in range(4):
                                ob = 4 + (tck % 2) * 2
                                pO = [self.pbank(ob), self.pbank(ob + 1)]
                                if moe_out_done[tck % 2]:
                                    p.wait("tensor", sve, moe_out_done[tck % 2])
                                for dc in range(8):
                                    for nb in range(2):
                                        p.run("tensor", lambda t, wb=wb, dc=dc, nb=nb, tck=tck, t0=t0, pO=pO: t.matmul(
                                            pO[nb][:, :],
                                            hid[:, dc, tck * 128:(tck + 1) * 128],
                                            w2q[wb][:, dc, nb * 512:(nb + 1) * 512],
                                            start=(dc == 0), stop=(dc == 7)),
                                            inc=(spe if (dc == 7 and nb == 1) else None))
                                p.wait("vector", spe, spe.n)
                                gtc = ts * 16 + (t0 // 128) + tck
                                lc = (t0 // 128) + tck
                                for nb in range(2):
                                    p.run("vector", lambda v, e=e, nb=nb, gtc=gtc, lc=lc, pO=pO: v.scalar_tensor_tensor(
                                        acc[:, lc * 1024 + nb * 512:lc * 1024 + (nb + 1) * 512],
                                        pO[nb][:, :], self.c_tm[:, gtc, e:e + 1],
                                        acc[:, lc * 1024 + nb * 512:lc * 1024 + (nb + 1) * 512],
                                        ALU.mult, ALU.add), inc=sve)
                                moe_out_done[tck % 2] = sve.n
                        wdone[wb] = spe.n
                        w2_done = spe.n
                p.wait("tensor", swl, swl.n)
                p.wait("sync", spe, spe.n)
                p.wait("vector", spe, spe.n)
                if ts == 0:
                    self.maybe_dump("acc0", acc[:, :], [128, 16384], F32)
                for lc in range(16):
                    b = lc % 2
                    gtc = ts * 16 + lc
                    pO = [self.pbank(4 + b * 2), self.pbank(5 + b * 2)]
                    if lc >= 1:
                        p.wait("sync", sst, sst.n)
                    p.dma("sync", xr[b][:, :], x_in[gtc * 128:(gtc + 1) * 128, :], sld)
                    p.dma("sync", xml[b][:, :], xmid_s[gtc * 128:(gtc + 1) * 128, :], sld)
                    p.wait("tensor", sve, sve.n)
                    for nb in range(2):
                        p.run("tensor", lambda t, gtc=gtc, nb=nb, pO=pO: t.matmul(
                            pO[nb][:, :], ewt[:, gtc, :], b2s[:, nb * 512:(nb + 1) * 512],
                            start=True, stop=True), inc=(spe if nb == 1 else None))
                    p.wait("vector", spe, spe.n)
                    p.wait("vector", sld, sld.n)
                    p.wait("vector", sst, st_done[b])
                    for nb in range(2):
                        p.run("vector", lambda v, lc=lc, nb=nb, pO=pO: v.tensor_add(
                            acc[:, lc * 1024 + nb * 512:lc * 1024 + (nb + 1) * 512],
                            acc[:, lc * 1024 + nb * 512:lc * 1024 + (nb + 1) * 512],
                            pO[nb][:, :]))
                    p.run("vector", lambda v, b=b, lc=lc: v.scalar_tensor_tensor(
                        t1[:, :], acc[:, lc * 1024:(lc + 1) * 1024], 0.5,
                        xr[b][:, :], ALU.mult, ALU.add))
                    p.run("vector", lambda v, b=b: v.scalar_tensor_tensor(
                        xo[b][:, :], xml[b][:, :], 0.5, t1[:, :], ALU.mult, ALU.add), inc=sve)
                    p.wait("sync", sve, sve.n)
                    p.dma("sync", x_out[gtc * 128:(gtc + 1) * 128, :], xo[b][:, :], sst)
                    st_done[b] = sst.n
        p.barrier()
        return sst

    def l0attn(self, x_in, x_out, uid):
        nc, p = self.nc, self.p
        srel_s = nc.dram_tensor(uid + "_srel", [NH, 9344], F32, kind="Internal").ap()
        mat3 = nc.dram_tensor(uid + "_mat3", [128, NH, 8, 8192], BF16, kind="Internal").ap()

        swl = p.sem(uid + "_wl")
        sld = p.sem(uid + "_ld")
        spe = p.sem(uid + "_pe")
        sve = p.sem(uid + "_ve")
        sse = p.sem(uid + "_se")
        sst = p.sem(uid + "_st")
        smt = p.sem(uid + "_mt")

        with p.scope():
            relg = p.sbuf(uid + "_relg", [128, 1024], F32)
            amp = p.sbuf(uid + "_amp", [128, 21], F32)
            shp = p.sbuf(uid + "_shp", [128, 21], F32)
            off = p.sbuf(uid + "_off", [128, 21], F32)
            nsh = p.sbuf(uid + "_nsh", [128, 21], F32)
            dd = p.sbuf(uid + "_dd", [128, 1024], F32)
            d2 = p.sbuf(uid + "_d2", [128, 1024], F32)
            gk = p.sbuf(uid + "_gk", [128, 1024], F32)
            accs = p.sbuf(uid + "_accs", [128, 1024], F32)
            p.dma("sync", relg[:, :], self.hbm["l0a_relgrid"][:, :], sld)
            p.dma("sync", amp[:, :], self.hbm["l0a_amp"][:, :], sld)
            p.dma("sync", shp[:, :], self.hbm["l0a_sharp"][:, :], sld)
            p.dma("sync", off[:, :], self.hbm["l0a_off"][:, :], sld)
            p.wait("scalar", sld, sld.n)
            p.run("scalar", lambda s: s.activation(nsh[:, :], shp[:, :], AF.Abs), inc=sse)
            p.wait("vector", sse, sse.n)
            p.run("vector", lambda v: v.tensor_scalar(nsh[:, :], nsh[:, :], -1.0, None, ALU.mult))
            p.run("vector", lambda v: v.memset(accs[:, :], 0.0), inc=sve)
            p.wait("vector", sld, sld.n)
            for k in range(21):
                p.run("vector", lambda v, k=k: v.tensor_scalar(
                    dd[:, :], relg[:, :], off[:, k:k + 1], None, ALU.subtract), inc=sve)
                p.wait("scalar", sve, sve.n)
                p.run("scalar", lambda s: s.activation(d2[:, :], dd[:, :], AF.Square))
                p.run("scalar", lambda s, k=k: s.activation(gk[:, :], d2[:, :], AF.Exp,
                                                            scale=nsh[:, k:k + 1]), inc=sse)
                p.wait("vector", sse, sse.n)
                p.run("vector", lambda v, k=k: v.scalar_tensor_tensor(
                    accs[:, :], gk[:, :], amp[:, k:k + 1], accs[:, :],
                    ALU.mult, ALU.add), inc=sve)
            p.run("vector", lambda v: v.tensor_scalar(accs[:, :], accs[:, :], 8.0, None, ALU.mult), inc=sve)
            p.wait("sync", sve, sve.n)
            p.dma("sync", bass.AP(srel_s.tensor, 1024, [[9344, NH], [1024, 8], [1, 1024]]),
                  accs[:, :], sst)
            p.wait("gpsimd", sst, sst.n)
            for pp in range(128):
                for ci in range(8):
                    o0 = 1023 - pp - ci * 128
                    p.dma("gpsimd", mat3[pp, :, ci, :],
                          bass.AP(srel_s.tensor, o0, [[9344, NH], [1, 8192]]), smt)

        xn_fm = self.fmA
        self.ln_to_fm(x_in, xn_fm, uid + "ln", banks=(0, 1))
        h_fm = xn_fm
        attn_fm = self.fmB

        with p.scope():
            _ws0 = p.sbuf(uid + "_ws0", [128, 8, 128], BF16)
            wslab = [_ws0, _ws0]
            _qk1 = p.sbuf(uid + "_qk1", [128, 4096], BF16)
            _qk2 = p.sbuf(uid + "_qk2", [128, 4096], BF16)
            _vh2 = p.sbuf(uid + "_vh2", [128, 4096], BF16)
            qt_ = [_qk1[0:64, :], _qk1[64:128, :]]
            kt_ = [_qk2[0:64, :], _qk2[64:128, :]]
            vh_ = [_vh2[0:64, :], _vh2[64:128, :]]
            vt = [p.sbuf(uid + f"_vt{i}", [128, TT, 65], BF16) for i in range(2)]
            esc = [p.sbuf(uid + f"_esc{i}", [128, 2048], BF16) for i in range(2)]
            tsc = [p.sbuf(uid + f"_tsc{i}", [128, 2048], F32) for i in range(2)]
            bt8 = [p.sbuf(uid + f"_bt8{i}", [128, 8, 512], BF16) for i in range(2)]
            rb = p.sbuf(uid + "_rb", [64, 512], F32)
            rcq = rb[0:1, :]
            pslab = [self.pbank(0), self.pbank(1)]
            pat = self.pbank(6)
            pbc = self.pbank(7)
            ptr = self.pbank_bf(0)
            wq_h = self.hbm["l0a_win"]

            def emit_slab(slab_idx, wb, dst_top, dst_bot, evac_done_prev):
                p.wait("gpsimd", spe, evac_done_prev)
                p.dma("gpsimd", wslab[wb][:, :, :],
                      wq_h[:, slab_idx * 128:(slab_idx + 1) * 128]
                      .rearrange("(kc p) m -> p kc m", p=128), swl)
                p.wait("tensor", swl, swl.n)
                for r in range(8):
                    pb = pslab[r % 2]
                    if r >= 2:
                        p.wait("tensor", sse, sse.n - 2)
                    for kc in range(8):
                        p.run("tensor", lambda t, wb=wb, r=r, kc=kc, pb=pb: t.matmul(
                            pb[:, :], wslab[wb][:, kc, :],
                            h_fm[:, kc, r * 512:(r + 1) * 512],
                            start=(kc == 0), stop=(kc == 7)),
                            inc=(spe if kc == 7 else None))
                    p.wait("scalar", spe, spe.n)
                    p.run("scalar", lambda s, r=r, pb=pb, dst_top=dst_top:
                          s.activation(dst_top[:, r * 512:(r + 1) * 512], pb[0:64, :],
                                       AF.Copy), inc=sse)
                    p.run("scalar", lambda s, r=r, pb=pb, dst_bot=dst_bot:
                          s.activation(dst_bot[:, r * 512:(r + 1) * 512], pb[64:128, :],
                                       AF.Copy), inc=sse)
                return spe.n

            p.wait("tensor", smt, smt.n)
            qk_used = {0: 0, 1: 0}
            slab_mm_done = 0
            vt_used = 0
            bt_used = {0: 0, 1: 0}
            for h in range(NH):
                hb = h % 2
                p.wait("scalar", spe, qk_used[hb])
                slab_mm_done = emit_slab(h, hb, qt_[hb], kt_[hb], slab_mm_done)
                if h % 2 == 0:
                    p.wait("scalar", spe, vt_used)
                    slab_mm_done = emit_slab(16 + h // 2, 1 - hb, vh_[0], vh_[1], slab_mm_done)
                    p.wait("tensor", sse, sse.n)
                    for half in range(2):
                        for r4 in range(4):
                            if not (half == 0 and r4 == 0):
                                p.wait("tensor", sve, sve.n)
                            for j in range(8):
                                ltk = r4 * 8 + j
                                p.run("tensor", lambda t, half=half, ltk=ltk, j=j: t.transpose(
                                    ptr[:, j * 64:(j + 1) * 64],
                                    vh_[half][:, ltk * 128:(ltk + 1) * 128],
                                    self.identb[half * 64:half * 64 + 64,
                                                half * 64:half * 64 + 64]),
                                    inc=(spe if j == 7 else None))
                            p.wait("vector", spe, spe.n)
                            p.run("vector", lambda v, half=half, r4=r4: v.tensor_copy(
                                vt[half][:, r4 * 8:(r4 + 1) * 8, 0:64],
                                ptr[:, 0:512].rearrange("p (t d) -> p t d", t=8)), inc=sve)
                    p.run("vector", lambda v: v.memset(vt[0][:, :, 64:65], 1.0))
                    p.run("vector", lambda v: v.memset(vt[1][:, :, 64:65], 1.0), inc=sve)
                    p.wait("tensor", sve, sve.n)
                p.wait("tensor", sse, sse.n)
                add_done = 0
                exp_done = {0: 0, 1: 0}
                pv_done = {0: 0, 1: 0}
                pscore = self.pspan(2, 4)
                for qt in range(8):
                    for gg in range(8):
                        eb = gg % 2
                        kcg = gg // 2
                        cib = (gg % 2) * 4
                        if gg % 2 == 0:
                            bb = kcg % 2
                            j0 = 4096 + qt * 512 - kcg * 1024
                            if bt_used[bb]:
                                p.wait("sync", sve, bt_used[bb])
                            p.dma("sync", bt8[bb][:, :, :],
                                  bass.AP(mat3.tensor, h * 8 * 8192 + j0,
                                          [[NH * 8 * 8192, 128], [8192, 8], [1, 512]]), sld)
                        if add_done:
                            p.wait("tensor", sve, add_done)
                        for i in range(4):
                            kc = gg * 4 + i
                            p.run("tensor", lambda t, hb=hb, kc=kc, qt=qt, i=i: t.matmul(
                                pscore[:, i * 512:(i + 1) * 512],
                                kt_[hb][:, kc * 128:(kc + 1) * 128],
                                qt_[hb][:, qt * 512:(qt + 1) * 512],
                                start=True, stop=True), inc=(spe if i == 3 else None))
                        p.wait("vector", spe, spe.n)
                        p.wait("vector", sld, sld.n)
                        if exp_done[eb]:
                            p.wait("vector", sse, exp_done[eb])
                        p.run("vector", lambda v, bb=bb, cib=cib, eb=eb: v.tensor_add(
                            tsc[eb][:, :], pscore[:, :],
                            bt8[bb][:, cib:cib + 4, :].rearrange("p a b -> p (a b)")), inc=sve)
                        add_done = sve.n
                        if gg % 2 == 1:
                            bt_used[bb] = sve.n
                        p.wait("scalar", sve, add_done)
                        if pv_done[eb]:
                            p.wait("scalar", spe, pv_done[eb])
                        p.run("scalar", lambda s, eb=eb: s.activation(
                            esc[eb][:, :], tsc[eb][:, :], AF.Exp,
                            bias=self.cc("neg_shift"), scale=0.125), inc=sse)
                        exp_done[eb] = sse.n
                        p.wait("tensor", sse, exp_done[eb])
                        for i in range(4):
                            kc = gg * 4 + i
                            p.run("tensor", lambda t, hb=hb, kc=kc, eb=eb, i=i: t.matmul(
                                pat[0:65, :], vt[hb][:, kc, :],
                                esc[eb][:, i * 512:(i + 1) * 512],
                                start=(kc == 0), stop=(kc == TT - 1)),
                                inc=(spe if i == 3 else None))
                        pv_done[eb] = spe.n
                    p.wait("vector", spe, spe.n)
                    p.run("vector", lambda v: v.tensor_copy(rcq[0:1, :], pat[64:65, :]))
                    p.hz("vector")
                    p.run("vector", lambda v: v.reciprocal(rcq[0:1, :], rcq[0:1, :]), inc=sve)
                    p.wait("tensor", sve, sve.n)
                    p.run("tensor", lambda t: t.matmul(
                        pbc[0:64, :], self.ones_row[0:1, 0:64], rcq[0:1, :],
                        start=True, stop=True), inc=spe)
                    p.wait("scalar", spe, spe.n)
                    p.run("scalar", lambda s: s.activation(rb[:, :], pbc[0:64, :], AF.Copy), inc=sse)
                    p.wait("vector", sse, sse.n)
                    p.run("vector", lambda v, h=h, qt=qt: v.tensor_mul(
                        attn_fm[(h % 2) * 64:(h % 2) * 64 + 64, h // 2, qt * 512:(qt + 1) * 512],
                        pat[0:64, :], rb[:, :]), inc=sve)
                    p.wait("tensor", sve, sve.n)
                qk_used[hb] = spe.n
                if h % 2 == 1:
                    vt_used = spe.n
        p.barrier()

        with p.scope():
            wg2 = p.sbuf(uid + "_wg2", [128, 8, 2048], BF16)
            bga = p.sbuf(uid + "_bga", [128, D], F32)
            bgg = p.sbuf(uid + "_bgg", [128, D], F32)
            xr = [p.sbuf(uid + f"_xr{i}", [128, D], F32) for i in range(2)]
            asb = p.sbuf(uid + "_asb", [128, D], F32)
            gsb = p.sbuf(uid + "_gsb", [128, D], F32)
            gt = p.sbuf(uid + "_gt", [128, D], F32)
            xo = [p.sbuf(uid + f"_xo{i}", [128, D], F32) for i in range(2)]
            p.dma("gpsimd", wg2[:, :, :], self.hbm["l0a_wgate"].rearrange("(c p) n -> p c n", p=128), swl)
            p.dma("gpsimd", bga[:, :], self.hbm["l0a_bga_bc"][:, :], swl)
            p.dma("gpsimd", bgg[:, :], self.hbm["l0a_bgg_bc"][:, :], swl)
            pa = [self.pbank(0), self.pbank(1)]
            pg = [self.pbank(2), self.pbank(3)]
            p.wait("tensor", swl, swl.n)
            st_done = {0: 0, 1: 0}
            ga_done = 0
            for tc in range(TT):
                b = tc % 2
                if tc >= 2:
                    p.wait("sync", sst, st_done[b])
                p.dma("sync", xr[b][:, :], x_in[tc * 128:(tc + 1) * 128, :], sld)
                if ga_done:
                    p.wait("tensor", sve, ga_done)
                for nb in range(2):
                    for c in range(8):
                        p.run("tensor", lambda t, tc=tc, nb=nb, c=c: t.matmul(
                            pa[nb][:, :], attn_fm[:, c, tc * 128:(tc + 1) * 128],
                            wg2[:, c, nb * 512:(nb + 1) * 512],
                            start=(c == 0), stop=(c == 7)), inc=(spe if c == 7 else None))
                for nb in range(2):
                    for c in range(8):
                        p.run("tensor", lambda t, tc=tc, nb=nb, c=c: t.matmul(
                            pg[nb][:, :], attn_fm[:, c, tc * 128:(tc + 1) * 128],
                            wg2[:, c, 1024 + nb * 512:1024 + (nb + 1) * 512],
                            start=(c == 0), stop=(c == 7)), inc=(spe if c == 7 else None))
                p.wait("vector", spe, spe.n - 2)
                for nb in range(2):
                    p.run("vector", lambda v, nb=nb: v.tensor_add(
                        asb[:, nb * 512:(nb + 1) * 512], pa[nb][:, :],
                        bga[:, nb * 512:(nb + 1) * 512]), inc=sve)
                p.wait("vector", spe, spe.n)
                for nb in range(2):
                    p.run("vector", lambda v, nb=nb: v.tensor_add(
                        gt[:, nb * 512:(nb + 1) * 512], pg[nb][:, :],
                        bgg[:, nb * 512:(nb + 1) * 512]), inc=sve)
                ga_done = sve.n
                p.wait("scalar", sve, sve.n)
                p.run("scalar", lambda s: s.activation(gsb[:, :], gt[:, :], AF.Sigmoid), inc=sse)
                p.wait("vector", sse, sse.n)
                p.wait("vector", sld, sld.n)
                p.run("vector", lambda v: v.tensor_mul(asb[:, :], asb[:, :], gsb[:, :]))
                p.run("vector", lambda v, b=b: v.tensor_add(xo[b][:, :], xr[b][:, :], asb[:, :]), inc=sve)
                p.wait("sync", sve, sve.n)
                p.dma("sync", x_out[tc * 128:(tc + 1) * 128, :], xo[b][:, :], sst)
                st_done[b] = sst.n
        p.barrier()
        return sst



STAGE_SEQ = ("l0ffn", "l0attn", "l0ffn", "l1ffn", "l1attn", "l1ffn")


def build_program(g, stages):
    nc = bass.Bass("TRN2", target_bir_lowering=False, debug=False)
    x = nc.dram_tensor("x", [S, D], F32, kind="ExternalInput").ap()
    y = nc.dram_tensor("y", [S, D], F32, kind="ExternalOutput").ap()
    b = Builder(nc, g)
    scratch = [nc.dram_tensor(f"res{i}", [S, D], F32, kind="Internal").ap()
               for i in range(max(0, len(stages) - 1))]
    last = None
    for i, st in enumerate(stages):
        src = x if i == 0 else scratch[i - 1]
        dst = y if i == len(stages) - 1 else scratch[i]
        uid = f"s{i}"
        fn = {"l0ffn": b.l0ffn, "l0attn": b.l0attn,
              "l1ffn": b.l1ffn, "l1attn": b.l1attn}[st]
        with b.p.scope():
            last = fn(src, dst, uid)
    b.p.wait("sync", last, last.n)
    with b.p.es:
        with nc.Block() as block:
            b.p.emit_into(block)
    return nc


def make_in_maps(g, xs):
    in_maps = []
    for xc in xs:
        m = {"x": np.ascontiguousarray(xc)}
        for name, arr in g.items():
            if not name.startswith("_"):
                m["g_" + name] = arr
        in_maps.append(m)
    return in_maps


def kernel(**inputs):
    g = prep_host_inputs(inputs)
    x = np.asarray(inputs["L0_attn_x"])
    nc = build_program(g, list(STAGE_SEQ))
    res = run_bass_kernel_spmd(nc, make_in_maps(g, [x[c] for c in range(8)]),
                               core_ids=list(range(8)))
    return np.stack([np.asarray(res.results[c]["y"]) for c in range(8)],
                    axis=0).astype(np.float32)


# version 09
# speedup vs baseline: 1.1563x; 1.0152x over previous
import numpy as np
import ml_dtypes
from contextlib import ExitStack

import concourse.bass as bass
import concourse.mybir as mybir
from concourse.bass_utils import run_bass_kernel_spmd

F32 = mybir.dt.float32
BF16 = mybir.dt.bfloat16
AF = mybir.ActivationFunctionType
ALU = mybir.AluOpType
AX = mybir.AxisListType

S = 4096
D = 1024
NH = 16
HD = 64
TT = S // 128

LN_EPS = 1e-5
RMS_EPS = 1.1920929e-07
EPS_LIN = 1e-6
SHIFT = 20.0

ENGINES = ("sync", "gpsimd", "tensor", "vector", "scalar")
DEBUG_SERIAL = False
_DBG_TENSORS = {}



def _bc(v, rows=128):
    v = np.ascontiguousarray(np.asarray(v, dtype=np.float32).reshape(-1))
    return np.ascontiguousarray(np.broadcast_to(v[None, :], (rows, v.size)))


def prep_host_inputs(inp):
    g = {}
    g["ident_bf16"] = np.eye(128, dtype=ml_dtypes.bfloat16)
    g["ident_f32"] = np.eye(128, dtype=np.float32)
    cc = {"ln_eps": LN_EPS, "rms_eps": RMS_EPS, "eps_lin": EPS_LIN,
          "neg_shift": -SHIFT, "one": 1.0, "zero": 0.0}
    g["_ccols_order"] = sorted(cc)
    g["ccols"] = np.ascontiguousarray(
        np.stack([np.full(128, cc[k], dtype=np.float32) for k in g["_ccols_order"]], axis=1))
    g["ones_col_bf16"] = np.ones((128, 1), dtype=ml_dtypes.bfloat16)
    g["ones_row_f32"] = np.ones((1, 128), dtype=np.float32)

    i = np.arange(0, HD, 2, dtype=np.float64)
    inv_freq = 1.0 / (10000.0 ** (i / HD))
    ang = np.outer(np.arange(S, dtype=np.float64), inv_freq)
    cos64 = np.repeat(np.cos(ang), 2, axis=1).astype(np.float32)
    sin = np.sin(ang).astype(np.float32)
    sgnsin64 = np.empty((S, HD), dtype=np.float32)
    sgnsin64[:, 0::2] = -sin
    sgnsin64[:, 1::2] = sin
    g["rope_cos"] = np.ascontiguousarray(np.tile(cos64, (1, 2 * NH)))
    g["rope_sgnsin"] = np.ascontiguousarray(np.tile(sgnsin64, (1, 2 * NH)))

    Wqkv = np.asarray(inp["L1_attn_Wqkv"])
    bqkv = np.asarray(inp["L1_attn_bqkv"])
    WqkvT = Wqkv.T
    cols = []
    for h in range(NH):
        cols.extend(range(h * HD, h * HD + HD))
        cols.extend(range(D + h * HD, D + h * HD + HD))
    for vp in range(NH // 2):
        cols.extend(range(2 * D + vp * 128, 2 * D + vp * 128 + 128))
    cols = np.array(cols)
    g["l1a_wqkvT"] = np.ascontiguousarray(WqkvT[:, cols])
    g["l1a_bqkv_bc"] = _bc(bqkv[cols])
    gq = np.asarray(inp["L1_attn_gq"]); gk = np.asarray(inp["L1_attn_gk"])
    gqk = np.empty(2 * D, dtype=np.float32)
    for h in range(NH):
        gqk[h * 128:h * 128 + HD] = gq[h * HD:(h + 1) * HD]
        gqk[h * 128 + HD:h * 128 + 128] = gk[h * HD:(h + 1) * HD]
    g["l1a_gqk_bc"] = _bc(gqk)
    g["l1a_woutT"] = np.ascontiguousarray(np.asarray(inp["L1_attn_Wout"]).T)
    g["l1a_bout_bc"] = _bc(inp["L1_attn_bout"])
    g["l1a_gout_bc"] = _bc(inp["L1_attn_gout"])

    g["l0f_gатe"] = None; del g["l0f_gатe"]
    g["l0f_wg"] = np.ascontiguousarray(np.concatenate(
        [np.asarray(inp["L0_ffn_gate_proj_w"]),
         np.asarray(inp["L0_ffn_shared_gate_w"])], axis=0))
    g["l0f_wu"] = np.ascontiguousarray(np.concatenate(
        [np.asarray(inp["L0_ffn_up_proj_w"]),
         np.asarray(inp["L0_ffn_shared_up_w"])], axis=0))
    g["l0f_wd"] = np.ascontiguousarray(np.concatenate(
        [np.asarray(inp["L0_ffn_down_proj_w"]),
         np.asarray(inp["L0_ffn_shared_down_w"])], axis=0))
    g["l0f_gatewT"] = np.ascontiguousarray(np.asarray(inp["L0_ffn_gate_w"]).T)
    g["l0f_beta_bc"] = _bc(inp["L0_ffn_beta"])

    w_in = np.asarray(inp["L0_attn_w_in"])
    cols = []
    for h in range(NH):
        cols.extend(range(2 * D + h * HD, 2 * D + h * HD + HD))
        cols.extend(range(h * HD, h * HD + HD))
    for vp in range(NH // 2):
        cols.extend(range(D + vp * 128, D + vp * 128 + 128))
    cols = np.array(cols)
    g["l0a_win"] = np.ascontiguousarray(w_in[:, cols])
    g["l0a_wgate"] = np.ascontiguousarray(np.asarray(inp["L0_attn_w_gate"]))
    bgate = np.asarray(inp["L0_attn_b_gate"])
    g["l0a_bga_bc"] = _bc(bgate[0:D])
    g["l0a_bgg_bc"] = _bc(bgate[D:2 * D])
    rep = lambda a: np.ascontiguousarray(np.repeat(np.asarray(a, np.float32), 8, axis=0))
    g["l0a_amp"] = rep(inp["L0_attn_amplitudes"])
    g["l0a_sharp"] = rep(inp["L0_attn_sharpness"])
    g["l0a_off"] = rep(inp["L0_attn_offsets"])
    pgrid = np.arange(128) % 8
    g["l0a_relgrid"] = np.ascontiguousarray(
        (4095 - 1024 * pgrid[:, None] - np.arange(1024)[None, :]).astype(np.float32))

    g["l1f_ln1g_bc"] = _bc(inp["L1_ffn_ln1_g"])
    g["l1f_ln1b_bc"] = _bc(inp["L1_ffn_ln1_b"])
    g["l1f_ln2g_bc"] = _bc(inp["L1_ffn_ln2_g"])
    g["l1f_ln2b_bc"] = _bc(inp["L1_ffn_ln2_b"])
    wqkv = np.asarray(inp["L1_ffn_w_qkv"])
    bqkv2 = np.asarray(inp["L1_ffn_b_qkv"])
    cols = []
    for h in range(NH):
        cols.extend(range(h * HD, h * HD + HD))
        cols.extend(range(D + h * HD, D + h * HD + HD))
    for vp in range(NH // 2):
        cols.extend(range(2 * D + vp * 128, 2 * D + vp * 128 + 128))
    cols = np.array(cols)
    g["l1f_wqkv"] = np.ascontiguousarray(wqkv[:, cols])
    g["l1f_bqkv"] = np.ascontiguousarray(bqkv2[cols].reshape(24, 128).T)
    g["l1f_wo"] = np.ascontiguousarray(np.asarray(inp["L1_ffn_w_o"]))
    g["l1f_bo_bc"] = _bc(inp["L1_ffn_b_o"])
    g["l1f_gatew"] = np.ascontiguousarray(np.asarray(inp["L1_ffn_gate_w"]))
    g["l1f_gateb_bc"] = _bc(inp["L1_ffn_gate_b"])
    g["l1f_W1"] = np.ascontiguousarray(np.asarray(inp["L1_ffn_W1"]))
    g["l1f_W2"] = np.ascontiguousarray(np.asarray(inp["L1_ffn_W2"]))
    b1 = np.asarray(inp["L1_ffn_b1"])
    g["l1f_b1T"] = np.ascontiguousarray(b1.reshape(8, 32, 128).transpose(0, 2, 1))
    g["l1f_b2"] = np.ascontiguousarray(np.asarray(inp["L1_ffn_b2"]))
    return g


def prep_core_inputs(inp, core):
    x = np.asarray(inp["L0_attn_x"])
    return {"x": np.ascontiguousarray(x[core])}



class Sem:
    __slots__ = ("h", "n")
    def __init__(self, handle):
        self.h = handle
        self.n = 0


class Prog:

    def __init__(self, nc):
        self.nc = nc
        self.es = ExitStack()
        self._scopes = [self.es]
        self.q = {e: [] for e in ENGINES}
        self._sems = {}
        self._waited = {e: {} for e in ENGINES}
        self._hzt = self.es.enter_context(nc.sbuf_tensor("hzt", [128, 1], F32))

    def scope(self):
        import contextlib
        prog = self

        @contextlib.contextmanager
        def _cm():
            st = ExitStack()
            prog._scopes.append(st)
            try:
                yield st
            finally:
                prog._scopes.pop()
                st.close()
        return _cm()

    def barrier(self):
        for e in ENGINES:
            for sem in list(self._sems.values()):
                if sem.n > 0:
                    self.wait(e, sem, sem.n)

    def sem(self, name):
        if name not in self._sems:
            self._sems[name] = Sem(self.es.enter_context(self.nc.semaphore(name)))
        return self._sems[name]

    def sbuf(self, name, shape, dtype):
        return self._scopes[-1].enter_context(self.nc.sbuf_tensor(name, shape, dtype))

    def psum(self, name, shape, dtype=F32):
        return self.es.enter_context(self.nc.psum_tensor(name, shape, dtype))

    def wait(self, eng, sem, val=None):
        if val is None:
            val = sem.n
        if val <= self._waited[eng].get(sem, -1):
            return
        self._waited[eng][sem] = val
        self.q[eng].append(lambda e, s=sem.h, v=val: e.wait_ge(s, v))

    def hz(self, eng):
        sem = self.sem("hz_" + eng)
        eng_dummy = {"vector": lambda e: e.memset(self._hzt[:, :], 0.0),
                     "scalar": lambda e: e.memzero(self._hzt[:, :]),
                     "gpsimd": lambda e: e.memset(self._hzt[:, :], 0.0)}[eng]
        self.run(eng, eng_dummy, inc=sem)
        self.wait(eng, sem, sem.n)

    def run(self, eng, fn, inc=None, inc_by=1):
        if inc is not None:
            inc.n += inc_by
            self.q[eng].append((fn, inc, inc_by))
        else:
            self.q[eng].append(fn)

    def dma(self, eng, out, in_, sem, **kw):
        def fn(e, out=out, in_=in_, kw=kw):
            return e.dma_start(out=out, in_=in_, **kw)
        self.run(eng, fn, inc=sem, inc_by=16)

    def emit_into(self, block):
        for eng in ENGINES:
            items = self.q[eng]
            if not items:
                continue
            def body(e, items=items):
                for it in items:
                    if isinstance(it, tuple):
                        fn, sem, k = it
                        fn(e).then_inc(sem.h, k)
                    else:
                        it(e)
            getattr(block, eng)(body)
            self.q[eng] = []


class Builder:

    def __init__(self, nc, g):
        self.nc = nc
        self.p = Prog(nc)
        self.g = g
        self.hbm = {}
        for name, arr in g.items():
            if name.startswith("_"):
                continue
            dt = BF16 if arr.dtype == ml_dtypes.bfloat16 else F32
            self.hbm[name] = nc.dram_tensor("g_" + name, list(arr.shape), dt,
                                            kind="ExternalInput").ap()
        self.ccols_order = g["_ccols_order"]
        p = self.p
        self.identb = p.sbuf("c_identb", [128, 128], BF16)
        self.identf = p.sbuf("c_identf", [128, 128], F32)
        self.ones_b = p.sbuf("c_ones_b", [128, 1], BF16)
        self.ones_row = p.sbuf("c_ones_row", [1, 128], F32)
        self.ccols = p.sbuf("c_ccols", [128, len(self.ccols_order)], F32)
        ld = p.sem("c_ld")
        p.dma("sync", self.identb[:, :], self.hbm["ident_bf16"][:, :], ld)
        p.dma("sync", self.identf[:, :], self.hbm["ident_f32"][:, :], ld)
        p.dma("sync", self.ones_b[:, :], self.hbm["ones_col_bf16"][:, :], ld)
        p.dma("sync", self.ones_row[:, :], self.hbm["ones_row_f32"][:, :], ld)
        p.dma("sync", self.ccols[:, :], self.hbm["ccols"][:, :], ld)
        for e in ("tensor", "vector", "scalar", "gpsimd"):
            p.wait(e, ld)
        self.fmA = p.sbuf("ar_fmA", [128, 8, S], BF16)
        self.fmB = p.sbuf("ar_fmB", [128, 8, S], BF16)
        self.c_tm = p.sbuf("ar_ctm", [128, TT, 9], F32)
        self.pball = p.psum("pball", [128, 4096], F32)
        self.pb = [self.pball[:, i * 512:(i + 1) * 512] for i in range(8)]
        self.dump_tags = set()
        self.dumps = {}

    def maybe_dump(self, tag, ap, shape, dtype):
        if tag not in self.dump_tags:
            return
        out = self.nc.dram_tensor("dbg_" + tag, list(shape), dtype,
                                  kind="ExternalOutput").ap()
        self.p.barrier()
        sdbg = self.p.sem("dbg")
        self.p.dma("gpsimd", out, ap, sdbg)
        self.p.wait("gpsimd", sdbg)
        self.dumps[tag] = out

    def pbank(self, i):
        return self.pb[i]

    def pbank_bf(self, i):
        return self.pb[i].bitcast(BF16)

    def pspan(self, i, n):
        return self.pball[:, i * 512:(i + n) * 512]

    def cc(self, name):
        i = self.ccols_order.index(name)
        return self.ccols[:, i:i + 1]


    def ln_to_fm(self, x_in, xn_fm, uid, xn_tm_hbm=None, g_bc=None, b_bc=None,
                 banks=(0, 1)):
        p = self.p
        with p.scope():
            sld = p.sem(uid + "_ld")
            sve = p.sem(uid + "_ve")
            sse = p.sem(uid + "_se")
            spe = p.sem(uid + "_pe")
            sst = p.sem(uid + "_st")

            xt = [p.sbuf(uid + f"_xt{i}", [128, D], F32) for i in range(2)]
            xc = [p.sbuf(uid + f"_xc{i}", [128, D], F32) for i in range(2)]
            sq = p.sbuf(uid + "_sq", [128, D], F32)
            st_ = p.sbuf(uid + "_stats", [128, 4], F32)
            xnb = [p.sbuf(uid + f"_xnb{i}", [128, D], BF16) for i in range(2)]
            xn32 = p.sbuf(uid + "_xn32", [128, D], F32) if xn_tm_hbm is not None else None
            ps = [self.pbank_bf(banks[0]), self.pbank_bf(banks[1])]
            if g_bc is not None:
                gb = p.sbuf(uid + "_gb", [128, D], F32)
                bb = p.sbuf(uid + "_bb", [128, D], F32)
                p.dma("sync", gb[:, :], g_bc, sld)
                p.dma("sync", bb[:, :], b_bc, sld)

            done = {0: 0, 1: 0}
            for tt in range(TT):
                b = tt % 2
                if tt >= 2:
                    p.wait("sync", sve, done[b])
                p.dma("sync", xt[b][:, :], x_in[tt * 128:(tt + 1) * 128, :], sld)
                p.wait("vector", sld, sld.n)
                p.run("vector", lambda v, b=b: v.reduce_sum(st_[:, 1:2], xt[b][:, :], axis=AX.X))
                p.hz("vector")
                p.run("vector", lambda v: v.tensor_scalar(st_[:, 2:3], st_[:, 1:2], 1.0 / D, None, ALU.mult))
                p.hz("vector")
                p.run("vector", lambda v, b=b: v.tensor_scalar(xc[b][:, :], xt[b][:, :], st_[:, 2:3], None, ALU.subtract), inc=sve)
                p.wait("scalar", sve, sve.n)
                p.run("scalar", lambda s, b=b: s.activation(sq[:, :], xc[b][:, :], AF.Square, accum_out=st_[:, 0:1]))
                p.hz("scalar")
                p.run("scalar", lambda s: s.activation(st_[:, 3:4], st_[:, 0:1], AF.Sqrt,
                                                       bias=self.cc("ln_eps"), scale=1.0 / D), inc=sse)
                p.wait("vector", sse, sse.n)
                p.run("vector", lambda v: v.reciprocal(st_[:, 3:4], st_[:, 3:4]))
                p.hz("vector")
                if g_bc is None:
                    p.run("vector", lambda v, b=b: v.tensor_scalar(
                        xnb[b][:, :], xc[b][:, :], st_[:, 3:4], None, ALU.mult), inc=sve)
                    if xn32 is not None:
                        p.wait("vector", sst, sst.n)
                        p.run("vector", lambda v, b=b: v.tensor_scalar(
                            xn32[:, :], xc[b][:, :], st_[:, 3:4], None, ALU.mult), inc=sve)
                else:
                    p.run("vector", lambda v, b=b: v.scalar_tensor_tensor(
                        sq[:, :], xc[b][:, :], st_[:, 3:4], gb[:, :], ALU.mult, ALU.mult))
                    p.run("vector", lambda v, b=b: v.tensor_add(xnb[b][:, :], sq[:, :], bb[:, :]), inc=sve)
                    if xn32 is not None:
                        p.wait("vector", sst, sst.n)
                        p.run("vector", lambda v: v.tensor_add(xn32[:, :], sq[:, :], bb[:, :]), inc=sve)
                if xn32 is not None:
                    p.wait("sync", sve, sve.n)
                    p.dma("sync", xn_tm_hbm[tt * 128:(tt + 1) * 128, :], xn32[:, :], sst)
                p.wait("tensor", sve, sve.n)
                if tt >= 2:
                    p.wait("tensor", sve, done[b])
                for c in range(8):
                    p.run("tensor", lambda t, b=b, c=c: t.transpose(
                        ps[b][:, c * 128:(c + 1) * 128], xnb[b][:, c * 128:(c + 1) * 128],
                        self.identb[:, :]), inc=(spe if c == 7 else None))
                p.wait("vector", spe, spe.n)
                p.run("vector", lambda v, b=b, tt=tt: v.tensor_copy(
                    xn_fm[:, :, tt * 128:(tt + 1) * 128],
                    ps[b][:, :].rearrange("p (c t) -> p c t", c=8)), inc=sve)
                done[b] = sve.n
                if DEBUG_SERIAL:
                    p.barrier()
        p.barrier()


    def l1attn(self, x_in, x_out, uid):
        nc, p = self.nc, self.p
        qf_s = nc.dram_tensor(uid + "_qf", [NH, S, HD], BF16, kind="Internal").ap()
        kf_s = nc.dram_tensor(uid + "_kf", [NH, S, HD], BF16, kind="Internal").ap()
        v_s = nc.dram_tensor(uid + "_v", [NH, S, HD], BF16, kind="Internal").ap()

        xn_fm = self.fmA
        self.ln_to_fm(x_in, xn_fm, uid + "ln", banks=(0, 1))
        self.maybe_dump("xnfm", xn_fm[:, :, :], [128, 8, S], BF16)

        swl = p.sem(uid + "_wl")
        sld = p.sem(uid + "_ld")
        spe = p.sem(uid + "_pe")
        sve = p.sem(uid + "_ve")
        sse = p.sem(uid + "_se")
        sst = p.sem(uid + "_st")

        flat = self.fmB[:, :, :].rearrange("p c t -> p (c t)")

        def carve32(lo_k, n_elems_f32):
            return flat[:, lo_k:lo_k + 2 * n_elems_f32].bitcast(F32)

        qkv = carve32(0, 3072)
        ta = carve32(6144, 2048)
        tb = carve32(10240, 2048)
        tc = carve32(14336, 2048)
        ct = carve32(18432, 2048)
        stb = carve32(22528, 2048)
        outb = flat[:, 26624:29696]

        with p.scope():
            wq3 = p.sbuf(uid + "_wq", [128, 8, 3072], BF16)
            wq = wq3[:, :, :]
            bq_bc = p.sbuf(uid + "_bqbc", [128, 3072], F32)
            gqk = p.sbuf(uid + "_gqk", [128, 2 * D], F32)
            p.dma("gpsimd", wq,
                  self.hbm["l1a_wqkvT"].rearrange("(kc p) n -> p kc n", p=128), swl)
            p.dma("gpsimd", bq_bc[:, :], self.hbm["l1a_bqkv_bc"][:, :], swl)
            p.dma("gpsimd", gqk[:, :], self.hbm["l1a_gqk_bc"][:, :], swl)
            rst = p.sbuf(uid + "_rst", [128, 4], F32)
            sqscr = p.sbuf(uid + "_sqscr", [128, 2 * D], F32)
            pq = [self.pbank(2), self.pbank(3)]

            def slabs(apx, lo, hi):
                return apx[:, 0:2048].rearrange("p (h td) -> p h td", h=NH)[:, :, lo:hi]

            evac_done = {0: 0, 1: 0}
            st_done = 0
            p.wait("tensor", swl, swl.n)
            for tt in range(TT):
                if tt >= 1:
                    p.wait("sync", sve, ve_done)
                p.dma("sync", ct, self.hbm["rope_cos"][tt * 128:(tt + 1) * 128, :], sld)
                p.dma("sync", stb, self.hbm["rope_sgnsin"][tt * 128:(tt + 1) * 128, :], sld)
                for nb in range(6):
                    pbix = nb % 2
                    if evac_done[pbix]:
                        p.wait("tensor", sve, evac_done[pbix])
                    for kc in range(8):
                        p.run("tensor", lambda t, tt=tt, nb=nb, kc=kc, pbix=pbix: t.matmul(
                            pq[pbix][:, :], xn_fm[:, kc, tt * 128:(tt + 1) * 128],
                            wq[:, kc, nb * 512:(nb + 1) * 512],
                            start=(kc == 0), stop=(kc == 7)),
                            inc=(spe if kc == 7 else None))
                    p.wait("vector", spe, spe.n)
                    if nb == 0:
                        p.wait("vector", sst, st_done)
                    p.run("vector", lambda v, nb=nb, pbix=pbix: v.tensor_add(
                        qkv[:, nb * 512:(nb + 1) * 512], pq[pbix][:, :],
                        bq_bc[:, nb * 512:(nb + 1) * 512]), inc=sve)
                    evac_done[pbix] = sve.n
                p.wait("scalar", sve, sve.n)
                p.run("scalar", lambda s: s.activation(
                    slabs(sqscr, 0, HD), slabs(qkv, 0, HD), AF.Square, accum_out=rst[:, 0:1]))
                p.run("scalar", lambda s: s.activation(
                    slabs(sqscr, HD, 128), slabs(qkv, HD, 128), AF.Square, accum_out=rst[:, 2:3]))
                p.hz("scalar")
                p.run("scalar", lambda s: s.activation(rst[:, 1:2], rst[:, 0:1], AF.Sqrt,
                                                       bias=self.cc("rms_eps"), scale=1.0 / D))
                p.run("scalar", lambda s: s.activation(rst[:, 3:4], rst[:, 2:3], AF.Sqrt,
                                                       bias=self.cc("rms_eps"), scale=1.0 / D), inc=sse)
                p.wait("vector", sse, sse.n)
                p.wait("vector", sld, sld.n)
                p.run("vector", lambda v: v.reciprocal(rst[:, 1:2], rst[:, 1:2]))
                p.run("vector", lambda v: v.reciprocal(rst[:, 3:4], rst[:, 3:4]))
                p.hz("vector")
                p.run("vector", lambda v: v.scalar_tensor_tensor(
                    slabs(ta, 0, HD), slabs(qkv, 0, HD), rst[:, 1:2],
                    slabs(gqk, 0, HD), ALU.mult, ALU.mult))
                p.run("vector", lambda v: v.scalar_tensor_tensor(
                    slabs(ta, HD, 128), slabs(qkv, HD, 128), rst[:, 3:4],
                    slabs(gqk, HD, 128), ALU.mult, ALU.mult))
                p.hz("vector")
                p.run("vector", lambda v: v.tensor_mul(tc[:, :], ta[:, :], ct[:, :]))
                p.run("vector", lambda v: v.tensor_copy(tb[:, 0:2048:2], ta[:, 1:2048:2]))
                p.run("vector", lambda v: v.tensor_copy(tb[:, 1:2048:2], ta[:, 0:2048:2]))
                p.run("vector", lambda v: v.tensor_mul(ta[:, :], tb[:, :], stb[:, :]))
                p.run("vector", lambda v: v.tensor_add(tb[:, :], tc[:, :], ta[:, :]))
                p.wait("vector", sst, st_done)
                p.run("vector", lambda v: v.tensor_scalar(
                    outb[:, 0:2048], tb[:, :], 0.0, None, ALU.max))
                p.run("vector", lambda v: v.tensor_copy(
                    outb[:, 2048:3072], qkv[:, 2048:3072]), inc=sve)
                ve_done = sve.n
                p.wait("scalar", sve, sve.n)
                for half in range(2):
                    r0 = half * 64
                    src_qk = outb[r0:r0 + 64, 0:2048].rearrange("p (h td) -> p h td", h=NH)
                    src_v = outb[r0:r0 + 64, 2048:3072].rearrange("p (vp two d) -> p vp two d", vp=8, two=2)
                    t0 = tt * 128 + r0
                    p.dma("scalar", qf_s[:, t0:t0 + 64, :].rearrange("h p d -> p h d"),
                          src_qk[:, :, 0:HD], sst)
                    p.dma("scalar", kf_s[:, t0:t0 + 64, :].rearrange("h p d -> p h d"),
                          src_qk[:, :, HD:128], sst)
                    p.dma("scalar", v_s[:, t0:t0 + 64, :].rearrange(
                        "(vp two) p d -> p vp two d", two=2), src_v, sst)
                st_done = sst.n
        p.barrier()
        self.maybe_dump("qfs", qf_s[:, :, :], [NH, S, HD], BF16)
        self.maybe_dump("kfs", kf_s[:, :, :], [NH, S, HD], BF16)
        self.maybe_dump("vs", v_s[:, :, :], [NH, S, HD], BF16)

        attn_fm = self.fmB
        with p.scope():
            kh = [p.sbuf(uid + f"_kh{i}", [128, TT, HD], BF16) for i in range(2)]
            vh = [p.sbuf(uid + f"_vh{i}", [128, TT, HD], BF16) for i in range(2)]
            qh = [p.sbuf(uid + f"_qh{i}", [128, TT, HD], BF16) for i in range(2)]
            vkt = p.sbuf(uid + "_vkt", [64, 65], BF16)
            qfm = p.sbuf(uid + "_qfm", [64, TT, 128], BF16)
            atm = p.sbuf(uid + "_atm", [128, TT, HD], BF16)
            rcp = p.sbuf(uid + "_rcp", [128, 4], F32)
            pvk = self.pbank(4)
            pqt = self.pbank_bf(5)
            pres = [self.pbank(6), self.pbank(7)]
            pat = self.pbank_bf(1)

            head_ve = {0: 0, 1: 0}
            head_pe = {0: 0, 1: 0}
            for h in range(NH):
                hb = h % 2
                if h >= 2:
                    p.wait("gpsimd", sve, head_ve[hb])
                    p.wait("gpsimd", spe, head_pe[hb])
                p.dma("gpsimd", kh[hb][:, :, :], kf_s[h].rearrange("(t p) d -> p t d", p=128), swl)
                p.dma("gpsimd", vh[hb][:, :, :], v_s[h].rearrange("(t p) d -> p t d", p=128), swl)
                p.dma("gpsimd", qh[hb][:, :, :], qf_s[h].rearrange("(t p) d -> p t d", p=128), swl)
                p.wait("tensor", swl, swl.n)
                for lt in range(TT):
                    p.run("tensor", lambda t, hb=hb, lt=lt: t.matmul(
                        pvk[0:64, 0:HD], kh[hb][:, lt, :], vh[hb][:, lt, :],
                        start=(lt == 0), stop=(lt == TT - 1)))
                    p.run("tensor", lambda t, hb=hb, lt=lt: t.matmul(
                        pvk[0:64, HD:65], kh[hb][:, lt, :], self.ones_b[:, :],
                        start=False, stop=(lt == TT - 1)),
                        inc=(spe if lt == TT - 1 else None))
                p.wait("scalar", spe, spe.n)
                p.run("scalar", lambda s: s.activation(vkt[:, :], pvk[0:64, 0:65], AF.Copy), inc=sse)
                for r in range(4):
                    for j in range(8):
                        lt = r * 8 + j
                        p.run("tensor", lambda t, hb=hb, lt=lt, j=j: t.transpose(
                            pqt[0:64, j * 128:(j + 1) * 128], qh[hb][:, lt, :],
                            self.identb[:, :]), inc=(spe if j == 7 else None))
                    p.wait("vector", spe, spe.n)
                    p.run("vector", lambda v, r=r: v.tensor_copy(
                        qfm[:, r * 8:(r + 1) * 8, :].rearrange("p t n -> p (t n)"),
                        pqt[0:64, :]), inc=sve)
                    p.wait("tensor", sve, sve.n)
                p.wait("tensor", sse, sse.n)
                res_done = {0: 0, 1: 0}
                for ltg in range(TT // 4):
                    pbix = ltg % 2
                    if res_done[pbix]:
                        p.wait("tensor", sve, res_done[pbix])
                    for i in range(4):
                        p.run("tensor", lambda t, ltg=ltg, i=i, pbix=pbix: t.matmul(
                            pres[pbix][:, i * 128:i * 128 + 65], qfm[:, ltg * 4 + i, :],
                            vkt[:, :], start=True, stop=True),
                            inc=(spe if i == 3 else None))
                    p.wait("vector", spe, spe.n)
                    p.run("vector", lambda v, pbix=pbix: v.tensor_scalar(
                        rcp[:, 0:4],
                        pres[pbix].rearrange("p (a b) -> p a b", b=128)[:, :, HD:HD + 1]
                        .rearrange("p a b -> p (a b)"),
                        self.cc("eps_lin"), None, ALU.add))
                    p.hz("vector")
                    p.run("vector", lambda v: v.reciprocal(rcp[:, 0:4], rcp[:, 0:4]))
                    p.hz("vector")
                    for i in range(4):
                        p.run("vector", lambda v, ltg=ltg, i=i, pbix=pbix: v.tensor_scalar(
                            atm[:, ltg * 4 + i, :], pres[pbix][:, i * 128:i * 128 + HD],
                            rcp[:, i:i + 1], None, ALU.mult), inc=(sve if i == 3 else None))
                    res_done[pbix] = sve.n
                p.wait("tensor", sve, sve.n)
                for r in range(4):
                    for j in range(8):
                        lt = r * 8 + j
                        p.run("tensor", lambda t, lt=lt, j=j: t.transpose(
                            pat[0:64, j * 128:(j + 1) * 128], atm[:, lt, :],
                            self.identb[:, :]), inc=(spe if j == 7 else None))
                    p.wait("vector", spe, spe.n)
                    p.run("vector", lambda v, h=h, r=r: v.tensor_copy(
                        attn_fm[(h % 2) * 64:(h % 2) * 64 + 64, h // 2,
                                r * 1024:(r + 1) * 1024], pat[0:64, :]), inc=sve)
                    p.wait("tensor", sve, sve.n)
                head_ve[hb] = sve.n
                head_pe[hb] = spe.n
        p.barrier()
        self.maybe_dump("attnfm", attn_fm[:, :, :], [128, 8, S], BF16)

        with p.scope():
            wo3 = p.sbuf(uid + "_wo", [128, 8, 1024], BF16)
            wo = wo3[:, :, :]
            p.dma("gpsimd", wo, self.hbm["l1a_woutT"].rearrange("(c p) n -> p c n", p=128), swl)
            bout = p.sbuf(uid + "_bout", [128, D], F32)
            gout = p.sbuf(uid + "_gout", [128, D], F32)
            p.dma("gpsimd", bout[:, :], self.hbm["l1a_bout_bc"][:, :], swl)
            p.dma("gpsimd", gout[:, :], self.hbm["l1a_gout_bc"][:, :], swl)

            xr = [p.sbuf(uid + f"_xr{i}", [128, D], F32) for i in range(2)]
            o32 = [p.sbuf(uid + f"_o32{i}", [128, D], F32) for i in range(2)]
            sq2 = p.sbuf(uid + "_sq2", [128, D], F32)
            st2 = p.sbuf(uid + "_st2", [128, 2], F32)
            po = [self.pbank(2), self.pbank(3)]
            xo = [p.sbuf(uid + f"_xo{i}", [128, D], F32) for i in range(2)]

            p.wait("tensor", swl, swl.n)
            e_done = {0: 0, 1: 0}
            po_done = {0: 0, 1: 0}
            for tt in range(TT):
                b = tt % 2
                if tt >= 2:
                    p.wait("sync", sst, e_done[b])
                p.dma("sync", xr[b][:, :], x_in[tt * 128:(tt + 1) * 128, :], sld)
                for nb in range(2):
                    if po_done[nb]:
                        p.wait("tensor", sve, po_done[nb])
                    for c in range(8):
                        p.run("tensor", lambda t, tt=tt, nb=nb, c=c: t.matmul(
                            po[nb][:, :], attn_fm[:, c, tt * 128:(tt + 1) * 128],
                            wo[:, c, nb * 512:(nb + 1) * 512],
                            start=(c == 0), stop=(c == 7)), inc=(spe if c == 7 else None))
                    p.wait("vector", spe, spe.n)
                    p.run("vector", lambda v, b=b, nb=nb: v.tensor_add(
                        o32[b][:, nb * 512:(nb + 1) * 512], po[nb][:, :],
                        bout[:, nb * 512:(nb + 1) * 512]), inc=sve)
                    po_done[nb] = sve.n
                p.wait("scalar", sve, sve.n)
                p.run("scalar", lambda s, b=b: s.activation(sq2[:, :], o32[b][:, :], AF.Square,
                                                            accum_out=st2[:, 0:1]))
                p.hz("scalar")
                p.run("scalar", lambda s: s.activation(st2[:, 1:2], st2[:, 0:1], AF.Sqrt,
                                                       bias=self.cc("rms_eps"), scale=1.0 / D), inc=sse)
                p.wait("vector", sse, sse.n)
                p.wait("vector", sld, sld.n)
                p.run("vector", lambda v: v.reciprocal(st2[:, 1:2], st2[:, 1:2]))
                p.hz("vector")
                p.run("vector", lambda v, b=b: v.scalar_tensor_tensor(
                    sq2[:, :], o32[b][:, :], st2[:, 1:2], gout[:, :], ALU.mult, ALU.mult))
                p.run("vector", lambda v, b=b: v.tensor_add(xo[b][:, :], xr[b][:, :], sq2[:, :]), inc=sve)
                p.wait("sync", sve, sve.n)
                p.dma("sync", x_out[tt * 128:(tt + 1) * 128, :], xo[b][:, :], sst)
                e_done[b] = sst.n
        p.barrier()
        return sst


    def l0ffn(self, x_in, x_out, uid):
        nc, p = self.nc, self.p
        NE = 9
        DFF = 2048
        xn_fm = self.fmA
        self.ln_to_fm(x_in, xn_fm, uid + "ln", banks=(0, 1))

        swl = p.sem(uid + "_wl")
        sld = p.sem(uid + "_ld")
        spe = p.sem(uid + "_pe")
        sve = p.sem(uid + "_ve")
        sse = p.sem(uid + "_se")
        sst = p.sem(uid + "_st")

        acc = self.fmB[:, :, :].rearrange("p c t -> p (c t)").bitcast(F32)

        with p.scope():
            gw = p.sbuf(uid + "_gw", [128, 8, 8], BF16)
            beta = p.sbuf(uid + "_beta", [128, 8], F32)
            z = p.sbuf(uid + "_z", [128, 8], F32)
            cmp_ = p.sbuf(uid + "_cmp", [128, 8], F32)
            rank = p.sbuf(uid + "_rank", [128, 8], F32)
            sig = p.sbuf(uid + "_sig", [128, 8], F32)
            pl = self.pbank(6)
            p.dma("gpsimd", gw[:, :, :],
                  self.hbm["l0f_gatewT"].rearrange("(kc p) e -> p kc e", p=128), swl)
            p.dma("gpsimd", beta[:, :], self.hbm["l0f_beta_bc"][:, :], swl)
            p.run("vector", lambda v: v.memset(self.c_tm[:, :, 8:9], 1.0), inc=sve)
            p.wait("tensor", swl, swl.n)
            prev_se = 0
            for tc in range(TT):
                if prev_se:
                    p.wait("tensor", sse, prev_se)
                for kc in range(8):
                    p.run("tensor", lambda t, tc=tc, kc=kc: t.matmul(
                        pl[:, 0:8], xn_fm[:, kc, tc * 128:(tc + 1) * 128],
                        gw[:, kc, :], start=(kc == 0), stop=(kc == 7)),
                        inc=(spe if kc == 7 else None))
                p.wait("vector", spe, spe.n)
                p.run("vector", lambda v: v.tensor_add(z[:, :], pl[:, 0:8], beta[:, :]), inc=sve)
                p.hz("vector")
                for j in range(8):
                    p.run("vector", lambda v, j=j: v.tensor_scalar(
                        cmp_[:, :], z[:, :], z[:, j:j + 1], None, ALU.is_lt))
                    if j == 0:
                        p.hz("vector")
                        p.run("vector", lambda v: v.tensor_copy(rank[:, :], cmp_[:, :]))
                    else:
                        p.hz("vector")
                        p.run("vector", lambda v: v.tensor_add(rank[:, :], rank[:, :], cmp_[:, :]))
                    p.hz("vector")
                p.run("vector", lambda v: v.tensor_scalar(
                    cmp_[:, :], rank[:, :], 2.0, None, ALU.is_lt), inc=sve)
                p.wait("scalar", spe, spe.n)
                p.wait("scalar", sve, sve.n)
                p.run("scalar", lambda s: s.activation(sig[:, :], pl[:, 0:8], AF.Sigmoid), inc=sse)
                prev_se = sse.n
                p.wait("vector", sse, sse.n)
                p.run("vector", lambda v, tc=tc: v.tensor_mul(
                    self.c_tm[:, tc, 0:8], sig[:, :], cmp_[:, :]), inc=sve)
        p.barrier()

        TSN = 2
        NQ = 4
        NTT = 4
        with p.scope():
            wg = [p.sbuf(uid + f"_wg{i}", [128, 8, 512], BF16) for i in range(2)]
            wu = [p.sbuf(uid + f"_wu{i}", [128, 8, 512], BF16) for i in range(2)]
            wd = [p.sbuf(uid + f"_wd{i}", [128, 4, 1024], BF16) for i in range(2)]
            hbf = p.sbuf(uid + "_hbf", [128, 4, 512], BF16)
            sg = p.sbuf(uid + "_sg", [128, 512], F32)
            xr = [p.sbuf(uid + f"_xr{i}", [128, D], F32) for i in range(2)]
            xo = [p.sbuf(uid + f"_xo{i}", [128, D], F32) for i in range(2)]

            wdone = {0: 0, 1: 0}
            st_done = {0: 0, 1: 0}
            out_done = {0: 0, 1: 0}
            gu_done = {}
            for ts in range(TSN):
                p.wait("vector", sst, sst.n)
                p.run("vector", lambda v: v.memset(acc[:, :], 0.0), inc=sve)
                p.wait("tensor", sve, sve.n)
                for e in range(NE):
                    for q in range(NQ):
                        it = (ts * NE + e) * NQ + q
                        wb = it % 2
                        if it >= 2:
                            p.wait("gpsimd", spe, wdone[wb])
                        p.dma("gpsimd", wg[wb][:, :, :],
                              self.hbm["l0f_wg"][e][:, q * 512:(q + 1) * 512]
                              .rearrange("(kc p) n -> p kc n", p=128), swl)
                        p.dma("gpsimd", wu[wb][:, :, :],
                              self.hbm["l0f_wu"][e][:, q * 512:(q + 1) * 512]
                              .rearrange("(kc p) n -> p kc n", p=128), swl)
                        p.dma("gpsimd", wd[wb][:, :, :],
                              self.hbm["l0f_wd"][e][q * 512:(q + 1) * 512, :]
                              .rearrange("(dc p) n -> p dc n", p=128), swl)
                        p.wait("tensor", swl, swl.n)
                        for tt in range(NTT):
                            t0 = ts * 2048 + tt * 512
                            for dc in range(4):
                                bA = self.pbank((dc % 2) * 2)
                                bB = self.pbank((dc % 2) * 2 + 1)
                                key = dc % 2
                                if gu_done.get(key):
                                    p.wait("tensor", sve, gu_done[key][0])
                                    p.wait("tensor", sse, gu_done[key][1])
                                for kc in range(8):
                                    p.run("tensor", lambda t, wb=wb, dc=dc, kc=kc, t0=t0, bA=bA: t.matmul(
                                        bA[:, :], wg[wb][:, kc, dc * 128:(dc + 1) * 128],
                                        xn_fm[:, kc, t0:t0 + 512],
                                        start=(kc == 0), stop=(kc == 7)),
                                        inc=(spe if kc == 7 else None))
                                for kc in range(8):
                                    p.run("tensor", lambda t, wb=wb, dc=dc, kc=kc, t0=t0, bB=bB: t.matmul(
                                        bB[:, :], wu[wb][:, kc, dc * 128:(dc + 1) * 128],
                                        xn_fm[:, kc, t0:t0 + 512],
                                        start=(kc == 0), stop=(kc == 7)),
                                        inc=(spe if kc == 7 else None))
                                p.wait("scalar", spe, spe.n - 1)
                                p.wait("scalar", sve, sve.n)
                                p.run("scalar", lambda s, bA=bA: s.activation(
                                    sg[:, :], bA[:, :], AF.Silu), inc=sse)
                                p.wait("vector", sse, sse.n)
                                p.wait("vector", spe, spe.n)
                                p.run("vector", lambda v, dc=dc, bB=bB: v.tensor_mul(
                                    hbf[:, dc, :], sg[:, :], bB[:, :]), inc=sve)
                                gu_done[dc % 2] = (sve.n, sse.n)
                            p.wait("tensor", sve, sve.n)
                            for tck in range(4):
                                ob = 4 + (tck % 2) * 2
                                pO = [self.pbank(ob), self.pbank(ob + 1)]
                                if out_done[tck % 2]:
                                    p.wait("tensor", sve, out_done[tck % 2])
                                for dc in range(4):
                                    for nb in range(2):
                                        p.run("tensor", lambda t, wb=wb, dc=dc, nb=nb, tck=tck, pO=pO: t.matmul(
                                            pO[nb][:, :],
                                            hbf[:, dc, tck * 128:(tck + 1) * 128],
                                            wd[wb][:, dc, nb * 512:(nb + 1) * 512],
                                            start=(dc == 0), stop=(dc == 3)),
                                            inc=(spe if (dc == 3 and nb == 1) else None))
                                p.wait("vector", spe, spe.n)
                                gtc = (t0 // 128) + tck
                                for nb in range(2):
                                    p.run("vector", lambda v, e=e, nb=nb, gtc=gtc, pO=pO: v.scalar_tensor_tensor(
                                        acc[:, (gtc % 16) * 1024 + nb * 512:(gtc % 16) * 1024 + (nb + 1) * 512],
                                        pO[nb][:, :], self.c_tm[:, gtc, e:e + 1],
                                        acc[:, (gtc % 16) * 1024 + nb * 512:(gtc % 16) * 1024 + (nb + 1) * 512],
                                        ALU.mult, ALU.add), inc=sve)
                                out_done[tck % 2] = sve.n
                        wdone[wb] = spe.n
                for tc in range(16):
                    b = tc % 2
                    gtc = ts * 16 + tc
                    if tc >= 2:
                        p.wait("sync", sst, st_done[b])
                    p.dma("sync", xr[b][:, :], x_in[gtc * 128:(gtc + 1) * 128, :], sld)
                    p.wait("vector", sld, sld.n)
                    p.wait("vector", sst, st_done[b])
                    p.run("vector", lambda v, b=b, tc=tc: v.scalar_tensor_tensor(
                        xo[b][:, :], acc[:, tc * 1024:(tc + 1) * 1024], 0.5,
                        xr[b][:, :], ALU.mult, ALU.add), inc=sve)
                    p.wait("sync", sve, sve.n)
                    p.dma("sync", x_out[gtc * 128:(gtc + 1) * 128, :], xo[b][:, :], sst)
                    st_done[b] = sst.n
        p.barrier()
        return sst

    def l1ffn(self, x_in, x_out, uid):
        nc, p = self.nc, self.p
        xin_s = nc.dram_tensor(uid + "_xin", [S, D], F32, kind="Internal").ap()
        xmid_s = nc.dram_tensor(uid + "_xmid", [S, D], F32, kind="Internal").ap()

        swl = p.sem(uid + "_wl")
        sld = p.sem(uid + "_ld")
        spe = p.sem(uid + "_pe")
        sve = p.sem(uid + "_ve")
        sse = p.sem(uid + "_se")
        sst = p.sem(uid + "_st")

        self.ln_to_fm(x_in, self.fmA, uid + "lnA", xn_tm_hbm=xin_s)
        self.ln_to_fm(xin_s, self.fmA, uid + "lnB",
                      g_bc=self.hbm["l1f_ln1g_bc"][:, :], b_bc=self.hbm["l1f_ln1b_bc"][:, :])
        h_fm = self.fmA
        attn_fm = self.fmB
        self.maybe_dump("hfm", self.fmA[:, :, :], [128, 8, S], BF16)

        with p.scope():
            wslab = [p.sbuf(uid + f"_ws{i}", [128, 8, 128], BF16) for i in range(2)]
            bq = p.sbuf(uid + "_bq", [128, 24], F32)
            qt_ = [p.sbuf(uid + f"_qt{i}", [64, 4096], BF16) for i in range(2)]
            kt_ = [p.sbuf(uid + f"_kt{i}", [64, 4096], BF16) for i in range(2)]
            vh_ = [p.sbuf(uid + f"_vh{i}", [64, 4096], BF16) for i in range(2)]
            vt = [p.sbuf(uid + f"_vt{i}", [128, TT, 65], BF16) for i in range(2)]
            esc = [p.sbuf(uid + f"_esc{i}", [128, 2048], BF16) for i in range(2)]
            rcq = p.sbuf(uid + "_rcq", [1, 512], F32)
            rb = p.sbuf(uid + "_rb", [64, 512], F32)
            p.dma("gpsimd", bq[:, :], self.hbm["l1f_bqkv"][:, :], swl)
            pslab = [self.pbank(0), self.pbank(1)]
            pat = self.pbank(6)
            pbc = self.pbank(7)
            ptr = self.pbank_bf(0)
            wq_h = self.hbm["l1f_wqkv"]

            def emit_slab(slab_idx, wb, dst_top, dst_bot, evac_done_prev):
                p.wait("gpsimd", spe, evac_done_prev)
                p.dma("gpsimd", wslab[wb][:, :, :],
                      wq_h[:, slab_idx * 128:(slab_idx + 1) * 128]
                      .rearrange("(kc p) m -> p kc m", p=128), swl)
                p.wait("tensor", swl, swl.n)
                for r in range(8):
                    pb = pslab[r % 2]
                    if r >= 2:
                        p.wait("tensor", sse, sse.n - 2)
                    for kc in range(8):
                        p.run("tensor", lambda t, wb=wb, r=r, kc=kc, pb=pb: t.matmul(
                            pb[:, :], wslab[wb][:, kc, :],
                            h_fm[:, kc, r * 512:(r + 1) * 512],
                            start=(kc == 0), stop=(kc == 7)),
                            inc=(spe if kc == 7 else None))
                    p.wait("scalar", spe, spe.n)
                    p.run("scalar", lambda s, r=r, pb=pb, dst_top=dst_top, slab_idx=slab_idx:
                          s.activation(dst_top[:, r * 512:(r + 1) * 512], pb[0:64, :],
                                       AF.Identity, bias=bq[0:64, slab_idx:slab_idx + 1]), inc=sse)
                    p.run("scalar", lambda s, r=r, pb=pb, dst_bot=dst_bot, slab_idx=slab_idx:
                          s.activation(dst_bot[:, r * 512:(r + 1) * 512], pb[64:128, :],
                                       AF.Identity, bias=bq[64:128, slab_idx:slab_idx + 1]), inc=sse)
                return spe.n

            qk_used = {0: 0, 1: 0}
            slab_mm_done = 0
            vt_used = 0
            for h in range(NH):
                hb = h % 2
                p.wait("scalar", spe, qk_used[hb])
                slab_mm_done = emit_slab(h, hb, qt_[hb], kt_[hb], slab_mm_done)
                if h % 2 == 0:
                    p.wait("scalar", spe, vt_used)
                    slab_mm_done = emit_slab(16 + h // 2, 1 - hb, vh_[0], vh_[1], slab_mm_done)
                    p.wait("tensor", sse, sse.n)
                    for half in range(2):
                        for r4 in range(4):
                            if not (half == 0 and r4 == 0):
                                p.wait("tensor", sve, sve.n)
                            for j in range(8):
                                ltk = r4 * 8 + j
                                p.run("tensor", lambda t, half=half, ltk=ltk, j=j: t.transpose(
                                    ptr[:, j * 64:(j + 1) * 64],
                                    vh_[half][:, ltk * 128:(ltk + 1) * 128],
                                    self.identb[0:64, 0:64]), inc=(spe if j == 7 else None))
                            p.wait("vector", spe, spe.n)
                            p.run("vector", lambda v, half=half, r4=r4: v.tensor_copy(
                                vt[half][:, r4 * 8:(r4 + 1) * 8, 0:64],
                                ptr[:, 0:512].rearrange("p (t d) -> p t d", t=8)), inc=sve)
                    p.run("vector", lambda v: v.memset(vt[0][:, :, 64:65], 1.0))
                    p.run("vector", lambda v: v.memset(vt[1][:, :, 64:65], 1.0), inc=sve)
                    p.wait("tensor", sve, sve.n)
                p.wait("tensor", sse, sse.n)
                exp_done = 0
                pv_done = {0: 0, 1: 0}
                pscore = self.pspan(2, 4)
                for qt in range(8):
                    for gg in range(8):
                        eb = gg % 2
                        if exp_done:
                            p.wait("tensor", sse, exp_done)
                        for i in range(4):
                            kc = gg * 4 + i
                            p.run("tensor", lambda t, hb=hb, kc=kc, qt=qt, i=i: t.matmul(
                                pscore[:, i * 512:(i + 1) * 512],
                                kt_[hb][:, kc * 128:(kc + 1) * 128],
                                qt_[hb][:, qt * 512:(qt + 1) * 512],
                                start=True, stop=True), inc=(spe if i == 3 else None))
                        p.wait("scalar", spe, spe.n)
                        if pv_done[eb]:
                            p.wait("scalar", spe, pv_done[eb])
                        p.run("scalar", lambda s, eb=eb: s.activation(
                            esc[eb][:, :], pscore[:, :], AF.Exp,
                            bias=self.cc("neg_shift"), scale=0.125), inc=sse)
                        exp_done = sse.n
                        p.wait("tensor", sse, exp_done)
                        for i in range(4):
                            kc = gg * 4 + i
                            p.run("tensor", lambda t, hb=hb, kc=kc, eb=eb, i=i: t.matmul(
                                pat[0:65, :], vt[hb][:, kc, :],
                                esc[eb][:, i * 512:(i + 1) * 512],
                                start=(kc == 0), stop=(kc == TT - 1)),
                                inc=(spe if i == 3 else None))
                        pv_done[eb] = spe.n
                    p.wait("vector", spe, spe.n)
                    p.run("vector", lambda v: v.tensor_copy(rcq[0:1, :], pat[64:65, :]))
                    p.hz("vector")
                    p.run("vector", lambda v: v.reciprocal(rcq[0:1, :], rcq[0:1, :]), inc=sve)
                    p.wait("tensor", sve, sve.n)
                    p.run("tensor", lambda t: t.matmul(
                        pbc[0:64, :], self.ones_row[0:1, 0:64], rcq[0:1, :],
                        start=True, stop=True), inc=spe)
                    p.wait("scalar", spe, spe.n)
                    p.run("scalar", lambda s: s.activation(rb[:, :], pbc[0:64, :], AF.Copy), inc=sse)
                    p.wait("vector", sse, sse.n)
                    p.run("vector", lambda v, h=h, qt=qt: v.tensor_mul(
                        attn_fm[(h % 2) * 64:(h % 2) * 64 + 64, h // 2, qt * 512:(qt + 1) * 512],
                        pat[0:64, :], rb[:, :]), inc=sve)
                    p.wait("tensor", sve, sve.n)
                qk_used[hb] = spe.n
                if h % 2 == 1:
                    vt_used = spe.n
        p.barrier()
        self.maybe_dump("attnfm", attn_fm[:, :, :], [128, 8, S], BF16)

        with p.scope():
            wo = p.sbuf(uid + "_wo", [128, 8, 1024], BF16)
            bo = p.sbuf(uid + "_bo", [128, D], F32)
            g2 = p.sbuf(uid + "_g2", [128, D], F32)
            b2c = p.sbuf(uid + "_b2c", [128, D], F32)
            gwl = p.sbuf(uid + "_gwl", [128, 8, 8], BF16)
            gbl = p.sbuf(uid + "_gbl", [128, 8], F32)
            xi = [p.sbuf(uid + f"_xi{i}", [128, D], F32) for i in range(2)]
            xm = [p.sbuf(uid + f"_xm{i}", [128, D], F32) for i in range(2)]
            xc2 = p.sbuf(uid + "_xc2", [128, D], F32)
            sqd = p.sbuf(uid + "_sqd", [128, D], F32)
            h2t = p.sbuf(uid + "_h2t", [128, D], BF16)
            st_ = p.sbuf(uid + "_st2", [128, 4], F32)
            zz = p.sbuf(uid + "_zz", [128, 8], F32)
            cmp_ = p.sbuf(uid + "_cmp", [128, 8], F32)
            rank = p.sbuf(uid + "_rank", [128, 8], F32)
            texp = p.sbuf(uid + "_texp", [128, 8], F32)
            tm_ = p.sbuf(uid + "_tm", [128, 8], F32)
            p.dma("gpsimd", wo[:, :, :], self.hbm["l1f_wo"].rearrange("(c p) n -> p c n", p=128), swl)
            p.dma("gpsimd", bo[:, :], self.hbm["l1f_bo_bc"][:, :], swl)
            p.dma("gpsimd", g2[:, :], self.hbm["l1f_ln2g_bc"][:, :], swl)
            p.dma("gpsimd", b2c[:, :], self.hbm["l1f_ln2b_bc"][:, :], swl)
            p.dma("gpsimd", gwl[:, :, :], self.hbm["l1f_gatew"].rearrange("(kc p) e -> p kc e", p=128), swl)
            p.dma("gpsimd", gbl[:, :], self.hbm["l1f_gateb_bc"][:, :], swl)
            po = [self.pbank(0), self.pbank(1)]
            pt2 = self.pbank_bf(2)
            plg = self.pbank(3)
            p.wait("tensor", swl, swl.n)
            st_done = {0: 0, 1: 0}
            po_done = {0: 0, 1: 0}
            lg_done = 0
            for tc in range(TT):
                b = tc % 2
                if tc >= 2:
                    p.wait("sync", sst, st_done[b])
                p.dma("sync", xi[b][:, :], xin_s[tc * 128:(tc + 1) * 128, :], sld)
                for nb in range(2):
                    if po_done[nb]:
                        p.wait("tensor", sve, po_done[nb])
                    for c in range(8):
                        p.run("tensor", lambda t, tc=tc, nb=nb, c=c: t.matmul(
                            po[nb][:, :], attn_fm[:, c, tc * 128:(tc + 1) * 128],
                            wo[:, c, nb * 512:(nb + 1) * 512],
                            start=(c == 0), stop=(c == 7)), inc=(spe if c == 7 else None))
                    p.wait("vector", spe, spe.n)
                    p.run("vector", lambda v, nb=nb: v.tensor_add(
                        sqd[:, nb * 512:(nb + 1) * 512], po[nb][:, :],
                        bo[:, nb * 512:(nb + 1) * 512]), inc=sve)
                    po_done[nb] = sve.n
                p.wait("vector", sld, sld.n)
                p.run("vector", lambda v, b=b: v.tensor_add(xm[b][:, :], sqd[:, :], xi[b][:, :]), inc=sve)
                p.wait("sync", sve, sve.n)
                p.dma("sync", xmid_s[tc * 128:(tc + 1) * 128, :], xm[b][:, :], sst)
                st_done[b] = sst.n
                p.run("vector", lambda v, b=b: v.reduce_sum(st_[:, 1:2], xm[b][:, :], axis=AX.X))
                p.hz("vector")
                p.run("vector", lambda v: v.tensor_scalar(st_[:, 2:3], st_[:, 1:2], 1.0 / D, None, ALU.mult))
                p.hz("vector")
                p.run("vector", lambda v, b=b: v.tensor_scalar(xc2[:, :], xm[b][:, :], st_[:, 2:3], None, ALU.subtract), inc=sve)
                p.wait("scalar", sve, sve.n)
                p.run("scalar", lambda s: s.activation(sqd[:, :], xc2[:, :], AF.Square, accum_out=st_[:, 0:1]))
                p.hz("scalar")
                p.run("scalar", lambda s: s.activation(st_[:, 3:4], st_[:, 0:1], AF.Sqrt,
                                                       bias=self.cc("ln_eps"), scale=1.0 / D), inc=sse)
                p.wait("vector", sse, sse.n)
                p.run("vector", lambda v: v.reciprocal(st_[:, 3:4], st_[:, 3:4]))
                p.hz("vector")
                p.run("vector", lambda v: v.scalar_tensor_tensor(
                    sqd[:, :], xc2[:, :], st_[:, 3:4], g2[:, :], ALU.mult, ALU.mult))
                p.run("vector", lambda v: v.tensor_add(h2t[:, :], sqd[:, :], b2c[:, :]), inc=sve)
                p.wait("tensor", sve, sve.n)
                for c in range(8):
                    p.run("tensor", lambda t, c=c: t.transpose(
                        pt2[:, c * 128:(c + 1) * 128], h2t[:, c * 128:(c + 1) * 128],
                        self.identb[:, :]), inc=(spe if c == 7 else None))
                p.wait("vector", spe, spe.n)
                p.run("vector", lambda v, tc=tc: v.tensor_copy(
                    self.fmA[:, :, tc * 128:(tc + 1) * 128],
                    pt2[:, :].rearrange("p (c t) -> p c t", c=8)), inc=sve)
                if lg_done:
                    p.wait("tensor", sse, lg_done)
                p.wait("tensor", sve, sve.n)
                for kc in range(8):
                    p.run("tensor", lambda t, tc=tc, kc=kc: t.matmul(
                        plg[:, 0:8], self.fmA[:, kc, tc * 128:(tc + 1) * 128], gwl[:, kc, :],
                        start=(kc == 0), stop=(kc == 7)), inc=(spe if kc == 7 else None))
                p.wait("vector", spe, spe.n)
                p.run("vector", lambda v: v.tensor_add(zz[:, :], plg[:, 0:8], gbl[:, :]))
                p.hz("vector")
                p.run("vector", lambda v: v.reduce_max(st_[:, 0:1], zz[:, :], axis=AX.X, negate=True), inc=sve)
                p.hz("vector")
                for j in range(8):
                    p.run("vector", lambda v, j=j: v.tensor_scalar(
                        cmp_[:, :], zz[:, :], zz[:, j:j + 1], None, ALU.is_lt))
                    p.hz("vector")
                    if j == 0:
                        p.run("vector", lambda v: v.tensor_copy(rank[:, :], cmp_[:, :]))
                    else:
                        p.run("vector", lambda v: v.tensor_add(rank[:, :], rank[:, :], cmp_[:, :]))
                    p.hz("vector")
                p.run("vector", lambda v: v.tensor_scalar(
                    rank[:, :], rank[:, :], 2.0, None, ALU.is_lt), inc=sve)
                p.wait("scalar", sve, sve.n)
                p.run("scalar", lambda s: s.activation(texp[:, :], zz[:, :], AF.Exp,
                                                       bias=st_[:, 0:1]), inc=sse)
                p.wait("vector", sse, sse.n)
                p.run("vector", lambda v: v.tensor_mul(tm_[:, :], texp[:, :], rank[:, :]))
                p.hz("vector")
                p.run("vector", lambda v: v.reduce_sum(st_[:, 1:2], tm_[:, :], axis=AX.X))
                p.hz("vector")
                p.run("vector", lambda v: v.reciprocal(st_[:, 1:2], st_[:, 1:2]))
                p.hz("vector")
                p.run("vector", lambda v, tc=tc: v.tensor_scalar(
                    self.c_tm[:, tc, 0:8], tm_[:, :], st_[:, 1:2], None, ALU.mult), inc=sve)
                lg_done = sve.n
        p.barrier()
        self.maybe_dump("xmid", xmid_s, [S, D], F32)
        self.maybe_dump("h2fm", self.fmA[:, :, :], [128, 8, S], BF16)
        self.maybe_dump("ctm", self.c_tm[:, :, :], [128, TT, 9], F32)

        ewt = p.sbuf(uid + "_ewt", [8, TT, 128], BF16)
        with p.scope():
            pew = self.pbank(5)
            for tc in range(TT):
                if tc:
                    p.wait("tensor", sse, sse.n)
                p.run("tensor", lambda t, tc=tc: t.transpose(
                    pew[0:8, 0:128], self.c_tm[:, tc, 0:8], self.identf[:, :]), inc=spe)
                p.wait("scalar", spe, spe.n)
                p.run("scalar", lambda s, tc=tc: s.activation(
                    ewt[:, tc, :], pew[0:8, 0:128], AF.Copy), inc=sse)
        p.barrier()

        NE2 = 8
        NQ = 4
        acc = self.fmB[:, :, :].rearrange("p c t -> p (c t)").bitcast(F32)
        with p.scope():
            w1q = [p.sbuf(uid + f"_w1q{i}", [128, 8, 1024], BF16) for i in range(2)]
            w2q_s = p.sbuf(uid + "_w2q", [128, 8, 1024], BF16)
            w2q = [w2q_s, w2q_s]
            b1s = p.sbuf(uid + "_b1s", [128, 32], F32)
            b2s = p.sbuf(uid + "_b2s", [8, 1024], BF16)
            hid = p.sbuf(uid + "_hid", [128, 8, 512], BF16)
            _w1f0 = w1q[0][:, :, :].rearrange("p a b -> p (a b)")
            _w1f1 = w1q[1][:, :, :].rearrange("p a b -> p (a b)")
            xr = [_w1f0[:, 0:2048].bitcast(F32)] * 2
            xml = [_w1f0[:, 2048:4096].bitcast(F32)] * 2
            t1 = _w1f1[:, 0:2048].bitcast(F32)
            xo = [_w1f1[:, 2048:4096].bitcast(F32)] * 2
            p.dma("gpsimd", b2s[:, :], self.hbm["l1f_b2"][:, :], swl)
            wdone = {0: 0, 1: 0}
            w2_done = 0
            b1_done = 0
            relu_done = {0: 0, 1: 0, 2: 0, 3: 0}
            moe_out_done = {0: 0, 1: 0}
            st_done = {0: 0, 1: 0}
            for ts in range(2):
                p.wait("vector", sst, sst.n)
                p.run("vector", lambda v: v.memset(acc[:, :], 0.0), inc=sve)
                p.wait("gpsimd", sst, sst.n)
                p.wait("gpsimd", sve, sve.n)
                for e in range(NE2):
                    p.wait("gpsimd", sse, b1_done)
                    p.dma("gpsimd", b1s[:, :], self.hbm["l1f_b1T"][e], swl)
                    for qr in range(NQ):
                        it = (ts * NE2 + e) * NQ + qr
                        wb = it % 2
                        if it >= 2:
                            p.wait("gpsimd", spe, wdone[wb])
                        p.dma("gpsimd", w1q[wb][:, :, :],
                              self.hbm["l1f_W1"][e][:, qr * 1024:(qr + 1) * 1024]
                              .rearrange("(kc p) n -> p kc n", p=128), swl)
                        if it >= 1:
                            p.wait("gpsimd", spe, w2_done)
                        p.dma("gpsimd", w2q_s[:, :, :],
                              self.hbm["l1f_W2"][e][qr * 1024:(qr + 1) * 1024, :]
                              .rearrange("(dc p) n -> p dc n", p=128), swl)
                        p.wait("tensor", swl, swl.n)
                        for tt in range(4):
                            t0 = tt * 512
                            for dc in range(8):
                                hb_ = self.pbank(dc % 4)
                                if relu_done[dc % 4]:
                                    p.wait("tensor", sse, relu_done[dc % 4])
                                for kc in range(8):
                                    p.run("tensor", lambda t, wb=wb, dc=dc, kc=kc, ts=ts, t0=t0, hb_=hb_: t.matmul(
                                        hb_[:, :], w1q[wb][:, kc, dc * 128:(dc + 1) * 128],
                                        self.fmA[:, kc, ts * 2048 + t0:ts * 2048 + t0 + 512],
                                        start=(kc == 0), stop=(kc == 7)),
                                        inc=(spe if kc == 7 else None))
                                p.wait("scalar", spe, spe.n)
                                p.run("scalar", lambda s, dc=dc, qr=qr, hb_=hb_: s.activation(
                                    hid[:, dc, :], hb_[:, :], AF.Relu,
                                    bias=b1s[:, qr * 8 + dc:qr * 8 + dc + 1]), inc=sse)
                                relu_done[dc % 4] = sse.n
                            b1_done = sse.n
                            p.wait("tensor", sse, sse.n)
                            for tck in range(4):
                                ob = 4 + (tck % 2) * 2
                                pO = [self.pbank(ob), self.pbank(ob + 1)]
                                if moe_out_done[tck % 2]:
                                    p.wait("tensor", sve, moe_out_done[tck % 2])
                                for dc in range(8):
                                    for nb in range(2):
                                        p.run("tensor", lambda t, wb=wb, dc=dc, nb=nb, tck=tck, t0=t0, pO=pO: t.matmul(
                                            pO[nb][:, :],
                                            hid[:, dc, tck * 128:(tck + 1) * 128],
                                            w2q[wb][:, dc, nb * 512:(nb + 1) * 512],
                                            start=(dc == 0), stop=(dc == 7)),
                                            inc=(spe if (dc == 7 and nb == 1) else None))
                                p.wait("vector", spe, spe.n)
                                gtc = ts * 16 + (t0 // 128) + tck
                                lc = (t0 // 128) + tck
                                for nb in range(2):
                                    p.run("vector", lambda v, e=e, nb=nb, gtc=gtc, lc=lc, pO=pO: v.scalar_tensor_tensor(
                                        acc[:, lc * 1024 + nb * 512:lc * 1024 + (nb + 1) * 512],
                                        pO[nb][:, :], self.c_tm[:, gtc, e:e + 1],
                                        acc[:, lc * 1024 + nb * 512:lc * 1024 + (nb + 1) * 512],
                                        ALU.mult, ALU.add), inc=sve)
                                moe_out_done[tck % 2] = sve.n
                        wdone[wb] = spe.n
                        w2_done = spe.n
                p.wait("tensor", swl, swl.n)
                p.wait("sync", spe, spe.n)
                p.wait("vector", spe, spe.n)
                if ts == 0:
                    self.maybe_dump("acc0", acc[:, :], [128, 16384], F32)
                for lc in range(16):
                    b = lc % 2
                    gtc = ts * 16 + lc
                    pO = [self.pbank(4 + b * 2), self.pbank(5 + b * 2)]
                    if lc >= 1:
                        p.wait("sync", sst, sst.n)
                    p.dma("sync", xr[b][:, :], x_in[gtc * 128:(gtc + 1) * 128, :], sld)
                    p.dma("sync", xml[b][:, :], xmid_s[gtc * 128:(gtc + 1) * 128, :], sld)
                    p.wait("tensor", sve, sve.n)
                    for nb in range(2):
                        p.run("tensor", lambda t, gtc=gtc, nb=nb, pO=pO: t.matmul(
                            pO[nb][:, :], ewt[:, gtc, :], b2s[:, nb * 512:(nb + 1) * 512],
                            start=True, stop=True), inc=(spe if nb == 1 else None))
                    p.wait("vector", spe, spe.n)
                    p.wait("vector", sld, sld.n)
                    p.wait("vector", sst, st_done[b])
                    for nb in range(2):
                        p.run("vector", lambda v, lc=lc, nb=nb, pO=pO: v.tensor_add(
                            acc[:, lc * 1024 + nb * 512:lc * 1024 + (nb + 1) * 512],
                            acc[:, lc * 1024 + nb * 512:lc * 1024 + (nb + 1) * 512],
                            pO[nb][:, :]))
                    p.run("vector", lambda v, b=b, lc=lc: v.scalar_tensor_tensor(
                        t1[:, :], acc[:, lc * 1024:(lc + 1) * 1024], 0.5,
                        xr[b][:, :], ALU.mult, ALU.add))
                    p.run("vector", lambda v, b=b: v.scalar_tensor_tensor(
                        xo[b][:, :], xml[b][:, :], 0.5, t1[:, :], ALU.mult, ALU.add), inc=sve)
                    p.wait("sync", sve, sve.n)
                    p.dma("sync", x_out[gtc * 128:(gtc + 1) * 128, :], xo[b][:, :], sst)
                    st_done[b] = sst.n
        p.barrier()
        return sst

    def l0attn_tables(self, uid):
        nc, p = self.nc, self.p
        srel_s = nc.dram_tensor(uid + "_srel", [NH, 9344], BF16, kind="Internal").ap()
        mat3 = nc.dram_tensor(uid + "_mat3", [128, NH, 8, 8192], BF16, kind="Internal").ap()
        self._l0a_mat3 = mat3
        sve = p.sem(uid + "_tve")
        sse = p.sem(uid + "_tse")
        sld = p.sem(uid + "_tld")
        sst = p.sem(uid + "_tst")
        smt = p.sem(uid + "_mt")
        self._l0a_smt = smt
        with p.scope():
            relg = p.sbuf(uid + "_relg", [128, 1024], F32)
            amp = p.sbuf(uid + "_amp", [128, 21], F32)
            shp = p.sbuf(uid + "_shp", [128, 21], F32)
            off = p.sbuf(uid + "_off", [128, 21], F32)
            nsh = p.sbuf(uid + "_nsh", [128, 21], F32)
            dd = p.sbuf(uid + "_dd", [128, 1024], F32)
            d2 = p.sbuf(uid + "_d2", [128, 1024], F32)
            gk = p.sbuf(uid + "_gk", [128, 1024], F32)
            accs = p.sbuf(uid + "_accs", [128, 1024], F32)
            p.dma("sync", relg[:, :], self.hbm["l0a_relgrid"][:, :], sld)
            p.dma("sync", amp[:, :], self.hbm["l0a_amp"][:, :], sld)
            p.dma("sync", shp[:, :], self.hbm["l0a_sharp"][:, :], sld)
            p.dma("sync", off[:, :], self.hbm["l0a_off"][:, :], sld)
            p.wait("scalar", sld, sld.n)
            p.run("scalar", lambda s: s.activation(nsh[:, :], shp[:, :], AF.Abs), inc=sse)
            p.wait("vector", sse, sse.n)
            p.run("vector", lambda v: v.tensor_scalar(nsh[:, :], nsh[:, :], -1.0, None, ALU.mult))
            p.run("vector", lambda v: v.memset(accs[:, :], 0.0), inc=sve)
            p.wait("vector", sld, sld.n)
            for k in range(21):
                p.run("vector", lambda v, k=k: v.tensor_scalar(
                    dd[:, :], relg[:, :], off[:, k:k + 1], None, ALU.subtract), inc=sve)
                p.wait("scalar", sve, sve.n)
                p.run("scalar", lambda s: s.activation(d2[:, :], dd[:, :], AF.Square))
                p.run("scalar", lambda s, k=k: s.activation(gk[:, :], d2[:, :], AF.Exp,
                                                            scale=nsh[:, k:k + 1]), inc=sse)
                p.wait("vector", sse, sse.n)
                p.run("vector", lambda v, k=k: v.scalar_tensor_tensor(
                    accs[:, :], gk[:, :], amp[:, k:k + 1], accs[:, :],
                    ALU.mult, ALU.add), inc=sve)
            p.wait("scalar", sve, sve.n)
            p.run("scalar", lambda s: s.activation(gk[:, :], accs[:, :], AF.Exp), inc=sse)
            p.wait("gpsimd", sse, sse.n)
            p.dma("gpsimd", bass.AP(srel_s.tensor, 1024, [[9344, NH], [1024, 8], [1, 1024]]),
                  gk[:, :], sst)
            p.wait("scalar", sst, sst.n)
            for pp in range(128):
                for ci in range(8):
                    o0 = 1023 - pp - ci * 128
                    p.dma("scalar", mat3[pp, :, ci, :],
                          bass.AP(srel_s.tensor, o0, [[9344, NH], [1, 8192]]), smt)
        for e in ENGINES:
            p.wait(e, sst, sst.n)

    def l0attn(self, x_in, x_out, uid):
        nc, p = self.nc, self.p
        mat3 = self._l0a_mat3
        smt = self._l0a_smt
        swl = p.sem(uid + "_wl")
        sld = p.sem(uid + "_ld")
        spe = p.sem(uid + "_pe")
        sve = p.sem(uid + "_ve")
        sse = p.sem(uid + "_se")
        sst = p.sem(uid + "_st")

        xn_fm = self.fmA
        self.ln_to_fm(x_in, xn_fm, uid + "ln", banks=(0, 1))
        h_fm = xn_fm
        attn_fm = self.fmB

        with p.scope():
            _ws0 = p.sbuf(uid + "_ws0", [128, 8, 128], BF16)
            wslab = [_ws0, _ws0]
            _qk1 = p.sbuf(uid + "_qk1", [128, 4096], BF16)
            _qk2 = p.sbuf(uid + "_qk2", [128, 4096], BF16)
            _vh2 = p.sbuf(uid + "_vh2", [128, 4096], BF16)
            qt_ = [_qk1[0:64, :], _qk1[64:128, :]]
            kt_ = [_qk2[0:64, :], _qk2[64:128, :]]
            vh_ = [_vh2[0:64, :], _vh2[64:128, :]]
            vt = [p.sbuf(uid + f"_vt{i}", [128, TT, 65], BF16) for i in range(2)]
            esc = [p.sbuf(uid + f"_esc{i}", [128, 2048], BF16) for i in range(2)]
            escr = p.sbuf(uid + "_escr", [128, 2048], BF16)
            bt8 = [p.sbuf(uid + f"_bt8{i}", [128, 8, 512], BF16) for i in range(2)]
            rb = p.sbuf(uid + "_rb", [64, 512], F32)
            rcq = rb[0:1, :]
            pslab = [self.pbank(0), self.pbank(1)]
            pat = self.pbank(6)
            pbc = self.pbank(7)
            ptr = self.pbank_bf(0)
            wq_h = self.hbm["l0a_win"]

            def emit_slab(slab_idx, wb, dst_top, dst_bot, evac_done_prev):
                p.wait("gpsimd", spe, evac_done_prev)
                p.dma("gpsimd", wslab[wb][:, :, :],
                      wq_h[:, slab_idx * 128:(slab_idx + 1) * 128]
                      .rearrange("(kc p) m -> p kc m", p=128), swl)
                p.wait("tensor", swl, swl.n)
                for r in range(8):
                    pb = pslab[r % 2]
                    if r >= 2:
                        p.wait("tensor", sse, sse.n - 2)
                    for kc in range(8):
                        p.run("tensor", lambda t, wb=wb, r=r, kc=kc, pb=pb: t.matmul(
                            pb[:, :], wslab[wb][:, kc, :],
                            h_fm[:, kc, r * 512:(r + 1) * 512],
                            start=(kc == 0), stop=(kc == 7)),
                            inc=(spe if kc == 7 else None))
                    p.wait("scalar", spe, spe.n)
                    p.run("scalar", lambda s, r=r, pb=pb, dst_top=dst_top:
                          s.activation(dst_top[:, r * 512:(r + 1) * 512], pb[0:64, :],
                                       AF.Copy), inc=sse)
                    p.run("scalar", lambda s, r=r, pb=pb, dst_bot=dst_bot:
                          s.activation(dst_bot[:, r * 512:(r + 1) * 512], pb[64:128, :],
                                       AF.Copy), inc=sse)
                return spe.n

            p.wait("tensor", smt, smt.n)
            qk_used = {0: 0, 1: 0}
            slab_mm_done = 0
            vt_used = 0
            bt_used = {0: 0, 1: 0}
            for h in range(NH):
                hb = h % 2
                p.wait("scalar", spe, qk_used[hb])
                slab_mm_done = emit_slab(h, hb, qt_[hb], kt_[hb], slab_mm_done)
                if h % 2 == 0:
                    p.wait("scalar", spe, vt_used)
                    slab_mm_done = emit_slab(16 + h // 2, 1 - hb, vh_[0], vh_[1], slab_mm_done)
                    p.wait("tensor", sse, sse.n)
                    for half in range(2):
                        for r4 in range(4):
                            if not (half == 0 and r4 == 0):
                                p.wait("tensor", sve, sve.n)
                            for j in range(8):
                                ltk = r4 * 8 + j
                                p.run("tensor", lambda t, half=half, ltk=ltk, j=j: t.transpose(
                                    ptr[:, j * 64:(j + 1) * 64],
                                    vh_[half][:, ltk * 128:(ltk + 1) * 128],
                                    self.identb[half * 64:half * 64 + 64,
                                                half * 64:half * 64 + 64]),
                                    inc=(spe if j == 7 else None))
                            p.wait("vector", spe, spe.n)
                            p.run("vector", lambda v, half=half, r4=r4: v.tensor_copy(
                                vt[half][:, r4 * 8:(r4 + 1) * 8, 0:64],
                                ptr[:, 0:512].rearrange("p (t d) -> p t d", t=8)), inc=sve)
                    p.run("vector", lambda v: v.memset(vt[0][:, :, 64:65], 1.0))
                    p.run("vector", lambda v: v.memset(vt[1][:, :, 64:65], 1.0), inc=sve)
                    p.wait("tensor", sve, sve.n)
                p.wait("tensor", sse, sse.n)
                exp_done = 0
                mul_done = {0: 0, 1: 0}
                pv_done = {0: 0, 1: 0}
                pscore = self.pspan(2, 4)
                for qt in range(8):
                    for gg in range(8):
                        eb = gg % 2
                        kcg = gg // 2
                        cib = (gg % 2) * 4
                        if gg % 2 == 0:
                            bb = kcg % 2
                            j0 = 4096 + qt * 512 - kcg * 1024
                            if bt_used[bb]:
                                p.wait("sync", sve, bt_used[bb])
                            p.dma("sync", bt8[bb][:, :, :],
                                  bass.AP(mat3.tensor, h * 8 * 8192 + j0,
                                          [[NH * 8 * 8192, 128], [8192, 8], [1, 512]]), sld)
                        if exp_done:
                            p.wait("tensor", sse, exp_done)
                        for i in range(4):
                            kc = gg * 4 + i
                            p.run("tensor", lambda t, hb=hb, kc=kc, qt=qt, i=i: t.matmul(
                                pscore[:, i * 512:(i + 1) * 512],
                                kt_[hb][:, kc * 128:(kc + 1) * 128],
                                qt_[hb][:, qt * 512:(qt + 1) * 512],
                                start=True, stop=True), inc=(spe if i == 3 else None))
                        p.wait("scalar", spe, spe.n)
                        p.wait("scalar", sve, mul_done[1 - eb])
                        p.run("scalar", lambda s: s.activation(
                            escr[:, :], pscore[:, :], AF.Exp,
                            bias=self.cc("neg_shift"), scale=0.125), inc=sse)
                        exp_done = sse.n
                        p.wait("vector", sse, exp_done)
                        p.wait("vector", sld, sld.n)
                        if pv_done[eb]:
                            p.wait("vector", spe, pv_done[eb])
                        p.run("vector", lambda v, bb=bb, cib=cib, eb=eb: v.tensor_mul(
                            esc[eb][:, :], escr[:, :],
                            bt8[bb][:, cib:cib + 4, :].rearrange("p a b -> p (a b)")), inc=sve)
                        mul_done[eb] = sve.n
                        if gg % 2 == 1:
                            bt_used[bb] = sve.n
                        p.wait("tensor", sve, mul_done[eb])
                        for i in range(4):
                            kc = gg * 4 + i
                            p.run("tensor", lambda t, hb=hb, kc=kc, eb=eb, i=i: t.matmul(
                                pat[0:65, :], vt[hb][:, kc, :],
                                esc[eb][:, i * 512:(i + 1) * 512],
                                start=(kc == 0), stop=(kc == TT - 1)),
                                inc=(spe if i == 3 else None))
                        pv_done[eb] = spe.n
                    p.wait("vector", spe, spe.n)
                    p.run("vector", lambda v: v.tensor_copy(rcq[0:1, :], pat[64:65, :]))
                    p.hz("vector")
                    p.run("vector", lambda v: v.reciprocal(rcq[0:1, :], rcq[0:1, :]), inc=sve)
                    p.wait("tensor", sve, sve.n)
                    p.run("tensor", lambda t: t.matmul(
                        pbc[0:64, :], self.ones_row[0:1, 0:64], rcq[0:1, :],
                        start=True, stop=True), inc=spe)
                    p.wait("scalar", spe, spe.n)
                    p.run("scalar", lambda s: s.activation(rb[:, :], pbc[0:64, :], AF.Copy), inc=sse)
                    p.wait("vector", sse, sse.n)
                    p.run("vector", lambda v, h=h, qt=qt: v.tensor_mul(
                        attn_fm[(h % 2) * 64:(h % 2) * 64 + 64, h // 2, qt * 512:(qt + 1) * 512],
                        pat[0:64, :], rb[:, :]), inc=sve)
                    p.wait("tensor", sve, sve.n)
                qk_used[hb] = spe.n
                if h % 2 == 1:
                    vt_used = spe.n
        p.barrier()

        with p.scope():
            wg2 = p.sbuf(uid + "_wg2", [128, 8, 2048], BF16)
            bga = p.sbuf(uid + "_bga", [128, D], F32)
            bgg = p.sbuf(uid + "_bgg", [128, D], F32)
            xr = [p.sbuf(uid + f"_xr{i}", [128, D], F32) for i in range(2)]
            asb = p.sbuf(uid + "_asb", [128, D], F32)
            gsb = p.sbuf(uid + "_gsb", [128, D], F32)
            gt = p.sbuf(uid + "_gt", [128, D], F32)
            xo = [p.sbuf(uid + f"_xo{i}", [128, D], F32) for i in range(2)]
            p.dma("gpsimd", wg2[:, :, :], self.hbm["l0a_wgate"].rearrange("(c p) n -> p c n", p=128), swl)
            p.dma("gpsimd", bga[:, :], self.hbm["l0a_bga_bc"][:, :], swl)
            p.dma("gpsimd", bgg[:, :], self.hbm["l0a_bgg_bc"][:, :], swl)
            pa = [self.pbank(0), self.pbank(1)]
            pg = [self.pbank(2), self.pbank(3)]
            p.wait("tensor", swl, swl.n)
            st_done = {0: 0, 1: 0}
            ga_done = 0
            for tc in range(TT):
                b = tc % 2
                if tc >= 2:
                    p.wait("sync", sst, st_done[b])
                p.dma("sync", xr[b][:, :], x_in[tc * 128:(tc + 1) * 128, :], sld)
                if ga_done:
                    p.wait("tensor", sve, ga_done)
                for nb in range(2):
                    for c in range(8):
                        p.run("tensor", lambda t, tc=tc, nb=nb, c=c: t.matmul(
                            pa[nb][:, :], attn_fm[:, c, tc * 128:(tc + 1) * 128],
                            wg2[:, c, nb * 512:(nb + 1) * 512],
                            start=(c == 0), stop=(c == 7)), inc=(spe if c == 7 else None))
                for nb in range(2):
                    for c in range(8):
                        p.run("tensor", lambda t, tc=tc, nb=nb, c=c: t.matmul(
                            pg[nb][:, :], attn_fm[:, c, tc * 128:(tc + 1) * 128],
                            wg2[:, c, 1024 + nb * 512:1024 + (nb + 1) * 512],
                            start=(c == 0), stop=(c == 7)), inc=(spe if c == 7 else None))
                p.wait("vector", spe, spe.n - 2)
                for nb in range(2):
                    p.run("vector", lambda v, nb=nb: v.tensor_add(
                        asb[:, nb * 512:(nb + 1) * 512], pa[nb][:, :],
                        bga[:, nb * 512:(nb + 1) * 512]), inc=sve)
                p.wait("vector", spe, spe.n)
                for nb in range(2):
                    p.run("vector", lambda v, nb=nb: v.tensor_add(
                        gt[:, nb * 512:(nb + 1) * 512], pg[nb][:, :],
                        bgg[:, nb * 512:(nb + 1) * 512]), inc=sve)
                ga_done = sve.n
                p.wait("scalar", sve, sve.n)
                p.run("scalar", lambda s: s.activation(gsb[:, :], gt[:, :], AF.Sigmoid), inc=sse)
                p.wait("vector", sse, sse.n)
                p.wait("vector", sld, sld.n)
                p.run("vector", lambda v: v.tensor_mul(asb[:, :], asb[:, :], gsb[:, :]))
                p.run("vector", lambda v, b=b: v.tensor_add(xo[b][:, :], xr[b][:, :], asb[:, :]), inc=sve)
                p.wait("sync", sve, sve.n)
                p.dma("sync", x_out[tc * 128:(tc + 1) * 128, :], xo[b][:, :], sst)
                st_done[b] = sst.n
        p.barrier()
        return sst



STAGE_SEQ = ("l0ffn", "l0attn", "l0ffn", "l1ffn", "l1attn", "l1ffn")


def build_program(g, stages):
    nc = bass.Bass("TRN2", target_bir_lowering=False, debug=False)
    x = nc.dram_tensor("x", [S, D], F32, kind="ExternalInput").ap()
    y = nc.dram_tensor("y", [S, D], F32, kind="ExternalOutput").ap()
    b = Builder(nc, g)
    scratch = [nc.dram_tensor(f"res{i}", [S, D], F32, kind="Internal").ap()
               for i in range(max(0, len(stages) - 1))]
    if "l0attn" in stages:
        with b.p.scope():
            b.l0attn_tables("tz")
    last = None
    for i, st in enumerate(stages):
        src = x if i == 0 else scratch[i - 1]
        dst = y if i == len(stages) - 1 else scratch[i]
        uid = f"s{i}"
        fn = {"l0ffn": b.l0ffn, "l0attn": b.l0attn,
              "l1ffn": b.l1ffn, "l1attn": b.l1attn}[st]
        with b.p.scope():
            last = fn(src, dst, uid)
    b.p.wait("sync", last, last.n)
    with b.p.es:
        with nc.Block() as block:
            b.p.emit_into(block)
    return nc


def make_in_maps(g, xs):
    in_maps = []
    for xc in xs:
        m = {"x": np.ascontiguousarray(xc)}
        for name, arr in g.items():
            if not name.startswith("_"):
                m["g_" + name] = arr
        in_maps.append(m)
    return in_maps


def kernel(**inputs):
    g = prep_host_inputs(inputs)
    x = np.asarray(inputs["L0_attn_x"])
    nc = build_program(g, list(STAGE_SEQ))
    res = run_bass_kernel_spmd(nc, make_in_maps(g, [x[c] for c in range(8)]),
                               core_ids=list(range(8)))
    return np.stack([np.asarray(res.results[c]["y"]) for c in range(8)],
                    axis=0).astype(np.float32)
